# version 10
# baseline (speedup 1.0000x reference)
import sys
import numpy as np
from contextlib import ExitStack, contextmanager

sys.path.insert(0, "/opt/trn_rl_repo")
import concourse.bass as bass
import concourse.mybir as mybir
from concourse.bass_utils import run_bass_kernel_spmd

F32 = mybir.dt.float32
BF16 = mybir.dt.bfloat16
I32 = mybir.dt.int32
AF = mybir.ActivationFunctionType
ALU = mybir.AluOpType
AXX = mybir.AxisListType.X
IOffs = bass.IndirectOffsetOnAxis

ENGINES = ("sync", "scalar", "vector", "tensor", "gpsimd")

D = 1024
DCH = D // 128
T = 8192
B = 4



class KB:

    def __init__(self, nc):
        self.nc = nc
        self.cmds = {e: [] for e in ENGINES}
        self.es = ExitStack()
        self.scopes = [self.es]
        self.bars = {k: self.es.enter_context(nc.semaphore("bar_" + k)) for k in "cdg"}
        self.cnt = {k: 0 for k in "cdg"}
        self.last = {e: {k: 0 for k in "cdg"} for e in ENGINES}
        self._consts = {}
        self._const_sb = self.es.enter_context(nc.sbuf_tensor("kconsts", [128, 16], F32))

    def c(self, value, p=128):
        if value not in self._consts:
            assert len(self._consts) < 16
            self._consts[value] = len(self._consts)
        i = self._consts[value]
        return self._const_sb[0:p, i:i + 1]

    def init_consts_step(self):
        def fn(v):
            last = None
            for val, i in self._consts.items():
                last = v.memset(self._const_sb[:, i:i + 1], float(val))
            return last if last is not None else v.nop()
        self.step(vector=self.comp(fn))

    def sbuf(self, name, shape, dt):
        return self.scopes[-1].enter_context(self.nc.sbuf_tensor(name, shape, dt))

    def psum(self, name, shape, dt=F32):
        return self.scopes[-1].enter_context(self.nc.psum_tensor(name, shape, dt))

    @contextmanager
    def scope(self):
        es = ExitStack()
        self.scopes.append(es)
        try:
            yield
        finally:
            es.close()
            self.scopes.pop()

    def step(self, **parts):
        starts = dict(self.cnt)
        for eng, (kind, incs, fn) in parts.items():
            waits = [(k, starts[k]) for k in "cdg"
                     if starts[k] > self.last[eng][k]]
            semk = "c" if kind == "comp" else ("g" if eng == "gpsimd" else "d")
            def cmd(engobj, _fn=fn, _w=waits, _sh=self.bars[semk]):
                for k, val in _w:
                    engobj.wait_ge(self.bars[k], val)
                _fn(engobj, _sh)
            self.cmds[eng].append(cmd)
            for k, val in waits:
                self.last[eng][k] = val
            self.cnt[semk] += incs

    def final_wait(self):
        tot = dict(self.cnt)
        def cmd(engobj, _t=tot):
            for k in "cdg":
                if _t[k]:
                    engobj.wait_ge(self.bars[k], _t[k])
        self.cmds["sync"].append(cmd)

    @staticmethod
    def _r(ap):
        return ap() if callable(ap) else ap

    def dma(self, eng, out, in_, slow=False):
        def fn(e, semh, _o=out, _i=in_):
            e.dma_start(out=self._r(_o), in_=self._r(_i),
                        allow_slow_non_contiguous=slow).then_inc(semh, 16)
        return ("dma", 16, fn)

    def dmas(self, pairs):
        def fn(e, semh, _p=pairs):
            for _o, _i in _p:
                e.dma_start(out=self._r(_o), in_=self._r(_i)).then_inc(semh, 16)
        return ("dma", 16 * len(pairs), fn)

    def comp(self, fn):
        def wrap(e, semh, _fn=fn):
            last = _fn(e)
            last.then_inc(semh, 1)
        return ("comp", 1, wrap)

    def build(self):
        block = self.es.enter_context(self.nc.Block())
        for e in ENGINES:
            cmds = self.cmds[e]
            def run(engobj, _cmds=cmds):
                for fn in _cmds:
                    fn(engobj)
            getattr(block, e)(run)
        self.es.close()


def fm(hbm_ap, t0, tt):
    return hbm_ap[:, t0:t0 + tt].rearrange("(c p) t -> p c t", p=128)


def sb3(sb_ap, tt):
    return sb_ap.rearrange("p (c t) -> p c t", t=tt)


def fm_rows(hbm_ap, d0, d1, t0, tt):
    return hbm_ap[d0:d1, t0:t0 + tt].rearrange("(c p) t -> p c t", p=128)



def emit_layernorm(kb, src_fn, dst_hbm, nm, ntok=T, eps=1e-5):
    TT = 1024
    x_db = [kb.sbuf(nm + "x%d" % k, [128, DCH * TT], F32) for k in range(2)]
    sq_sb = kb.sbuf(nm + "q", [128, DCH * TT], F32)
    y_sb = kb.sbuf(nm + "y", [128, DCH * TT], BF16)
    st_sb = kb.sbuf(nm + "s", [1, 2 * TT], F32)
    ones_f = kb.sbuf(nm + "1", [128, 1], F32)
    ones_r = kb.sbuf(nm + "r", [1, 128], F32)
    ps_sum = kb.psum(nm + "psu", [1, TT])
    ps_sq = kb.psum(nm + "psq", [1, TT])
    ps_m = kb.psum(nm + "pm", [128, TT])
    ps_a = kb.psum(nm + "pa", [128, TT])

    kb.step(vector=kb.comp(lambda v: (v.memset(ones_f[:], 1.0),
                                      v.memset(ones_r[:], 1.0))[-1]))

    m_row = st_sb[0:1, 0:TT]
    a_row = st_sb[0:1, TT:2 * TT]

    ntile_ln = ntok // TT
    kb.step(sync=kb.dma("sync", sb3(x_db[0][:, :], TT), src_fn(0, TT)))
    for i in range(ntile_ln):
        t0 = i * TT
        x_sb = x_db[i % 2]
        parts = {"scalar": kb.comp(lambda s, _x=x_sb: s.activation(sq_sb[:, :], _x[:, :],
                                                                   AF.Square))}
        if i + 1 < ntile_ln:
            parts["sync"] = kb.dma("sync", sb3(x_db[(i + 1) % 2][:, :], TT),
                                   src_fn((i + 1) * TT, TT))
        kb.step(**parts)

        def sums(t, _x=x_sb):
            for c in range(DCH):
                for hh in range(2):
                    t.matmul(ps_sum[:, hh * 512:(hh + 1) * 512], ones_f[:, :],
                             _x[:, c * TT + hh * 512: c * TT + (hh + 1) * 512],
                             start=(c == 0), stop=(c == DCH - 1))
            for c in range(DCH):
                for hh in range(2):
                    last = t.matmul(ps_sq[:, hh * 512:(hh + 1) * 512], ones_f[:, :],
                                    sq_sb[:, c * TT + hh * 512: c * TT + (hh + 1) * 512],
                                    start=(c == 0), stop=(c == DCH - 1))
            return last
        kb.step(tensor=kb.comp(sums))

        kb.step(vector=kb.comp(lambda v: (
            v.tensor_scalar(m_row, ps_sum[:, :], 1.0 / D, None, op0=ALU.mult),
            v.tensor_scalar(a_row, ps_sq[:, :], 1.0 / D, None, op0=ALU.mult))[-1]))
        kb.step(vector=kb.comp(lambda v: v.tensor_tensor(ps_sum[0:1, :], m_row, m_row,
                                                         op=ALU.mult)))
        kb.step(vector=kb.comp(lambda v: v.tensor_tensor(a_row, a_row, ps_sum[0:1, :],
                                                         op=ALU.subtract)))
        kb.step(scalar=kb.comp(lambda s: s.activation(a_row, a_row, AF.Ln, bias=kb.c(eps, p=1))))
        kb.step(scalar=kb.comp(lambda s: s.activation(a_row, a_row, AF.Exp, scale=-0.5)))

        def bcast(t):
            for hh in range(2):
                t.matmul(ps_m[:, hh * 512:(hh + 1) * 512], ones_r[:, :],
                         m_row[0:1, hh * 512:(hh + 1) * 512], start=True, stop=True)
            for hh in range(2):
                last = t.matmul(ps_a[:, hh * 512:(hh + 1) * 512], ones_r[:, :],
                                a_row[0:1, hh * 512:(hh + 1) * 512], start=True, stop=True)
            return last
        kb.step(tensor=kb.comp(bcast))

        def apply1(v, _x=x_sb):
            for c in range(DCH):
                sl = slice(c * TT, (c + 1) * TT)
                last = v.tensor_tensor(sq_sb[:, sl], _x[:, sl], ps_m[:, :], op=ALU.subtract)
            return last

        def apply2(v):
            for c in range(DCH):
                sl = slice(c * TT, (c + 1) * TT)
                last = v.tensor_tensor(y_sb[:, sl], sq_sb[:, sl], ps_a[:, :], op=ALU.mult)
            return last
        kb.step(vector=kb.comp(apply1))
        kb.step(vector=kb.comp(apply2))
        kb.step(sync=kb.dma("sync", fm(dst_hbm, t0, TT), sb3(y_sb[:, :], TT)))



def emit_l0_attn(kb, xT, lnT, x1T, w, scratch):
    nc = kb.nc
    wqkv = kb.sbuf("a_wqkv", [128, DCH * 3072], BF16)
    wout = kb.sbuf("a_wout", [128, DCH * D], BF16)
    bout = kb.sbuf("a_bout", [128, DCH], F32)
    ones_b = kb.sbuf("a_1b", [128, 1], BF16)
    ones_rb = kb.sbuf("a_1rb", [1, 128], F32)
    kb.step(
        gpsimd=kb.dmas([
            (sb3(wqkv[:, :], 3072), w["L0a_wqkvT"].rearrange("(c p) e -> p c e", p=128)),
            (sb3(wout[:, :], D), w["L0a_woutT"].rearrange("(c p) e -> p c e", p=128)),
        ]),
        scalar=kb.dma("scalar", bout[:, :], w["L0a_bout"]),
        vector=kb.comp(lambda v: (v.memset(ones_b[:], 1.0),
                                  v.memset(ones_rb[:], 1.0))[-1]),
    )
    k_tok, v_tok, ksum_h = scratch["k_tok"], scratch["v_tok"], scratch["ksum"]
    kv_f = kb.sbuf("a3_kv", [128, DCH * D], F32)
    kv_b = kb.sbuf("a3_kvb", [128, DCH * D], BF16)

    a2 = kb.scopes
    kb.scopes.append(ExitStack())
    TTILE = 128
    lt = kb.sbuf("a2_lt", [128, DCH * TTILE], BF16)
    k_sb = kb.sbuf("a2_k", [128, D], BF16)
    v_sb = kb.sbuf("a2_v", [128, D], BF16)
    ps_k = [kb.psum("a2_pk%d" % h, [128, 512]) for h in range(2)]
    ps_v = [kb.psum("a2_pv%d" % h, [128, 512]) for h in range(2)]
    ps_ks = kb.psum("a2_ks", [1, D])
    ks_row = kb.sbuf("a2_ksr", [1, D], F32)
    ntile = T // TTILE
    for i in range(ntile):
        t0 = i * TTILE
        kb.step(sync=kb.dma("sync", sb3(lt[:, :], TTILE), fm(lnT, t0, TTILE)))

        def mmkv(t, _i=i):
            for h in range(2):
                for c in range(DCH):
                    t.matmul(ps_k[h][:, :], lt[:, c * TTILE:(c + 1) * TTILE],
                             wqkv[:, c * 3072 + 1024 + h * 512: c * 3072 + 1024 + (h + 1) * 512],
                             start=(c == 0), stop=(c == DCH - 1))
            for h in range(2):
                for c in range(DCH):
                    last = t.matmul(ps_v[h][:, :], lt[:, c * TTILE:(c + 1) * TTILE],
                                    wqkv[:, c * 3072 + 2048 + h * 512: c * 3072 + 2048 + (h + 1) * 512],
                                    start=(c == 0), stop=(c == DCH - 1))
            return last
        kb.step(tensor=kb.comp(mmkv))

        def evac(s, _i=i):
            s.activation(k_sb[:, 0:512], ps_k[0][:, :], AF.Relu)
            s.activation(k_sb[:, 512:1024], ps_k[1][:, :], AF.Relu)
            s.copy(v_sb[:, 0:512], ps_v[0][:, :])
            return s.copy(v_sb[:, 512:1024], ps_v[1][:, :])
        def keps(v, _i=i):
            return v.tensor_scalar(k_sb[:, :], k_sb[:, :], 1e-6, None, op0=ALU.add)
        kb.step(scalar=kb.comp(evac))
        kb.step(vector=kb.comp(keps))

        def ksmm(t, _i=i):
            t.matmul(ps_ks[0:1, 0:512], ones_b[:, :], k_sb[:, 0:512],
                     start=(_i == 0), stop=(_i == ntile - 1))
            return t.matmul(ps_ks[0:1, 512:1024], ones_b[:, :], k_sb[:, 512:1024],
                            start=(_i == 0), stop=(_i == ntile - 1))
        kb.step(
            tensor=kb.comp(ksmm),
            sync=kb.dma("sync", k_tok[t0:t0 + TTILE, :], k_sb[:, :]),
            gpsimd=kb.dma("gpsimd", v_tok[t0:t0 + TTILE, :], v_sb[:, :]),
        )
    kb.step(vector=kb.comp(lambda v: v.tensor_copy(ks_row[:, :], ps_ks[:, :])))
    kb.step(sync=kb.dma("sync", ksum_h.ap(), ks_row[:, :]))

    kb.scopes.pop().close()
    kb.scopes.append(ExitStack())
    kt = kb.sbuf("a3_kt", [128, D], BF16)
    vt = kb.sbuf("a3_vt", [128, 512], BF16)
    ps_kv = [kb.psum("a3_p%d" % c, [128, 512]) for c in range(DCH)]
    for half in range(2):
        e0 = half * 512
        for i in range(ntile):
            t0 = i * TTILE
            kb.step(sync=kb.dma("sync", kt[:, :], k_tok[t0:t0 + TTILE, :]),
                    gpsimd=kb.dma("gpsimd", vt[:, :], v_tok[t0:t0 + TTILE, e0:e0 + 512]))

            def mmkv2(t, _i=i):
                for c in range(DCH):
                    last = t.matmul(ps_kv[c][:, :], kt[:, c * 128:(c + 1) * 128], vt[:, :],
                                    start=(_i == 0), stop=(_i == ntile - 1))
                return last
            kb.step(tensor=kb.comp(mmkv2))

        def evkv(v, _e0=e0):
            for c in range(DCH):
                v.tensor_copy(kv_f[:, c * D + _e0: c * D + _e0 + 512], ps_kv[c][:, :])
                last = v.tensor_copy(kv_b[:, c * D + _e0: c * D + _e0 + 512], ps_kv[c][:, :])
            return last
        kb.step(vector=kb.comp(evkv))

    kb.scopes.pop().close()
    kb.scopes.append(ExitStack())
    TQ = 512
    ln_db = [kb.sbuf("a4_ln%d" % k, [128, DCH * TQ], BF16) for k in range(2)]
    x_db = [kb.sbuf("a4_xd%d" % k, [128, DCH * TQ], F32) for k in range(2)]
    q_sb = kb.sbuf("a4_q", [128, DCH * TQ], BF16)
    at_sb = kb.sbuf("a4_at", [128, DCH * TQ], BF16)
    x1_sb = kb.sbuf("a4_x1", [128, DCH * TQ], F32)
    z_row = kb.sbuf("a4_z", [1, TQ], F32)
    zbc_sb = kb.sbuf("a4_zb", [128, TQ], F32)
    ksum_c = kb.sbuf("a4_ks", [128, DCH], BF16)
    ps_qp = [kb.psum("a4_pq%d" % k, [128, TQ]) for k in range(2)]
    ps_z = kb.psum("a4_pz", [1, TQ])
    ps_zb = kb.psum("a4_pzb", [128, TQ])
    ps_ap = [kb.psum("a4_pa%d" % k, [128, TQ]) for k in range(2)]
    ps_yp = [kb.psum("a4_py%d" % k, [128, TQ]) for k in range(2)]
    kb.step(gpsimd=kb.dma("gpsimd", ksum_c[:, :],
                          ksum_h.ap().rearrange("(c p) -> p c", p=128), slow=True))

    NT4 = T // TQ
    kb.step(sync=kb.dma("sync", sb3(ln_db[0][:, :], TQ), fm(lnT, 0, TQ)),
            gpsimd=kb.dma("gpsimd", sb3(x_db[0][:, :], TQ), fm(xT, 0, TQ)))
    for i in range(NT4):
        t0 = i * TQ
        ln_t = ln_db[i % 2]
        x_t = x_db[i % 2]
        def mmq(t, _ec, _ln=ln_t):
            for c in range(DCH):
                last = t.matmul(ps_qp[_ec % 2][:, :],
                                wqkv[:, c * 3072 + _ec * 128: c * 3072 + _ec * 128 + 128],
                                _ln[:, c * TQ:(c + 1) * TQ],
                                start=(c == 0), stop=(c == DCH - 1))
            return last

        def evq(s, _ec):
            return s.activation(q_sb[:, _ec * TQ:(_ec + 1) * TQ], ps_qp[_ec % 2][:, :], AF.Relu)
        for ec in range(DCH + 1):
            parts = {}
            if ec < DCH:
                parts["tensor"] = kb.comp(lambda t, _e=ec, _f=mmq: _f(t, _e))
            if ec > 0:
                parts["scalar"] = kb.comp(lambda s, _e=ec - 1, _f=evq: _f(s, _e))
            if ec == 0 and i + 1 < NT4:
                parts["sync"] = kb.dma("sync", sb3(ln_db[(i + 1) % 2][:, :], TQ),
                                       fm(lnT, (i + 1) * TQ, TQ))
                parts["gpsimd"] = kb.dma("gpsimd", sb3(x_db[(i + 1) % 2][:, :], TQ),
                                         fm(xT, (i + 1) * TQ, TQ))
            kb.step(**parts)
        kb.step(vector=kb.comp(lambda v: v.tensor_scalar(q_sb[:, :], q_sb[:, :], 1e-6, None, op0=ALU.add)))
        def mmz(t):
            for c in range(DCH):
                last = t.matmul(ps_z[:, :], ksum_c[:, c:c + 1], q_sb[:, c * TQ:(c + 1) * TQ],
                                start=(c == 0), stop=(c == DCH - 1))
            return last
        kb.step(tensor=kb.comp(mmz))
        kb.step(vector=kb.comp(lambda v: v.tensor_scalar(z_row[:, :], ps_z[:, :], 1e-6,
                                                         None, op0=ALU.add)))
        kb.step(vector=kb.comp(lambda v: v.reciprocal(z_row[:, :], z_row[:, :])))
        kb.step(tensor=kb.comp(lambda t: t.matmul(ps_zb[:, :], ones_rb[:, :], z_row[:, :],
                                                  start=True, stop=True)))
        kb.step(vector=kb.comp(lambda v: v.tensor_copy(zbc_sb[:, :], ps_zb[:, :])))
        def mma(t, _ec):
            for c in range(DCH):
                last = t.matmul(ps_ap[_ec % 2][:, :],
                                kv_b[:, c * D + _ec * 128: c * D + _ec * 128 + 128],
                                q_sb[:, c * TQ:(c + 1) * TQ],
                                start=(c == 0), stop=(c == DCH - 1))
            return last

        def eva(v, _ec):
            return v.scalar_tensor_tensor(at_sb[:, _ec * TQ:(_ec + 1) * TQ],
                                          ps_ap[_ec % 2][:, :], 1.0, zbc_sb[:, :],
                                          op0=ALU.mult, op1=ALU.mult)
        for ec in range(DCH + 1):
            parts = {}
            if ec < DCH:
                parts["tensor"] = kb.comp(lambda t, _e=ec, _f=mma: _f(t, _e))
            if ec > 0:
                parts["vector"] = kb.comp(lambda v, _e=ec - 1, _f=eva: _f(v, _e))
            kb.step(**parts)
        def mmyy(t, _fc):
            for ec in range(DCH):
                last = t.matmul(ps_yp[_fc % 2][:, :],
                                wout[:, ec * D + _fc * 128: ec * D + _fc * 128 + 128],
                                at_sb[:, ec * TQ:(ec + 1) * TQ],
                                start=(ec == 0), stop=(ec == DCH - 1))
            return last

        def evyy(v, _fc, _x=x_t):
            return v.scalar_tensor_tensor(x1_sb[:, _fc * TQ:(_fc + 1) * TQ],
                                          ps_yp[_fc % 2][:, :], bout[:, _fc:_fc + 1],
                                          _x[:, _fc * TQ:(_fc + 1) * TQ],
                                          op0=ALU.add, op1=ALU.add)
        for fc in range(DCH + 1):
            parts = {}
            if fc < DCH:
                parts["tensor"] = kb.comp(lambda t, _x=fc, _f=mmyy: _f(t, _x))
            if fc > 0:
                parts["vector"] = kb.comp(lambda v, _x=fc - 1, _f=evyy: _f(v, _x))
            kb.step(**parts)
        kb.step(sync=kb.dma("sync", fm(x1T, t0, TQ), sb3(x1_sb[:, :], TQ)))
    kb.scopes.pop().close()



NSEL = 512
DFF0 = 2816


def emit_l0_ffn(kb, x1T, lnT, x2T, w, scr):
    nc = kb.nc
    NT = T // 128
    ln_tok = scr["b_lntok"]
    b_row = scr["b_row"]

    stok = kb.sbuf("b_stok", [128, NT], F32)
    rank_tok = kb.sbuf("b_rank", [128, NT], F32)
    xsel_tok = kb.sbuf("b_xselt", [128, 4 * D], BF16)
    xselT = kb.sbuf("b_xselT", [128, DCH * NSEL], BF16)
    x_attnT = kb.sbuf("b_xat", [128, DCH * NSEL], BF16)
    delta_tok = kb.sbuf("b_dtok", [128, 4 * D], BF16)
    id128b = kb.sbuf("b_idb", [128, 128], BF16)
    id128f = kb.sbuf("b_idf", [128, 128], F32)
    ones_c = kb.sbuf("b_1c", [128, 1], F32)
    ones_r = kb.sbuf("b_1r", [1, 128], F32)
    kb.step(sync=kb.dmas([(id128f[:, :], w["ident_f32"])]),
            vector=kb.comp(lambda v: (v.memset(ones_c[:], 1.0),
                                      v.memset(ones_r[:], 1.0))[-1]))
    ones_cb = kb.sbuf("b_1cb", [128, 1], BF16)
    kb.step(vector=kb.comp(lambda v: (v.tensor_copy(id128b[:, :], id128f[:, :]),
                                      v.memset(ones_cb[:], 1.0))[-1]))

    def _b1():
        x_sb = kb.sbuf("b1x", [128, DCH * 128], F32)
        sq_sb = kb.sbuf("b1q", [128, DCH * 128], F32)
        wrc = kb.sbuf("b1wr", [128, DCH], F32)
        msum = kb.sbuf("b1ms", [128, NT], F32)
        qsum = kb.sbuf("b1qs", [128, NT], F32)
        sraw = kb.sbuf("b1sr", [128, NT], F32)
        m_all = kb.sbuf("b1m", [128, NT], F32)
        nvar = kb.sbuf("b1nv", [128, NT], F32)
        a_all = kb.sbuf("b1a", [128, NT], F32)
        t1 = kb.sbuf("b1t1", [128, NT], F32)
        swrow = kb.sbuf("b1sw", [1, DCH], F32)
        swc = kb.sbuf("b1swc", [128, 1], F32)
        ps_m = kb.psum("b1pm", [128, 1])
        ps_q = kb.psum("b1pq", [128, 1])
        ps_s = kb.psum("b1ps", [128, 1])
        ps_sw = kb.psum("b1psw", [1, DCH])
        ps_swc = kb.psum("b1pswc", [128, 1])
        kb.step(sync=kb.dma("sync", wrc[:, :], w["L0f_wrc"]))
        kb.step(tensor=kb.comp(lambda t: t.matmul(ps_sw[:, :], ones_c[:, :], wrc[:, :],
                                                  start=True, stop=True)))
        kb.step(vector=kb.comp(lambda v: v.tensor_reduce(swrow[0:1, 0:1], ps_sw[:, :],
                                                         axis=AXX, op=ALU.add)))
        kb.step(tensor=kb.comp(lambda t: t.matmul(ps_swc[:, :], ones_r[:, :], swrow[0:1, 0:1],
                                                  start=True, stop=True)))
        kb.step(vector=kb.comp(lambda v: v.tensor_copy(swc[:, :], ps_swc[:, :])))
        for i in range(NT):
            kb.step(sync=kb.dma("sync", sb3(x_sb[:, :], 128), fm(x1T, i * 128, 128)))
            kb.step(scalar=kb.comp(lambda s: s.activation(sq_sb[:, :], x_sb[:, :], AF.Square)))

            def mm(t):
                for c in range(DCH):
                    t.matmul(ps_m[:, :], x_sb[:, c * 128:(c + 1) * 128], ones_c[:, :],
                             start=(c == 0), stop=(c == DCH - 1))
                for c in range(DCH):
                    t.matmul(ps_q[:, :], sq_sb[:, c * 128:(c + 1) * 128], ones_c[:, :],
                             start=(c == 0), stop=(c == DCH - 1))
                for c in range(DCH):
                    last = t.matmul(ps_s[:, :], x_sb[:, c * 128:(c + 1) * 128],
                                    wrc[:, c:c + 1],
                                    start=(c == 0), stop=(c == DCH - 1))
                return last
            kb.step(tensor=kb.comp(mm))

            def collect(v, _i=i):
                v.tensor_copy(msum[:, _i:_i + 1], ps_m[:, :])
                v.tensor_copy(qsum[:, _i:_i + 1], ps_q[:, :])
                return v.tensor_copy(sraw[:, _i:_i + 1], ps_s[:, :])
            kb.step(vector=kb.comp(collect))

        def vstep(fn):
            kb.step(vector=kb.comp(fn))
        vstep(lambda v: (v.tensor_scalar(m_all[:, :], msum[:, :], 1.0 / D, None, op0=ALU.mult),
                         v.tensor_scalar(qsum[:, :], qsum[:, :], 1.0 / D, None, op0=ALU.mult))[-1])
        vstep(lambda v: v.tensor_tensor(nvar[:, :], m_all[:, :], m_all[:, :], op=ALU.mult))
        vstep(lambda v: v.tensor_tensor(nvar[:, :], nvar[:, :], qsum[:, :], op=ALU.subtract))
        kb.step(scalar=kb.comp(lambda s: s.activation(a_all[:, :], nvar[:, :], AF.Ln,
                                                      scale=-1.0, bias=kb.c(1e-5))))
        kb.step(scalar=kb.comp(lambda s: s.activation(a_all[:, :], a_all[:, :], AF.Exp,
                                                      scale=-0.5)))
        vstep(lambda v: v.tensor_scalar(t1[:, :], m_all[:, :], swc[:, 0:1], None, op0=ALU.mult))
        vstep(lambda v: v.tensor_tensor(t1[:, :], sraw[:, :], t1[:, :], op=ALU.subtract))
        vstep(lambda v: v.tensor_tensor(stok[:, :], t1[:, :], a_all[:, :], op=ALU.mult))

    with kb.scope():
        _b1()

    def _b1b():
        lt = kb.sbuf("b1b_lt", [128, DCH * 128], BF16)
        tt_sb = kb.sbuf("b1b_tt", [128, D], BF16)
        ps_t = kb.psum("b1b_pt", [128, D], BF16)
        for i in range(NT):
            kb.step(sync=kb.dma("sync", sb3(lt[:, :], 128), fm(lnT, i * 128, 128)))

            def trs(t):
                for c in range(DCH):
                    last = t.transpose(ps_t[:, c * 128:(c + 1) * 128],
                                       lt[:, c * 128:(c + 1) * 128], id128b[:, :])
                return last
            kb.step(tensor=kb.comp(trs))
            kb.step(vector=kb.comp(lambda v: v.tensor_copy(tt_sb[:, :], ps_t[:, :])))
            kb.step(sync=kb.dma("sync", ln_tok[i * 128:(i + 1) * 128, :], tt_sb[:, :]))

    with kb.scope():
        _b1b()

    def _b2():
        iota512 = kb.sbuf("b2i512", [128, NSEL], F32)
        pt_sb = kb.sbuf("b2pt", [128, NSEL], BF16)
        lt_tok = kb.sbuf("b2lt", [128, D], BF16)
        kb.step(sync=kb.dma("sync", iota512[:, :], w["iota512"]))
        with kb.scope():
            bigbc = kb.sbuf("b2bc", [128, T], F32)
            row = kb.sbuf("b2row", [1, T], F32)
            junk = kb.sbuf("b2jk", [128, T], BF16)
            bounce = kb.sbuf("b2bn", [NT, 128], F32)
            ps_bn = kb.psum("b2pbn", [NT, 128])
            ps_bc = kb.psum("b2pbc", [128, 512])

            def bounce_out(src_sb):
                kb.step(tensor=kb.comp(lambda t: t.transpose(ps_bn[:, :], src_sb, id128f[:, :])))
                kb.step(vector=kb.comp(lambda v: v.tensor_copy(bounce[:, :], ps_bn[:, :])))
                kb.step(sync=kb.dma("sync", b_row.ap(), bounce[:, :]))

            bounce_out(stok[:, :])
            kb.step(sync=kb.dma("sync", row[:, :],
                                b_row.ap().rearrange("i p -> (i p)").rearrange("(a b) -> a b", a=1)))
            for j in range(T // 512):
                kb.step(tensor=kb.comp(lambda t, _j=j: t.matmul(
                    ps_bc[:, :], ones_r[:, :], row[:, _j * 512:(_j + 1) * 512],
                    start=True, stop=True)))
                kb.step(vector=kb.comp(lambda v, _j=j: v.tensor_copy(
                    bigbc[:, _j * 512:(_j + 1) * 512], ps_bc[:, :])))

            for i in range(NT):
                kb.step(vector=kb.comp(lambda v, _i=i: v.tensor_scalar(
                    junk[:, :], bigbc[:, :], stok[:, _i:_i + 1], 0.0,
                    op0=ALU.is_gt, op1=ALU.add, accum_out=rank_tok[:, _i:_i + 1])))
            bounce_out(rank_tok[:, :])

        kb.scopes.append(ExitStack())
        ps_g = [kb.psum("b2pg%d" % sc, [128, D]) for sc in range(4)]
        for i in range(NT):
            kb.step(sync=kb.dma("sync", lt_tok[:, :], ln_tok[i * 128:(i + 1) * 128, :]),
                    vector=kb.comp(lambda v, _i=i: v.tensor_scalar(
                        pt_sb[:, :], iota512[:, :], rank_tok[:, _i:_i + 1], None,
                        op0=ALU.is_equal)))

            def mmg(t, _i=i):
                for sc in range(4):
                    for half in range(2):
                        last = t.matmul(ps_g[sc][:, half * 512:(half + 1) * 512],
                                        pt_sb[:, sc * 128:(sc + 1) * 128],
                                        lt_tok[:, half * 512:(half + 1) * 512],
                                        start=(_i == 0), stop=(_i == NT - 1))
                return last
            kb.step(tensor=kb.comp(mmg))

        def evg(v):
            for sc in range(4):
                last = v.tensor_copy(xsel_tok[:, sc * D:(sc + 1) * D], ps_g[sc][:, :])
            return last
        kb.step(vector=kb.comp(evg))
        kb.scopes.pop().close()
        ps_tx = kb.psum("b2ptx", [128, 512], BF16)
        for dc in range(DCH):
            def trx(t, _dc=dc):
                for sc in range(4):
                    last = t.transpose(ps_tx[:, sc * 128:(sc + 1) * 128],
                                       xsel_tok[:, sc * D + _dc * 128: sc * D + _dc * 128 + 128],
                                       id128b[:, :])
                return last
            kb.step(tensor=kb.comp(trx))
            kb.step(vector=kb.comp(lambda v, _dc=dc: v.tensor_copy(
                xselT[:, _dc * NSEL:(_dc + 1) * NSEL], ps_tx[:, :])))

    with kb.scope():
        _b2()

    def _b3a():
        wqkv = kb.sbuf("b3wq", [128, DCH * 3072], BF16)
        wout = kb.sbuf("b3wo", [128, DCH * D], BF16)
        g1c = kb.sbuf("b3g1", [128, DCH], F32)
        mask = kb.sbuf("b3mk", [128, 4 * NSEL], F32)
        sq_sb = kb.sbuf("b3sq", [128, DCH * NSEL], F32)
        rr_bc = kb.sbuf("b3rr", [128, NSEL], F32)
        rowbuf = kb.sbuf("b3row", [1, NSEL], F32)
        h1T = kb.sbuf("b3h1", [128, DCH * NSEL], BF16)
        qkvT = kb.sbuf("b3qkv", [128, 24 * NSEL], BF16)
        vtok = kb.sbuf("b3vt", [128, 4 * D], BF16)
        s4_sb = kb.sbuf("b3s4", [128, 4 * NSEL], F32)
        cols = kb.sbuf("b3c", [128, 8], F32)
        a4_sb = kb.sbuf("b3a4", [128, 4 * NSEL], BF16)
        at4_sb = kb.sbuf("b3at4", [128, 4 * NSEL], BF16)
        oT = kb.sbuf("b3oT", [128, DCH * NSEL], BF16)
        ps_a = kb.psum("b3pa", [128, NSEL])
        ps_b = kb.psum("b3pb", [128, NSEL])
        ps_s4 = [ps_a, ps_b, kb.psum("b3ps2", [128, NSEL]), kb.psum("b3ps3", [128, NSEL])]
        ps_at2 = kb.psum("b3pat", [128, 2 * NSEL], BF16)
        ps_o = kb.psum("b3po", [128, NSEL])
        kb.step(gpsimd=kb.dmas([
            (sb3(wqkv[:, :], 3072), w["L0f_wqkvT"].rearrange("(c p) e -> p c e", p=128)),
            (sb3(wout[:, :], D), w["L0f_woutT"].rearrange("(c p) e -> p c e", p=128)),
        ]), sync=kb.dmas([
            (g1c[:, :], w["L0f_g1c"]),
            (mask[:, :].rearrange("p (q k) -> p q k", k=NSEL),
             w["mask512"].rearrange("(q p) k -> p q k", p=128)),
        ]))

        def rms_rows(src_T, gcol_sb, out_T, eps):
            kb.step(scalar=kb.comp(lambda s: s.activation(sq_sb[:, :], src_T, AF.Square)))

            def mmq(t):
                for c in range(DCH):
                    last = t.matmul(ps_r[:, :], ones_c[:, :],
                                    sq_sb[:, c * NSEL:(c + 1) * NSEL],
                                    start=(c == 0), stop=(c == DCH - 1))
                return last
            kb.step(tensor=kb.comp(mmq))
            kb.step(scalar=kb.comp(lambda s: s.activation(rowbuf[:, :], ps_r[:, :], AF.Ln,
                                                          scale=1.0 / D, bias=kb.c(eps, p=1))))
            kb.step(scalar=kb.comp(lambda s: s.activation(rowbuf[:, :], rowbuf[:, :], AF.Exp,
                                                          scale=-0.5)))
            kb.step(tensor=kb.comp(lambda t: t.matmul(ps_rr[:, :], ones_r[:, :], rowbuf[:, :],
                                                      start=True, stop=True)))
            kb.step(vector=kb.comp(lambda v: v.tensor_copy(rr_bc[:, :], ps_rr[:, :])))

            def app(v):
                for c in range(DCH):
                    sl = slice(c * NSEL, (c + 1) * NSEL)
                    last = v.scalar_tensor_tensor(out_T[:, sl], src_T[:, sl],
                                                  gcol_sb[:, c:c + 1], rr_bc[:, :],
                                                  op0=ALU.mult, op1=ALU.mult)
                return last
            kb.step(vector=kb.comp(app))

        kb.scopes.append(ExitStack())
        ps_r = kb.psum("b3pr", [1, NSEL])
        ps_rr = kb.psum("b3prr", [128, NSEL])
        rms_rows(xselT[:, :], g1c, h1T, 1e-6)
        kb.scopes.pop().close()

        def mmp(t, _ec):
            for c in range(DCH):
                last = t.matmul(ps_a[:, :] if _ec % 2 == 0 else ps_b[:, :],
                                wqkv[:, c * 3072 + _ec * 128: c * 3072 + _ec * 128 + 128],
                                h1T[:, c * NSEL:(c + 1) * NSEL],
                                start=(c == 0), stop=(c == DCH - 1))
            return last

        def evp(s, _ec):
            return s.activation(qkvT[:, _ec * NSEL:(_ec + 1) * NSEL],
                                ps_a[:, :] if _ec % 2 == 0 else ps_b[:, :],
                                AF.Copy, scale=(0.125 if _ec < 8 else 1.0))
        for ec in range(25):
            parts = {}
            if ec < 24:
                parts["tensor"] = kb.comp(lambda t, _e=ec, _f=mmp: _f(t, _e))
            if ec > 0:
                parts["scalar"] = kb.comp(lambda s, _e=ec - 1, _f=evp: _f(s, _e))
            kb.step(**parts)

        for sc in range(4):
            for half in range(2):
                def mmv(t, _sc=sc, _h=half):
                    for c in range(DCH):
                        last = t.matmul(ps_a[:, :],
                                        h1T[:, c * NSEL + _sc * 128: c * NSEL + _sc * 128 + 128],
                                        wqkv[:, c * 3072 + 2048 + _h * 512: c * 3072 + 2048 + (_h + 1) * 512],
                                        start=(c == 0), stop=(c == DCH - 1))
                    return last
                kb.step(tensor=kb.comp(mmv))
                kb.step(vector=kb.comp(lambda v, _sc=sc, _h=half: v.tensor_copy(
                    vtok[:, _sc * D + _h * 512: _sc * D + (_h + 1) * 512], ps_a[:, :])))

        for h in range(16):
            ec_q = h // 2
            ec_k = 8 + h // 2
            pb = 64 * (h % 2)
            k_ap = qkvT[pb:pb + 64, ec_k * NSEL:(ec_k + 1) * NSEL]

            def mmS4(t, _h=h, _pb=pb, _eq=ec_q, _k=k_ap):
                for qc in range(4):
                    q_ap = qkvT[_pb:_pb + 64, _eq * NSEL + qc * 128: _eq * NSEL + qc * 128 + 128]
                    last = t.matmul(ps_s4[qc][:, :], q_ap, _k, start=True, stop=True)
                return last
            kb.step(tensor=kb.comp(mmS4))
            kb.step(vector=kb.comp(lambda v: [
                v.tensor_tensor(s4_sb[:, qc * NSEL:(qc + 1) * NSEL], ps_s4[qc][:, :],
                                mask[:, qc * NSEL:(qc + 1) * NSEL], op=ALU.add)
                for qc in range(4)][-1]))
            kb.step(vector=kb.comp(lambda v: [
                v.tensor_reduce(cols[:, qc:qc + 1], s4_sb[:, qc * NSEL:(qc + 1) * NSEL],
                                axis=AXX, op=ALU.max) for qc in range(4)][-1]))
            kb.step(vector=kb.comp(lambda v: [
                v.tensor_scalar(s4_sb[:, qc * NSEL:(qc + 1) * NSEL],
                                s4_sb[:, qc * NSEL:(qc + 1) * NSEL],
                                cols[:, qc:qc + 1], None, op0=ALU.subtract)
                for qc in range(4)][-1]))
            kb.step(scalar=kb.comp(lambda s: s.activation(s4_sb[:, :], s4_sb[:, :], AF.Exp)))
            kb.step(vector=kb.comp(lambda v: [
                v.tensor_reduce(cols[:, 4 + qc:5 + qc], s4_sb[:, qc * NSEL:(qc + 1) * NSEL],
                                axis=AXX, op=ALU.add) for qc in range(4)][-1]))
            kb.step(vector=kb.comp(lambda v: v.reciprocal(cols[:, 4:8], cols[:, 4:8])))
            kb.step(vector=kb.comp(lambda v: [
                v.tensor_scalar(a4_sb[:, qc * NSEL:(qc + 1) * NSEL],
                                s4_sb[:, qc * NSEL:(qc + 1) * NSEL],
                                cols[:, 4 + qc:5 + qc], None, op0=ALU.mult)
                for qc in range(4)][-1]))

            for r in range(2):
                def tra4(t, _r=r):
                    for qq in range(2):
                        qc = 2 * _r + qq
                        for kc in range(4):
                            last = t.transpose(
                                ps_at2[:, qq * NSEL + kc * 128: qq * NSEL + kc * 128 + 128],
                                a4_sb[:, qc * NSEL + kc * 128: qc * NSEL + kc * 128 + 128],
                                id128b[:, :])
                    return last
                kb.step(tensor=kb.comp(tra4))
                kb.step(vector=kb.comp(lambda v, _r=r: v.tensor_copy(
                    at4_sb[:, _r * 2 * NSEL:(_r + 1) * 2 * NSEL], ps_at2[:, :])))

            def mmo4(t, _h=h, _pb=pb):
                for qc in range(4):
                    for kc in range(4):
                        last = t.matmul(ps_o[_pb:_pb + 64, qc * 128:(qc + 1) * 128],
                                        vtok[:, kc * D + _h * 64: kc * D + _h * 64 + 64],
                                        at4_sb[:, qc * NSEL + kc * 128: qc * NSEL + kc * 128 + 128],
                                        start=(kc == 0), stop=(kc == 3))
                return last
            kb.step(tensor=kb.comp(mmo4))
            if h % 2 == 1:
                hp = h // 2
                kb.step(vector=kb.comp(lambda v, _hp=hp: v.tensor_copy(
                    oT[:, _hp * NSEL:(_hp + 1) * NSEL], ps_o[:, :])))

        for fc in range(DCH):
            def mmw(t, _fc=fc):
                for ec in range(DCH):
                    last = t.matmul(ps_a[:, :],
                                    wout[:, ec * D + _fc * 128: ec * D + _fc * 128 + 128],
                                    oT[:, ec * NSEL:(ec + 1) * NSEL],
                                    start=(ec == 0), stop=(ec == DCH - 1))
                return last
            kb.step(tensor=kb.comp(mmw))
            kb.step(vector=kb.comp(lambda v, _fc=fc: v.tensor_copy(
                x_attnT[:, _fc * NSEL:(_fc + 1) * NSEL], ps_a[:, :])))

    with kb.scope():
        _b3a()

    def _b3b():
        w1T = kb.sbuf("b4w1", [128, DCH * DFF0], BF16)
        w2T = kb.sbuf("b4w2", [128, DCH * DFF0], BF16)
        w3fc2 = [kb.sbuf("b4w3%d" % k, [128, 22 * 128], BF16) for k in range(2)]
        g2c = kb.sbuf("b4g2", [128, DCH], F32)
        s2T = kb.sbuf("b4s2", [128, DCH * NSEL], BF16)
        sq_sb = kb.sbuf("b4sq", [128, DCH * NSEL], BF16)
        rr_bc = kb.sbuf("b4rr", [128, NSEL], F32)
        rowbuf = kb.sbuf("b4row", [1, NSEL], F32)
        h2T = kb.sbuf("b4h2", [128, DCH * NSEL], BF16)
        uT = kb.sbuf("b4u", [128, 22 * NSEL], BF16)
        tsi2 = [kb.sbuf("b4si%d" % k, [128, NSEL], F32) for k in range(2)]
        dT = kb.sbuf("b4dT", [128, DCH * NSEL], BF16)
        ps_r = kb.psum("b4pr", [1, NSEL])
        ps_rr = kb.psum("b4prr", [128, NSEL])
        ps_1p = [kb.psum("b4p1%d" % k, [128, NSEL]) for k in range(2)]
        ps_2p = [kb.psum("b4p2%d" % k, [128, NSEL]) for k in range(3)]
        ps_tx = kb.psum("b4ptx", [128, 512], BF16)
        kb.step(gpsimd=kb.dmas([
            (sb3(w1T[:, :], DFF0), w["L0f_w1T"].rearrange("(c p) e -> p c e", p=128)),
            (sb3(w2T[:, :], DFF0), w["L0f_w2T"].rearrange("(c p) e -> p c e", p=128)),
        ]), sync=kb.dma("sync", g2c[:, :], w["L0f_g2c"]))

        kb.step(vector=kb.comp(lambda v: [
            v.tensor_tensor(s2T[:, c * NSEL:(c + 1) * NSEL],
                            xselT[:, c * NSEL:(c + 1) * NSEL],
                            x_attnT[:, c * NSEL:(c + 1) * NSEL], op=ALU.add)
            for c in range(DCH)][-1]))

        def rms_rows2(src_T, gcol_sb, out_T, eps):
            kb.step(scalar=kb.comp(lambda s: s.activation(sq_sb[:, :], src_T, AF.Square)))

            def mmq(t):
                for c in range(DCH):
                    last = t.matmul(ps_r[:, :], ones_cb[:, :],
                                    sq_sb[:, c * NSEL:(c + 1) * NSEL],
                                    start=(c == 0), stop=(c == DCH - 1))
                return last
            kb.step(tensor=kb.comp(mmq))
            kb.step(scalar=kb.comp(lambda s: s.activation(rowbuf[:, :], ps_r[:, :], AF.Ln,
                                                          scale=1.0 / D, bias=kb.c(eps, p=1))))
            kb.step(scalar=kb.comp(lambda s: s.activation(rowbuf[:, :], rowbuf[:, :], AF.Exp,
                                                          scale=-0.5)))
            kb.step(tensor=kb.comp(lambda t: t.matmul(ps_rr[:, :], ones_r[:, :], rowbuf[:, :],
                                                      start=True, stop=True)))
            kb.step(vector=kb.comp(lambda v: v.tensor_copy(rr_bc[:, :], ps_rr[:, :])))

            def app(v):
                for c in range(DCH):
                    sl = slice(c * NSEL, (c + 1) * NSEL)
                    last = v.scalar_tensor_tensor(out_T[:, sl], src_T[:, sl],
                                                  gcol_sb[:, c:c + 1], rr_bc[:, :],
                                                  op0=ALU.mult, op1=ALU.mult)
                return last
            kb.step(vector=kb.comp(app))

        rms_rows2(s2T[:, :], g2c, h2T, 1e-6)

        def mmu(t, _jc):
            for c in range(DCH):
                t.matmul(ps_1p[_jc % 2][:, :],
                         w1T[:, c * DFF0 + _jc * 128: c * DFF0 + _jc * 128 + 128],
                         h2T[:, c * NSEL:(c + 1) * NSEL],
                         start=(c == 0), stop=(c == DCH - 1))
            for c in range(DCH):
                last = t.matmul(ps_2p[_jc % 3][:, :],
                                w2T[:, c * DFF0 + _jc * 128: c * DFF0 + _jc * 128 + 128],
                                h2T[:, c * NSEL:(c + 1) * NSEL],
                                start=(c == 0), stop=(c == DCH - 1))
            return last

        def evsi(s, _jc):
            return s.activation(tsi2[_jc % 2][:, :], ps_1p[_jc % 2][:, :], AF.Silu)

        def evmu(v, _jc):
            return v.tensor_tensor(uT[:, _jc * NSEL:(_jc + 1) * NSEL], tsi2[_jc % 2][:, :],
                                   ps_2p[_jc % 3][:, :], op=ALU.mult)
        for jc in range(24):
            parts = {}
            if jc < 22:
                parts["tensor"] = kb.comp(lambda t, _j=jc, _f=mmu: _f(t, _j))
            if 0 < jc <= 22:
                parts["scalar"] = kb.comp(lambda s, _j=jc - 1, _f=evsi: _f(s, _j))
            if jc > 1:
                parts["vector"] = kb.comp(lambda v, _j=jc - 2, _f=evmu: _f(v, _j))
            kb.step(**parts)

        for fc in range(DCH):
            kb.step(gpsimd=kb.dma("gpsimd",
                                  w3fc2[fc % 2][:, :].rearrange("p (c f) -> p c f", f=128),
                                  w["L0f_w3T"][:, fc * 128:(fc + 1) * 128]
                                  .rearrange("(c p) f -> p c f", p=128)))

            def mmf(t, _fc=fc):
                for hc in range(22):
                    last = t.matmul(ps_1p[_fc % 2][:, :],
                                    w3fc2[_fc % 2][:, :][:, hc * 128:(hc + 1) * 128],
                                    uT[:, hc * NSEL:(hc + 1) * NSEL],
                                    start=(hc == 0), stop=(hc == 21))
                return last
            kb.step(tensor=kb.comp(mmf))
            kb.step(vector=kb.comp(lambda v, _fc=fc: v.tensor_tensor(
                dT[:, _fc * NSEL:(_fc + 1) * NSEL], ps_1p[_fc % 2][:, :],
                x_attnT[:, _fc * NSEL:(_fc + 1) * NSEL], op=ALU.add)))

        for sc in range(4):
            for dc in range(DCH):
                kb.step(tensor=kb.comp(lambda t, _sc=sc, _dc=dc: t.transpose(
                    ps_tx[:, (_dc % 4) * 128:((_dc % 4) + 1) * 128],
                    dT[:, _dc * NSEL + _sc * 128: _dc * NSEL + _sc * 128 + 128],
                    id128b[:, :])))
                kb.step(vector=kb.comp(lambda v, _sc=sc, _dc=dc: v.tensor_copy(
                    delta_tok[:, _sc * D + _dc * 128: _sc * D + (_dc + 1) * 128],
                    ps_tx[:, (_dc % 4) * 128:((_dc % 4) + 1) * 128])))

    with kb.scope():
        _b3b()

    def _b4():
        rbc = kb.sbuf("b5rbc", [128, T], F32)
        row = kb.sbuf("b5row", [1, T], F32)
        slotc = kb.sbuf("b5slc", [128, 4], F32)
        p_sc = kb.sbuf("b5p", [128, 4 * 512], BF16)
        x1_t = kb.sbuf("b5x1", [128, DCH * 512], F32)
        ln_t = kb.sbuf("b5ln", [128, DCH * 512], BF16)
        x2_t = kb.sbuf("b5x2", [128, DCH * 512], F32)
        ps_bc = kb.psum("b5pbc", [128, 512])
        ps_y = kb.psum("b5py", [128, 512])
        kb.step(sync=kb.dma("sync", slotc[:, :], w["slotcol4"]))
        kb.step(sync=kb.dma("sync", row[:, :],
                            b_row.ap().rearrange("i p -> (i p)").rearrange("(a b) -> a b", a=1)))
        for j in range(T // 512):
            kb.step(tensor=kb.comp(lambda t, _j=j: t.matmul(
                ps_bc[:, :], ones_r[:, :], row[:, _j * 512:(_j + 1) * 512],
                start=True, stop=True)))
            kb.step(vector=kb.comp(lambda v, _j=j: v.tensor_copy(
                rbc[:, _j * 512:(_j + 1) * 512], ps_bc[:, :])))
        for sp in range(T // 512):
            t0 = sp * 512
            kb.step(sync=kb.dmas([
                (sb3(x1_t[:, :], 512), fm(x1T, t0, 512)),
                (sb3(ln_t[:, :], 512), fm(lnT, t0, 512)),
            ]), vector=kb.comp(lambda v, _sp=sp: [
                v.tensor_scalar(p_sc[:, sc * 512:(sc + 1) * 512],
                                rbc[:, _sp * 512:(_sp + 1) * 512], slotc[:, sc:sc + 1],
                                None, op0=ALU.is_equal)
                for sc in range(4)][-1]))
            for dc in range(DCH):
                def mms(t, _dc=dc):
                    for sc in range(4):
                        last = t.matmul(ps_y[:, :],
                                        delta_tok[:, sc * D + _dc * 128: sc * D + (_dc + 1) * 128],
                                        p_sc[:, sc * 512:(sc + 1) * 512],
                                        start=(sc == 0), stop=(sc == 3))
                    return last
                kb.step(tensor=kb.comp(mms))

                def evs1(v, _dc=dc):
                    sl = slice(_dc * 512, (_dc + 1) * 512)
                    return v.tensor_tensor(x2_t[:, sl], ps_y[:, :], x1_t[:, sl], op=ALU.add)
                def evs2(v, _dc=dc):
                    sl = slice(_dc * 512, (_dc + 1) * 512)
                    return v.tensor_tensor(x2_t[:, sl], x2_t[:, sl], ln_t[:, sl], op=ALU.add)
                kb.step(vector=kb.comp(evs1))
                kb.step(vector=kb.comp(evs2))
            kb.step(sync=kb.dma("sync", fm(x2T, t0, 512), sb3(x2_t[:, :], 512)))

    with kb.scope():
        _b4()



NH1 = 8
DQK = 64
DV1 = 128
CL = 128
NCH = T // CL


def emit_l1_attn(kb, x2T, lnT, x3T, w, scr):
    nc = kb.nc
    qkT_h = scr["c_qkT"]
    sogT_h = scr["c_sogT"]
    ktok_h = scr["c_ktok"]
    vtok_h = scr["c_vtok"]
    if_h = scr["c_if"]
    htok_h = scr["c_htok"]

    ones_c = kb.sbuf("c_1c", [128, 1], F32)
    ones_cb = kb.sbuf("c_1cb", [128, 1], BF16)
    ones_r = kb.sbuf("c_1r", [1, 128], F32)
    id128f = kb.sbuf("c_idf", [128, 128], F32)
    id128b = kb.sbuf("c_idb", [128, 128], BF16)
    kb.step(sync=kb.dma("sync", id128f[:, :], w["ident_f32"]),
            vector=kb.comp(lambda v: (v.memset(ones_c[:], 1.0), v.memset(ones_cb[:], 1.0),
                                      v.memset(ones_r[:], 1.0))[-1]))
    kb.step(vector=kb.comp(lambda v: v.tensor_copy(id128b[:, :], id128f[:, :])))

    def _c1a():
        TQ = 512
        wqk = kb.sbuf("c1wqk", [128, DCH * D], BF16)
        wv = kb.sbuf("c1wv", [128, DCH * D], BF16)
        wog = kb.sbuf("c1wog", [128, DCH * D], BF16)
        ln_t = kb.sbuf("c1ln", [128, DCH * TQ], BF16)
        o_sb2 = [kb.sbuf("c1o%d" % k, [128, TQ], BF16) for k in range(2)]
        kt_sb = kb.sbuf("c1kt", [128, 512], BF16)
        vt_sb = kb.sbuf("c1vt", [128, D], BF16)
        ps_ab = [kb.psum("c1pa%d" % k, [128, TQ]) for k in range(2)]
        ps_b = kb.psum("c1pb", [128, TQ])
        kb.step(gpsimd=kb.dmas([
            (sb3(wqk[:, :], D), w["L1a_wqkT"].rearrange("(c p) e -> p c e", p=128)),
            (sb3(wv[:, :], D), w["L1a_wvT"].rearrange("(c p) e -> p c e", p=128)),
            (sb3(wog[:, :], D), w["L1a_wogT"].rearrange("(c p) e -> p c e", p=128)),
        ]))
        for i in range(T // TQ):
            t0 = i * TQ
            kb.step(sync=kb.dma("sync", sb3(ln_t[:, :], TQ), fm(lnT, t0, TQ)))
            def mmc1(t, _j):
                wsrc, ec = (wqk, _j) if _j < DCH else (wog, _j - DCH)
                for c in range(DCH):
                    last = t.matmul(ps_ab[_j % 2][:, :],
                                    wsrc[:, c * D + ec * 128: c * D + ec * 128 + 128],
                                    ln_t[:, c * TQ:(c + 1) * TQ],
                                    start=(c == 0), stop=(c == DCH - 1))
                return last

            def evc1(s, _j):
                if _j < DCH:
                    return s.activation(o_sb2[_j % 2][:, :], ps_ab[_j % 2][:, :], AF.Copy,
                                        scale=(0.125 if _j < 4 else 1.0))
                return s.activation(o_sb2[_j % 2][:, :], ps_ab[_j % 2][:, :], AF.Sigmoid)

            def stc1(kbx, _j, _t0):
                ec = _j if _j < DCH else _j - DCH
                dst = qkT_h if _j < DCH else sogT_h
                return kb.dma("scalar", dst[ec * 128:(ec + 1) * 128, _t0:_t0 + TQ],
                              o_sb2[_j % 2][:, :])
            for j in range(2 * DCH + 2):
                parts = {}
                if j < 2 * DCH:
                    parts["tensor"] = kb.comp(lambda t, _j=j, _f=mmc1: _f(t, _j))
                if 0 < j <= 2 * DCH:
                    parts["scalar"] = kb.comp(lambda s, _j=j - 1, _f=evc1: _f(s, _j))
                if 1 < j:
                    parts["gpsimd"] = stc1(kb, j - 2, t0)
                kb.step(**parts)
            for sub in range(4):
                def mmkt(t, _sub=sub):
                    for c in range(DCH):
                        last = t.matmul(ps_ab[0][:, :],
                                        ln_t[:, c * TQ + _sub * 128: c * TQ + (_sub + 1) * 128],
                                        wqk[:, c * D + 512: c * D + 1024],
                                        start=(c == 0), stop=(c == DCH - 1))
                    return last
                kb.step(tensor=kb.comp(mmkt))
                kb.step(vector=kb.comp(lambda v: v.tensor_copy(kt_sb[:, :], ps_ab[0][:, :])))
                kb.step(sync=kb.dma("sync", ktok_h[t0 + sub * 128:t0 + (sub + 1) * 128, :], kt_sb[:, :]))
                for half in range(2):
                    def mmvt(t, _sub=sub, _h=half):
                        for c in range(DCH):
                            last = t.matmul(ps_b[:, :],
                                            ln_t[:, c * TQ + _sub * 128: c * TQ + (_sub + 1) * 128],
                                            wv[:, c * D + _h * 512: c * D + (_h + 1) * 512],
                                            start=(c == 0), stop=(c == DCH - 1))
                        return last
                    kb.step(tensor=kb.comp(mmvt))
                    kb.step(vector=kb.comp(lambda v, _h=half: v.tensor_copy(
                        vt_sb[:, _h * 512:(_h + 1) * 512], ps_b[:, :])))
                kb.step(sync=kb.dma("sync", vtok_h[t0 + sub * 128:t0 + (sub + 1) * 128, :], vt_sb[:, :]))

    with kb.scope():
        _c1a()

    def _c1b():
        x_sb = kb.sbuf("cbx", [128, DCH * 128], F32)
        sq_sb = kb.sbuf("cbq", [128, DCH * 128], F32)
        wif = kb.sbuf("cbwif", [128, DCH * 16], F32)
        msum = kb.sbuf("cbms", [128, NCH], F32)
        qsum = kb.sbuf("cbqs", [128, NCH], F32)
        raw = kb.sbuf("cbraw", [128, NCH * 16], F32)
        m_all = kb.sbuf("cbm", [128, NCH], F32)
        nvar = kb.sbuf("cbnv", [128, NCH], F32)
        a_all = kb.sbuf("cba", [128, NCH], F32)
        tmp = kb.sbuf("cbtmp", [128, NCH * 16], F32)
        bif = kb.sbuf("cbb", [128, NCH * 16], F32)
        swbc = kb.sbuf("cbsw", [128, 16], F32)
        swrow = kb.sbuf("cbswr", [1, 16], F32)
        ps_m = kb.psum("cbpm", [128, 1])
        ps_q = kb.psum("cbpq", [128, 1])
        ps_r = kb.psum("cbpr", [128, 16])
        ps_sw = kb.psum("cbpsw", [1, 16])
        ps_swb = kb.psum("cbpswb", [128, 16])
        kb.step(sync=kb.dmas([
            (sb3(wif[:, :], 16), w["L1a_wifT"].rearrange("(c p) e -> p c e", p=128)),
            (bif[:, :], w["L1a_bif"]),
        ]))
        def mmsw(t):
            for c in range(DCH):
                last = t.matmul(ps_sw[:, :], ones_c[:, :], wif[:, c * 16:(c + 1) * 16],
                                start=(c == 0), stop=(c == DCH - 1))
            return last
        kb.step(tensor=kb.comp(mmsw))
        kb.step(vector=kb.comp(lambda v: v.tensor_copy(swrow[:, :], ps_sw[:, :])))
        kb.step(tensor=kb.comp(lambda t: t.matmul(ps_swb[:, :], ones_r[:, :], swrow[:, :],
                                                  start=True, stop=True)))
        kb.step(vector=kb.comp(lambda v: v.tensor_copy(swbc[:, :], ps_swb[:, :])))
        for i in range(NCH):
            kb.step(sync=kb.dma("sync", sb3(x_sb[:, :], 128), fm(x2T, i * 128, 128)))
            kb.step(scalar=kb.comp(lambda s: s.activation(sq_sb[:, :], x_sb[:, :], AF.Square)))

            def mm(t):
                for c in range(DCH):
                    t.matmul(ps_m[:, :], x_sb[:, c * 128:(c + 1) * 128], ones_c[:, :],
                             start=(c == 0), stop=(c == DCH - 1))
                for c in range(DCH):
                    t.matmul(ps_q[:, :], sq_sb[:, c * 128:(c + 1) * 128], ones_c[:, :],
                             start=(c == 0), stop=(c == DCH - 1))
                for c in range(DCH):
                    last = t.matmul(ps_r[:, :], x_sb[:, c * 128:(c + 1) * 128],
                                    wif[:, c * 16:(c + 1) * 16],
                                    start=(c == 0), stop=(c == DCH - 1))
                return last
            kb.step(tensor=kb.comp(mm))

            def collect(v, _i=i):
                v.tensor_copy(msum[:, _i:_i + 1], ps_m[:, :])
                v.tensor_copy(qsum[:, _i:_i + 1], ps_q[:, :])
                return v.tensor_copy(raw[:, _i * 16:(_i + 1) * 16], ps_r[:, :])
            kb.step(vector=kb.comp(collect))

        def vstep(fn):
            kb.step(vector=kb.comp(fn))
        vstep(lambda v: (v.tensor_scalar(m_all[:, :], msum[:, :], 1.0 / D, None, op0=ALU.mult),
                         v.tensor_scalar(qsum[:, :], qsum[:, :], 1.0 / D, None, op0=ALU.mult))[-1])
        vstep(lambda v: v.tensor_tensor(nvar[:, :], m_all[:, :], m_all[:, :], op=ALU.mult))
        vstep(lambda v: v.tensor_tensor(nvar[:, :], nvar[:, :], qsum[:, :], op=ALU.subtract))
        kb.step(scalar=kb.comp(lambda s: s.activation(a_all[:, :], nvar[:, :], AF.Ln,
                                                      scale=-1.0, bias=kb.c(1e-5))))
        kb.step(scalar=kb.comp(lambda s: s.activation(a_all[:, :], a_all[:, :], AF.Exp,
                                                      scale=-0.5)))

        def ptile(fn):
            def run(v):
                for i in range(NCH):
                    last = fn(v, i)
                return last
            vstep(run)
        ptile(lambda v, i: v.tensor_scalar(tmp[:, i * 16:(i + 1) * 16], swbc[:, :],
                                           m_all[:, i:i + 1], None, op0=ALU.mult))
        vstep(lambda v: v.tensor_tensor(tmp[:, :], raw[:, :], tmp[:, :], op=ALU.subtract))
        ptile(lambda v, i: v.tensor_scalar(tmp[:, i * 16:(i + 1) * 16], tmp[:, i * 16:(i + 1) * 16],
                                           a_all[:, i:i + 1], None, op0=ALU.mult))
        vstep(lambda v: v.tensor_tensor(tmp[:, :], tmp[:, :], bif[:, :], op=ALU.add))
        kb.step(scalar=kb.comp(lambda s: s.activation(tmp[:, :], tmp[:, :], AF.Tanh,
                                                      scale=1.0 / 15.0)))
        vstep(lambda v: v.tensor_scalar(tmp[:, :], tmp[:, :], 15.0, None, op0=ALU.mult))
        kb.step(sync=kb.dma("sync",
                            if_h.rearrange("(i p) f -> p i f", p=128),
                            tmp[:, :].rearrange("p (i f) -> p i f", f=16)))

    with kb.scope():
        _c1b()

    def _c2():
        qk_db = [kb.sbuf("c2qk%d" % k, [64, 16 * CL], BF16) for k in range(2)]
        ktk_db = [kb.sbuf("c2kt%d" % k, [128, 512], BF16) for k in range(2)]
        vtk_db = [kb.sbuf("c2vt%d" % k, [128, D], BF16) for k in range(2)]
        ift_db = [kb.sbuf("c2if%d" % k, [128, 16], F32) for k in range(2)]
        floc = kb.sbuf("c2fl", [128, 8], F32)
        g_tok = kb.sbuf("c2g", [128, 8], F32)
        gT = kb.sbuf("c2gT", [8, 128], F32)
        flT = kb.sbuf("c2flT", [8, 128], F32)
        rmA = kb.sbuf("c2rmA", [8, 128], F32)
        rmB = kb.sbuf("c2rmB", [8, 128], F32)
        dT = kb.sbuf("c2dT", [8, 128], F32)
        d_tok = kb.sbuf("c2dt", [128, 8], F32)
        m_tok = kb.sbuf("c2mt", [128, 8], F32)
        em_tok = kb.sbuf("c2em", [128, 8], F32)
        ea_tok = kb.sbuf("c2ea", [128, 8], F32)
        e_tok = kb.sbuf("c2et", [128, 8], F32)
        acol = kb.sbuf("c2ac", [8, 1], F32)
        arow = kb.sbuf("c2ar", [1, 8], F32)
        dlrow = kb.sbuf("c2dlr", [1, 8], F32)
        eacol = kb.sbuf("c2eac", [8, 1], F32)
        earow = kb.sbuf("c2ear", [1, 8], F32)
        ea64 = kb.sbuf("c2ea64", [64, 8], F32)
        t8 = kb.sbuf("c2t8", [8, 1], F32)
        utri = kb.sbuf("c2ut", [128, 128], F32)
        eh8 = kb.sbuf("c2eh", [8, 8 * 128], F32)
        masku = kb.sbuf("c2mk", [128, 8 * CL], F32)
        w_sb = kb.sbuf("c2w", [128, 8 * CL], F32)
        t1_sb = kb.sbuf("c2t1", [128, 8 * CL], F32)
        sw_sb = kb.sbuf("c2sw", [128, 8 * CL], BF16)
        o_sb = kb.sbuf("c2o", [128, 8 * CL], F32)
        num_sb = kb.sbuf("c2num", [128, 8 * CL], F32)
        h_sb = kb.sbuf("c2h", [128, 8 * CL], BF16)
        ke_sb = kb.sbuf("c2ke", [128, 512], BF16)
        qn1 = kb.sbuf("c2qn1", [128, 8], F32)
        qn2 = kb.sbuf("c2qn2", [128, 8], F32)
        dn = kb.sbuf("c2dn", [128, 8], F32)
        rcp = kb.sbuf("c2rcp", [128, 8], F32)
        cn_sb = kb.sbuf("c2cn", [64, 8 * DV1], F32)
        nn_sb = kb.sbuf("c2nn", [64, 8], F32)
        cn_bf = kb.sbuf("c2cnb", [64, 8 * DV1], BF16)
        nn_bf = kb.sbuf("c2nnb", [64, 8], BF16)
        ps_sm = kb.psum("c2psm", [128, 512])
        ps_t8 = ps_sm[0:8, 0:256]
        ps_tr = ps_sm[:, 256:264]
        ps_f = ps_sm[:, 264:272]
        ps_qn = ps_sm[:, 272:288]
        ps_b8 = ps_sm[:, 288:296]
        ps_nn = ps_sm[0:64, 296:304]
        ps_b64 = ps_sm[0:64, 304:312]
        ps_s = kb.psum("c2ps", [128, 8 * CL])
        ps_w = kb.psum("c2pw", [128, 8 * CL])
        ps_o = kb.psum("c2po", [128, 8 * CL])
        ps_cn = ps_o[0:64, :]
        kb.step(sync=kb.dmas([
            (utri[:, :], w["utri128"]),
            (masku[:, :], w["maskU8"]),
            (eh8[:, :], w["eh8"]),
        ]), vector=kb.comp(lambda v: (v.memset(cn_sb[:, :], 0.0), v.memset(nn_sb[:, :], 0.0),
                                      v.memset(cn_bf[:, :], 0.0), v.memset(nn_bf[:, :], 0.0),
                                      v.memset(acol[:, :], 0.0), v.memset(arow[:, :], 0.0))[-1]))

        def vs(fn):
            kb.step(vector=kb.comp(fn))

        def chunk_load(ch):
            t0 = ch * CL
            k = ch % 2
            return kb.dmas([
                (qk_db[k][:, :].rearrange("p (c t) -> p c t", t=CL),
                 qkT_h[:, t0:t0 + CL].rearrange("(c p) t -> p c t", p=64)),
                (ktk_db[k][:, :], ktok_h[t0:t0 + CL, :]),
                (vtk_db[k][:, :], vtok_h[t0:t0 + CL, :]),
                (ift_db[k][:, :], if_h[t0:t0 + CL, :]),
            ])
        kb.step(sync=chunk_load(0))
        for ch in range(NCH):
            t0 = ch * CL
            qk_ch = qk_db[ch % 2]
            ktk = ktk_db[ch % 2]
            vtk = vtk_db[ch % 2]
            if_t = ift_db[ch % 2]
            parts = {"tensor": kb.comp(lambda t, _if=if_t: t.matmul(
                ps_f[:, :], utri[:, :], _if[:, 8:16], start=True, stop=True))}
            if ch + 1 < NCH:
                parts["sync"] = chunk_load(ch + 1)
            kb.step(**parts)
            vs(lambda v: v.tensor_copy(floc[:, :], ps_f[:, :]))
            vs(lambda v, _if=if_t: v.tensor_tensor(g_tok[:, :], _if[:, 0:8], floc[:, :], op=ALU.subtract))
            def tr1(t):
                t.transpose(ps_t8[0:8, 0:128], g_tok[:, :], id128f[:, :])
                return t.transpose(ps_t8[0:8, 128:256], floc[:, :], id128f[:, :])
            kb.step(tensor=kb.comp(tr1))
            vs(lambda v: (v.tensor_copy(gT[:, :], ps_t8[0:8, 0:128]),
                          v.tensor_copy(flT[:, :], ps_t8[0:8, 128:256]))[-1])
            vs(lambda v: v.tensor_copy(rmA[:, :], gT[:, :]))
            cur, nxt = rmA, rmB
            s = 1
            while s < CL:
                def rmstep(v, _c=cur, _n=nxt, _s=s):
                    v.tensor_copy(_n[:, 0:_s], _c[:, 0:_s])
                    return v.tensor_tensor(_n[:, _s:], _c[:, _s:], _c[:, 0:CL - _s], op=ALU.max)
                vs(rmstep)
                cur, nxt = nxt, cur
                s *= 2
            vs(lambda v, _c=cur: v.tensor_scalar(dT[:, :], _c[:, :], acol[:, 0:1], None, op0=ALU.max))
            kb.step(tensor=kb.comp(lambda t: t.transpose(ps_tr[:, :], dT[:, :], id128f[0:8, 0:8])))
            vs(lambda v: v.tensor_copy(d_tok[:, :], ps_tr[:, :]))
            vs(lambda v: v.tensor_tensor(m_tok[:, :], floc[:, :], d_tok[:, :], op=ALU.add))
            kb.step(scalar=kb.comp(lambda s: s.activation(em_tok[:, :], m_tok[:, :], AF.Exp,
                                                          scale=-1.0)))
            kb.step(tensor=kb.comp(lambda t: t.matmul(ps_b8[:, :], ones_r[:, :], arow[:, :],
                                                      start=True, stop=True)))
            vs(lambda v: v.tensor_tensor(ea_tok[:, :], ps_b8[:, :], d_tok[:, :], op=ALU.subtract))
            kb.step(scalar=kb.comp(lambda s: s.activation(ea_tok[:, :], ea_tok[:, :], AF.Exp)))
            def bcD(t):
                for h in range(NH1):
                    last = t.matmul(ps_w[:, h * CL:(h + 1) * CL], eh8[:, h * 128:(h + 1) * 128],
                                    dT[:, :], start=True, stop=True)
                return last
            kb.step(tensor=kb.comp(bcD))

            def wsub(v):
                for h in range(NH1):
                    last = v.tensor_scalar(t1_sb[:, h * CL:(h + 1) * CL],
                                           ps_w[:, h * CL:(h + 1) * CL],
                                           g_tok[:, h:h + 1], None, op0=ALU.subtract)
                return last
            vs(wsub)
            vs(lambda v: v.tensor_tensor(t1_sb[:, :], t1_sb[:, :], masku[:, :], op=ALU.subtract))
            kb.step(scalar=kb.comp(lambda s: s.activation(w_sb[:, :], t1_sb[:, :], AF.Exp,
                                                          scale=-1.0)))
            def mmS(t, _qk=qk_ch):
                for h in range(NH1):
                    last = t.matmul(ps_s[:, h * CL:(h + 1) * CL],
                                    _qk[:, (8 + h) * CL:(9 + h) * CL],
                                    _qk[:, h * CL:(h + 1) * CL],
                                    start=True, stop=True)
                return last
            kb.step(tensor=kb.comp(mmS))
            vs(lambda v: v.tensor_tensor(sw_sb[:, :], ps_s[:, :], w_sb[:, :], op=ALU.mult))

            def mmO(t, _vt=vtk):
                for h in range(NH1):
                    t.matmul(ps_o[:, h * CL:(h + 1) * CL], sw_sb[:, h * CL:(h + 1) * CL],
                             _vt[:, h * DV1:(h + 1) * DV1], start=True, stop=True)
                for h in range(NH1):
                    last = t.matmul(ps_qn[:, h:h + 1], sw_sb[:, h * CL:(h + 1) * CL],
                                    ones_cb[:, :], start=True, stop=True)
                return last
            kb.step(tensor=kb.comp(mmO))

            def mmQC(t, _qk=qk_ch):
                for h in range(NH1):
                    t.matmul(ps_w[:, h * CL:(h + 1) * CL],
                             _qk[:, h * CL:(h + 1) * CL],
                             cn_bf[:, h * DV1:(h + 1) * DV1], start=True, stop=True)
                for h in range(NH1):
                    last = t.matmul(ps_qn[:, 8 + h:9 + h],
                                    _qk[:, h * CL:(h + 1) * CL],
                                    nn_bf[:, h:h + 1], start=True, stop=True)
                return last
            kb.step(tensor=kb.comp(mmQC))
            vs(lambda v: v.tensor_copy(o_sb[:, :], ps_o[:, :]))

            def addinter(v):
                for h in range(NH1):
                    last = v.scalar_tensor_tensor(num_sb[:, h * CL:(h + 1) * CL],
                                                  ps_w[:, h * CL:(h + 1) * CL],
                                                  ea_tok[:, h:h + 1],
                                                  o_sb[:, h * CL:(h + 1) * CL],
                                                  op0=ALU.mult, op1=ALU.add)
                return last
            vs(addinter)
            vs(lambda v: v.tensor_copy(qn1[:, :], ps_qn[:, 0:8]))
            vs(lambda v: v.tensor_tensor(qn2[:, :], ea_tok[:, :], ps_qn[:, 8:16], op=ALU.mult))
            vs(lambda v: v.tensor_tensor(qn1[:, :], qn1[:, :], qn2[:, :], op=ALU.add))
            kb.step(scalar=kb.comp(lambda s: s.activation(qn1[:, :], qn1[:, :], AF.Abs)))
            vs(lambda v: v.tensor_tensor(dn[:, :], qn1[:, :], em_tok[:, :], op=ALU.max))
            vs(lambda v: v.tensor_scalar(dn[:, :], dn[:, :], 1e-6, None, op0=ALU.add))
            vs(lambda v: v.reciprocal(rcp[:, :], dn[:, :]))

            def hdiv(v):
                for h in range(NH1):
                    last = v.tensor_scalar(h_sb[:, h * CL:(h + 1) * CL],
                                           num_sb[:, h * CL:(h + 1) * CL],
                                           rcp[:, h:h + 1], None, op0=ALU.mult)
                return last
            vs(hdiv)
            kb.step(sync=kb.dma("sync", htok_h[t0:t0 + CL, :], h_sb[:, :]))
            kb.step(tensor=kb.comp(lambda t: t.transpose(ps_t8[0:1, 0:8], dT[:, 127:128],
                                                         id128f[0:8, 0:8])))
            vs(lambda v: v.tensor_copy(dlrow[:, :], ps_t8[0:1, 0:8]))
            kb.step(tensor=kb.comp(lambda t: t.matmul(ps_b8[:, :], ones_r[:, :], dlrow[:, :],
                                                      start=True, stop=True)))
            vs(lambda v: v.tensor_tensor(e_tok[:, :], g_tok[:, :], ps_b8[:, :], op=ALU.subtract))
            kb.step(scalar=kb.comp(lambda s: s.activation(e_tok[:, :], e_tok[:, :], AF.Exp)))

            def kemul(v, _kt=ktk):
                for h in range(NH1):
                    last = v.tensor_scalar(ke_sb[:, h * DQK:(h + 1) * DQK],
                                           _kt[:, h * DQK:(h + 1) * DQK],
                                           e_tok[:, h:h + 1], None, op0=ALU.mult)
                return last
            vs(kemul)

            def mmCN(t, _vt=vtk):
                for h in range(NH1):
                    t.matmul(ps_cn[:, h * DV1:(h + 1) * DV1], ke_sb[:, h * DQK:(h + 1) * DQK],
                             _vt[:, h * DV1:(h + 1) * DV1], start=True, stop=True)
                for h in range(NH1):
                    last = t.matmul(ps_nn[:, h:h + 1], ke_sb[:, h * DQK:(h + 1) * DQK],
                                    ones_cb[:, :], start=True, stop=True)
                return last
            kb.step(tensor=kb.comp(mmCN))
            vs(lambda v: v.tensor_tensor(t8[:, :], acol[:, :], dT[:, 127:128], op=ALU.subtract))
            kb.step(scalar=kb.comp(lambda s: s.activation(t8[:, :], t8[:, :], AF.Exp)))
            vs(lambda v: v.tensor_copy(eacol[:, :], t8[:, :]))
            kb.step(tensor=kb.comp(lambda t: t.transpose(ps_t8[0:1, 0:8], eacol[:, :],
                                                         id128f[0:8, 0:8])))
            vs(lambda v: v.tensor_copy(earow[:, :], ps_t8[0:1, 0:8]))
            kb.step(tensor=kb.comp(lambda t: t.matmul(ps_b64[:, :], ones_r[0:1, 0:64],
                                                      earow[:, :], start=True, stop=True)))
            vs(lambda v: v.tensor_copy(ea64[:, :], ps_b64[:, :]))

            def cnup(v):
                for h in range(NH1):
                    last = v.scalar_tensor_tensor(cn_sb[:, h * DV1:(h + 1) * DV1],
                                                  cn_sb[:, h * DV1:(h + 1) * DV1],
                                                  ea64[:, h:h + 1],
                                                  ps_cn[:, h * DV1:(h + 1) * DV1],
                                                  op0=ALU.mult, op1=ALU.add)
                return last
            vs(cnup)

            def nnup(v):
                for h in range(NH1):
                    last = v.scalar_tensor_tensor(nn_sb[:, h:h + 1], nn_sb[:, h:h + 1],
                                                  ea64[:, h:h + 1], ps_nn[:, h:h + 1],
                                                  op0=ALU.mult, op1=ALU.add)
                return last
            vs(nnup)
            vs(lambda v: (v.tensor_copy(cn_bf[:, :], cn_sb[:, :]),
                          v.tensor_copy(nn_bf[:, :], nn_sb[:, :]))[-1])
            vs(lambda v: v.tensor_tensor(acol[:, :], dT[:, 127:128], flT[:, 127:128], op=ALU.add))
            kb.step(tensor=kb.comp(lambda t: t.transpose(ps_t8[0:1, 0:8], acol[:, :],
                                                         id128f[0:8, 0:8])))
            vs(lambda v: v.tensor_copy(arow[:, :], ps_t8[0:1, 0:8]))

    with kb.scope():
        _c2()

    def _c3():
        TQ = 512
        wout = kb.sbuf("c3wo", [128, DCH * D], BF16)
        lnw = kb.sbuf("c3lw", [128, DCH], F32)
        lnb = kb.sbuf("c3lb", [128, DCH], F32)
        h_tok = kb.sbuf("c3ht", [128, D], BF16)
        hT = kb.sbuf("c3hT", [128, DCH * TQ], BF16)
        sq_sb = kb.sbuf("c3sq", [128, DCH * TQ], F32)
        sog = kb.sbuf("c3sog", [128, DCH * TQ], BF16)
        x2_t = kb.sbuf("c3x2", [128, DCH * TQ], F32)
        hout = kb.sbuf("c3ho", [128, DCH * TQ], BF16)
        rows = kb.sbuf("c3rw", [1, 2 * TQ], F32)
        mbc = kb.sbuf("c3mb", [128, TQ], F32)
        abc = kb.sbuf("c3ab", [128, TQ], F32)
        x3_t = kb.sbuf("c3x3", [128, DCH * TQ], F32)
        ps_tx = kb.psum("c3ptx", [128, 512], BF16)
        ps_s = kb.psum("c3psu", [1, TQ])
        ps_q = kb.psum("c3psq", [1, TQ])
        ps_m = kb.psum("c3pm", [128, TQ])
        ps_a = kb.psum("c3pa", [128, TQ])
        ps_yq = [kb.psum("c3py%d" % k, [128, TQ]) for k in range(2)]
        kb.step(gpsimd=kb.dma("gpsimd", sb3(wout[:, :], D),
                              w["L1a_woutT"].rearrange("(c p) e -> p c e", p=128)),
                sync=kb.dmas([(lnw[:, :], w["L1a_lnwc"]), (lnb[:, :], w["L1a_lnbc"])]))
        m_row = rows[0:1, 0:TQ]
        a_row = rows[0:1, TQ:2 * TQ]
        for i in range(T // TQ):
            t0 = i * TQ
            for sub in range(4):
                kb.step(sync=kb.dma("sync", h_tok[:, :], htok_h[t0 + sub * 128:t0 + (sub + 1) * 128, :]))

                def trh1(t, _sub=sub):
                    for c in range(4):
                        last = t.transpose(ps_tx[:, c * 128:(c + 1) * 128],
                                           h_tok[:, c * 128:(c + 1) * 128], id128b[:, :])
                    return last
                def cph1(v, _sub=sub):
                    for c in range(4):
                        last = v.tensor_copy(hT[:, c * TQ + _sub * 128: c * TQ + (_sub + 1) * 128],
                                             ps_tx[:, c * 128:(c + 1) * 128])
                    return last
                def trh2(t, _sub=sub):
                    for c in range(4, DCH):
                        last = t.transpose(ps_tx[:, (c - 4) * 128:(c - 3) * 128],
                                           h_tok[:, c * 128:(c + 1) * 128], id128b[:, :])
                    return last
                def cph2(v, _sub=sub):
                    for c in range(4, DCH):
                        last = v.tensor_copy(hT[:, c * TQ + _sub * 128: c * TQ + (_sub + 1) * 128],
                                             ps_tx[:, (c - 4) * 128:(c - 3) * 128])
                    return last
                kb.step(tensor=kb.comp(trh1))
                kb.step(vector=kb.comp(cph1))
                kb.step(tensor=kb.comp(trh2))
                kb.step(vector=kb.comp(cph2))
            kb.step(sync=kb.dmas([
                (sb3(sog[:, :], TQ), fm(sogT_h, t0, TQ)),
                (sb3(x2_t[:, :], TQ), fm(x2T, t0, TQ)),
            ]), scalar=kb.comp(lambda s: s.activation(sq_sb[:, :], hT[:, :], AF.Square)))

            def mmstat(t):
                for c in range(DCH):
                    t.matmul(ps_s[:, :], ones_cb[:, :], hT[:, c * TQ:(c + 1) * TQ],
                             start=(c == 0), stop=(c == DCH - 1))
                for c in range(DCH):
                    last = t.matmul(ps_q[:, :], ones_c[:, :], sq_sb[:, c * TQ:(c + 1) * TQ],
                                    start=(c == 0), stop=(c == DCH - 1))
                return last
            kb.step(tensor=kb.comp(mmstat))

            kb.step(vector=kb.comp(lambda v: (
                v.tensor_scalar(m_row, ps_s[:, :], 1.0 / D, None, op0=ALU.mult),
                v.tensor_scalar(a_row, ps_q[:, :], 1.0 / D, None, op0=ALU.mult))[-1]))
            kb.step(vector=kb.comp(lambda v: v.tensor_tensor(ps_s[0:1, :], m_row, m_row,
                                                             op=ALU.mult)))
            kb.step(vector=kb.comp(lambda v: v.tensor_tensor(a_row, a_row, ps_s[0:1, :],
                                                             op=ALU.subtract)))
            kb.step(scalar=kb.comp(lambda s: s.activation(a_row, a_row, AF.Ln,
                                                          bias=kb.c(1e-6, p=1))))
            kb.step(scalar=kb.comp(lambda s: s.activation(a_row, a_row, AF.Exp, scale=-0.5)))

            def bc(t):
                t.matmul(ps_m[:, :], ones_r[:, :], m_row, start=True, stop=True)
                return t.matmul(ps_a[:, :], ones_r[:, :], a_row, start=True, stop=True)
            kb.step(tensor=kb.comp(bc))
            kb.step(vector=kb.comp(lambda v: (v.tensor_copy(mbc[:, :], ps_m[:, :]),
                                              v.tensor_copy(abc[:, :], ps_a[:, :]))[-1]))

            def ap1(v):
                for c in range(DCH):
                    sl = slice(c * TQ, (c + 1) * TQ)
                    last = v.tensor_tensor(sq_sb[:, sl], hT[:, sl], mbc[:, :], op=ALU.subtract)
                return last
            def ap2(v):
                for c in range(DCH):
                    sl = slice(c * TQ, (c + 1) * TQ)
                    last = v.tensor_tensor(sq_sb[:, sl], sq_sb[:, sl], abc[:, :], op=ALU.mult)
                return last
            def ap3(v):
                for c in range(DCH):
                    sl = slice(c * TQ, (c + 1) * TQ)
                    last = v.tensor_scalar(sq_sb[:, sl], sq_sb[:, sl], lnw[:, c:c + 1],
                                           None, op0=ALU.mult)
                return last
            def ap4(v):
                for c in range(DCH):
                    sl = slice(c * TQ, (c + 1) * TQ)
                    last = v.scalar_tensor_tensor(hout[:, sl], sq_sb[:, sl], lnb[:, c:c + 1],
                                                  sog[:, sl], op0=ALU.add, op1=ALU.mult)
                return last
            kb.step(vector=kb.comp(ap1))
            kb.step(vector=kb.comp(ap2))
            kb.step(vector=kb.comp(ap3))
            kb.step(vector=kb.comp(ap4))

            def mmy3(t, _fc):
                for vc in range(DCH):
                    last = t.matmul(ps_yq[_fc % 2][:, :],
                                    wout[:, vc * D + _fc * 128: vc * D + _fc * 128 + 128],
                                    hout[:, vc * TQ:(vc + 1) * TQ],
                                    start=(vc == 0), stop=(vc == DCH - 1))
                return last

            def evy3(v, _fc):
                return v.tensor_tensor(x3_t[:, _fc * TQ:(_fc + 1) * TQ], ps_yq[_fc % 2][:, :],
                                       x2_t[:, _fc * TQ:(_fc + 1) * TQ], op=ALU.add)
            for fc in range(DCH + 1):
                parts = {}
                if fc < DCH:
                    parts["tensor"] = kb.comp(lambda t, _x=fc, _f=mmy3: _f(t, _x))
                if fc > 0:
                    parts["vector"] = kb.comp(lambda v, _x=fc - 1, _f=evy3: _f(v, _x))
                kb.step(**parts)
            kb.step(sync=kb.dma("sync", fm(x3T, t0, TQ), sb3(x3_t[:, :], TQ)))

    with kb.scope():
        _c3()



TOWN = 4096
CAP_E = 640
BIGI = 1024.0


def fm_dyn(hbm_ap, idx_sv, tt):
    return hbm_ap[:, bass.ts(idx_sv, tt)].rearrange("(c p) t -> p c t", p=128)


def emit_l1_ffn(kb, x3T, out_tok, w, scr):
    nc = kb.nc
    lnT = scr["d_ln"]
    acc_h = scr["d_acc"]
    fe_h = scr["d_fe"]
    pt_h = scr["d_pt"]
    hf_sb = kb.sbuf("d_hf", [1, 1], I32)
    slot = kb.sbuf("d_slot", [128, TOWN // 128], F32)
    sloti = kb.sbuf("d_sloti", [128, TOWN // 128], I32)
    regs = {}
    kb.step(sync=kb.dma("sync", hf_sb[:, :], w["own_hf"]))

    def loadreg(eng_name):
        def fn(e):
            regs[eng_name] = e.value_load(hf_sb[0:1, 0:1])
            return e.nop()
        return kb.comp(fn)
    kb.step(sync=loadreg("sync"))

    x3own = scr["d_x3own"]
    def cpy(e, semh):
        for q in range(4):
            e.dma_start(out=x3own[:, q * 1024:(q + 1) * 1024],
                        in_=x3T[:, bass.ts(regs["sync"] * 4 + q, 1024)]).then_inc(semh, 16)
    kb.step(sync=("dma", 64, cpy))

    def xsrc(t0, tt):
        return fm(x3own, t0, tt)

    with kb.scope():
        emit_layernorm(kb, xsrc, lnT, "ln3", ntok=TOWN)

    def _d2():
        NT = TOWN // 128
        x_sb = kb.sbuf("d2x", [128, DCH * 128], F32)
        sq_sb = kb.sbuf("d2q", [128, DCH * 128], F32)
        rwT = kb.sbuf("d2rw", [128, DCH * 8], F32)
        ones_c = kb.sbuf("d2o", [128, 1], F32)
        ones_r = kb.sbuf("d2or", [1, 128], F32)
        iota_big = kb.sbuf("d2ib", [128, 8], F32)
        srbc = kb.sbuf("d2sr", [128, 8], F32)
        msum = kb.sbuf("d2ms", [128, NT], F32)
        qsum = kb.sbuf("d2qs", [128, NT], F32)
        raw = kb.sbuf("d2raw", [128, NT * 8], F32)
        tmp = kb.sbuf("d2tmp", [128, NT * 8], F32)
        lall = kb.sbuf("d2l", [128, NT * 8], F32)
        m_all = kb.sbuf("d2m", [128, NT], F32)
        q_all = kb.sbuf("d2qq", [128, NT], F32)
        mm2 = kb.sbuf("d2m2", [128, NT], F32)
        nvar = kb.sbuf("d2nv", [128, NT], F32)
        a_all = kb.sbuf("d2a", [128, NT], F32)
        mx_all = kb.sbuf("d2mx", [128, NT], F32)
        sm_all = kb.sbuf("d2sm", [128, NT], F32)
        rc_all = kb.sbuf("d2rc", [128, NT], F32)
        pm_all = kb.sbuf("d2pm_", [128, NT], F32)
        ptil = kb.sbuf("d2pt", [128, NT], F32)
        band = kb.sbuf("d2bd", [128, NT], F32)
        ecol = kb.sbuf("d2ec", [128, NT], F32)
        fe_sb = kb.sbuf("d2fe", [128, NT], F32)
        pbc = kb.sbuf("d2pb", [128, TOWN], F32)
        prow = kb.sbuf("d2pr", [1, TOWN], F32)
        junk = kb.sbuf("d2jk", [128, TOWN], BF16)
        r1 = kb.sbuf("d2r1", [128, NT], F32)
        r2 = kb.sbuf("d2r2", [128, NT], F32)
        sr_col = kb.sbuf("d2src", [8, 1], F32)
        sr_row = kb.sbuf("d2srw", [1, 8], F32)
        id128 = kb.sbuf("d2id", [128, 128], F32)
        fe_t = kb.sbuf("d2fet", [NT, 128], F32)
        ps_m = kb.psum("d2pm", [128, 1])
        ps_q = kb.psum("d2pq", [128, 1])
        ps_r = kb.psum("d2praw", [128, 8])
        ps_sr = kb.psum("d2psr", [8, 1])
        ps_srr = kb.psum("d2psrr", [1, 8])
        ps_bc = kb.psum("d2pbc", [128, 512])
        ps_fe = kb.psum("d2pfe", [NT, 128])

        kb.step(sync=kb.dmas([
            (sb3(rwT[:, :], 8), w["L1f_rwT"].rearrange("(c p) e -> p c e", p=128)),
            (id128[:, :], w["ident_f32"]),
            (iota_big[:, :], w["iota8_big"]),
        ]), vector=kb.comp(lambda v: (v.memset(ones_c[:], 1.0),
                                      v.memset(ones_r[:], 1.0))[-1]))

        def srmm(t):
            for c in range(DCH):
                last = t.matmul(ps_sr[:, :], rwT[:, c * 8:(c + 1) * 8], ones_c[:, :],
                                start=(c == 0), stop=(c == DCH - 1))
            return last
        kb.step(tensor=kb.comp(srmm))
        kb.step(vector=kb.comp(lambda v: v.tensor_copy(sr_col[:, :], ps_sr[:, :])))
        kb.step(tensor=kb.comp(lambda t: t.transpose(ps_srr[:, :], sr_col[:, :], id128[0:8, 0:8])))
        kb.step(vector=kb.comp(lambda v: v.tensor_copy(sr_row[:, :], ps_srr[:, :])))
        kb.step(tensor=kb.comp(lambda t: t.matmul(ps_r[:, :], ones_r[:, :], sr_row[:, :],
                                                  start=True, stop=True)))
        kb.step(vector=kb.comp(lambda v: v.tensor_copy(srbc[:, :], ps_r[:, :])))

        for i in range(NT):
            kb.step(sync=kb.dma("sync", sb3(x_sb[:, :], 128), xsrc(i * 128, 128)))
            kb.step(scalar=kb.comp(lambda s: s.activation(sq_sb[:, :], x_sb[:, :], AF.Square)))

            def mm(t):
                for c in range(DCH):
                    t.matmul(ps_m[:, :], x_sb[:, c * 128:(c + 1) * 128], ones_c[:, :],
                             start=(c == 0), stop=(c == DCH - 1))
                for c in range(DCH):
                    t.matmul(ps_q[:, :], sq_sb[:, c * 128:(c + 1) * 128], ones_c[:, :],
                             start=(c == 0), stop=(c == DCH - 1))
                for c in range(DCH):
                    last = t.matmul(ps_r[:, :], x_sb[:, c * 128:(c + 1) * 128],
                                    rwT[:, c * 8:(c + 1) * 8],
                                    start=(c == 0), stop=(c == DCH - 1))
                return last
            kb.step(tensor=kb.comp(mm))

            def collect(v, _i=i):
                v.tensor_copy(msum[:, _i:_i + 1], ps_m[:, :])
                v.tensor_copy(qsum[:, _i:_i + 1], ps_q[:, :])
                return v.tensor_copy(raw[:, _i * 8:(_i + 1) * 8], ps_r[:, :])
            kb.step(vector=kb.comp(collect))

        def vstep(fn):
            kb.step(vector=kb.comp(fn))

        def per_tile(fn):
            def run(v):
                for i in range(NT):
                    last = fn(v, i)
                return last
            vstep(run)

        vstep(lambda v: (v.tensor_scalar(m_all[:, :], msum[:, :], 1.0 / D, None, op0=ALU.mult),
                         v.tensor_scalar(q_all[:, :], qsum[:, :], 1.0 / D, None, op0=ALU.mult))[-1])
        vstep(lambda v: v.tensor_tensor(mm2[:, :], m_all[:, :], m_all[:, :], op=ALU.mult))
        vstep(lambda v: v.tensor_tensor(nvar[:, :], mm2[:, :], q_all[:, :], op=ALU.subtract))
        kb.step(scalar=kb.comp(lambda s: s.activation(a_all[:, :], nvar[:, :], AF.Ln,
                                                      scale=-1.0, bias=kb.c(1e-5))))
        kb.step(scalar=kb.comp(lambda s: s.activation(a_all[:, :], a_all[:, :], AF.Exp, scale=-0.5)))
        per_tile(lambda v, i: v.tensor_scalar(tmp[:, i * 8:(i + 1) * 8], srbc[:, :],
                                              m_all[:, i:i + 1], None, op0=ALU.mult))
        vstep(lambda v: v.tensor_tensor(lall[:, :], raw[:, :], tmp[:, :], op=ALU.subtract))
        per_tile(lambda v, i: v.tensor_scalar(lall[:, i * 8:(i + 1) * 8], lall[:, i * 8:(i + 1) * 8],
                                              a_all[:, i:i + 1], None, op0=ALU.mult))
        per_tile(lambda v, i: v.tensor_reduce(mx_all[:, i:i + 1], lall[:, i * 8:(i + 1) * 8],
                                              axis=AXX, op=ALU.max))
        per_tile(lambda v, i: v.tensor_scalar(lall[:, i * 8:(i + 1) * 8], lall[:, i * 8:(i + 1) * 8],
                                              mx_all[:, i:i + 1], None, op0=ALU.subtract))
        kb.step(scalar=kb.comp(lambda s: s.activation(lall[:, :], lall[:, :], AF.Exp)))
        per_tile(lambda v, i: v.tensor_reduce(sm_all[:, i:i + 1], lall[:, i * 8:(i + 1) * 8],
                                              axis=AXX, op=ALU.add))
        vstep(lambda v: v.reciprocal(rc_all[:, :], sm_all[:, :]))
        per_tile(lambda v, i: v.tensor_scalar(lall[:, i * 8:(i + 1) * 8], lall[:, i * 8:(i + 1) * 8],
                                              rc_all[:, i:i + 1], None, op0=ALU.mult))
        per_tile(lambda v, i: v.tensor_reduce(pm_all[:, i:i + 1], lall[:, i * 8:(i + 1) * 8],
                                              axis=AXX, op=ALU.max))
        per_tile(lambda v, i: v.tensor_scalar(tmp[:, i * 8:(i + 1) * 8], lall[:, i * 8:(i + 1) * 8],
                                              pm_all[:, i:i + 1], None, op0=ALU.is_equal))
        vstep(lambda v: v.tensor_scalar(tmp[:, :], tmp[:, :], BIGI, None, op0=ALU.mult))
        per_tile(lambda v, i: v.tensor_tensor(tmp[:, i * 8:(i + 1) * 8], iota_big[:, :],
                                              tmp[:, i * 8:(i + 1) * 8], op=ALU.subtract))
        per_tile(lambda v, i: v.tensor_reduce(ecol[:, i:i + 1], tmp[:, i * 8:(i + 1) * 8],
                                              axis=AXX, op=ALU.min))
        vstep(lambda v: v.tensor_scalar(ptil[:, :], ecol[:, :], 2.0, None, op0=ALU.mult))
        vstep(lambda v: v.tensor_scalar(band[:, :], ptil[:, :], 2.0, None, op0=ALU.add))
        vstep(lambda v: v.tensor_tensor(ptil[:, :], ptil[:, :], pm_all[:, :], op=ALU.add))

        kb.step(sync=kb.dma("sync", pt_h.ap(), ptil[:, :]))
        kb.step(sync=kb.dma("sync", prow[:, :],
                            pt_h.ap().rearrange("p i -> (p i)").rearrange("(a b) -> a b", a=1)))
        for j in range(8):
            kb.step(tensor=kb.comp(lambda t, _j=j: t.matmul(
                ps_bc[:, :], ones_r[:, :], prow[:, _j * 512:(_j + 1) * 512],
                start=True, stop=True)))
            kb.step(vector=kb.comp(lambda v, _j=j: v.tensor_copy(
                pbc[:, _j * 512:(_j + 1) * 512], ps_bc[:, :])))

        for i in range(NT):
            kb.step(vector=kb.comp(lambda v, _i=i: v.tensor_scalar(
                junk[:, :], pbc[:, :], ptil[:, _i:_i + 1], 0.0,
                op0=ALU.is_gt, op1=ALU.add, accum_out=r1[:, _i:_i + 1])))
            kb.step(vector=kb.comp(lambda v, _i=i: v.tensor_scalar(
                junk[:, :], pbc[:, :], band[:, _i:_i + 1], 0.0,
                op0=ALU.is_ge, op1=ALU.add, accum_out=r2[:, _i:_i + 1])))

        vstep(lambda v: v.tensor_tensor(r1[:, :], r1[:, :], r2[:, :], op=ALU.subtract))
        vstep(lambda v: v.tensor_scalar(r1[:, :], r1[:, :], float(CAP_E), None, op0=ALU.is_lt))
        vstep(lambda v: v.tensor_tensor(fe_sb[:, :], ecol[:, :], r1[:, :], op=ALU.mult))
        kb.step(tensor=kb.comp(lambda t: t.transpose(ps_fe[:, :], fe_sb[:, :], id128[:, :])))
        kb.step(vector=kb.comp(lambda v: v.tensor_copy(fe_t[:, :], ps_fe[:, :])))
        kb.step(sync=kb.dma("sync", fe_h.ap().rearrange("(i p) -> i p", p=128), fe_t[:, :]))

    with kb.scope():
        _d2()

    S0C = 1536
    SEG = [0, S0C] + [S0C + CAP_E * e for e in range(1, 8)]
    NSLOT = SEG[8]

    def _d3():
        NT = TOWN // 128
        ln_sort = scr["d_lnsort"]
        idb = kb.sbuf("d3idb", [128, 128], BF16)
        idf = kb.sbuf("d3idf", [128, 128], F32)
        iota8 = kb.sbuf("d3io8", [128, 8], F32)
        utris = kb.sbuf("d3uts", [128, 128], F32)
        eh32 = kb.sbuf("d3eh", [32, 32 * 128], F32)
        seg8 = kb.sbuf("d3sg", [128, 8], F32)
        cap8 = kb.sbuf("d3cp", [128, 8], F32)
        tokid = kb.sbuf("d3tk", [128, NT], I32)
        ones_c = kb.sbuf("d3oc", [128, 1], F32)
        fe2 = kb.sbuf("d3fe", [128, NT], F32)
        fe_rows = kb.sbuf("d3fer", [32, 128], F32)
        lnst = kb.sbuf("d3lns", [128, DCH * 128], BF16)
        lt2 = kb.sbuf("d3lt2", [128, D], BF16)
        oh_sb = kb.sbuf("d3oh", [128, 8], F32)
        t8 = kb.sbuf("d3t8", [128, 8], F32)
        rw = kb.sbuf("d3rw", [128, NT], F32)
        capr = kb.sbuf("d3cr", [128, 1], F32)
        cntT = kb.sbuf("d3cT", [8, 32], F32)
        cnt2 = kb.sbuf("d3c2", [32, 8], F32)
        adjT = kb.sbuf("d3aj", [32, 8], F32)
        zln = kb.sbuf("d3z", [128, D], BF16)
        bigi = kb.sbuf("d3bg", [128, 64], I32)
        ps_tx = kb.psum("d3ptx", [128, 512], BF16)
        ps_sm = kb.psum("d3psm", [128, 128])
        ps_c = ps_sm[0:8, 0:32]
        ps_p = ps_sm[:, 32:40]
        ps_c2 = ps_sm[0:32, 40:48]
        ps_bc = ps_sm[:, 48:56]
        ps_fe2 = ps_sm[:, 56:88]
        kb.step(sync=kb.dmas([
            (idf[:, :], w["ident_f32"]),
            (utris[:, :], w["utris128"]),
            (eh32[:, :], w["eh32"]),
            (seg8[:, :], w["seg8c"]),
            (cap8[:, :], w["cap8c"]),
            (tokid[:, :], w["tokid32i"]),
            (iota8[:, :], w["iota8c"]),
        ]), gpsimd=kb.dma("gpsimd", fe_rows[:, :],
                          fe_h.ap().rearrange("(i p) -> i p", p=128)),
           vector=kb.comp(lambda v: (v.memset(ones_c[:], 1.0), v.memset(zln[:, :], 0.0),
                                     v.memset(bigi[:, :], float(1 << 29)))[-1]))
        kb.step(tensor=kb.comp(lambda t: t.transpose(ps_fe2[:, :], fe_rows[:, :],
                                                     idf[0:32, 0:32])))
        kb.step(vector=kb.comp(lambda v: v.tensor_copy(fe2[:, :], ps_fe2[:, :])))
        kb.step(vector=kb.comp(lambda v: v.tensor_copy(idb[:, :], idf[:, :])))
        for i in range(NSLOT // 128):
            kb.step(gpsimd=kb.dma("gpsimd", ln_sort[i * 128:(i + 1) * 128, :], zln[:, :]))
        for i in range(NT):
            kb.step(vector=kb.comp(lambda v, _i=i: v.tensor_scalar(
                oh_sb[:, :], iota8[:, :], fe2[:, _i:_i + 1], None, op0=ALU.is_equal)))

            def mmc(t, _i=i):
                t.matmul(ps_c[:, _i:_i + 1], oh_sb[:, :], ones_c[:, :], start=True, stop=True)
                return t.matmul(ps_p[:, :], utris[:, :], oh_sb[:, :], start=True, stop=True)
            kb.step(tensor=kb.comp(mmc))
            kb.step(vector=kb.comp(lambda v: v.tensor_tensor(t8[:, :], oh_sb[:, :],
                                                             ps_p[:, :], op=ALU.mult)))
            kb.step(vector=kb.comp(lambda v, _i=i: v.tensor_reduce(
                rw[:, _i:_i + 1], t8[:, :], axis=AXX, op=ALU.add)))
        kb.step(vector=kb.comp(lambda v: v.tensor_copy(cntT[:, :], ps_c[:, :])))
        kb.step(tensor=kb.comp(lambda t: t.transpose(ps_c2[:, :], cntT[:, :], idf[0:8, 0:8])))
        kb.step(vector=kb.comp(lambda v: v.tensor_copy(cnt2[:, :], ps_c2[:, :])))
        kb.step(tensor=kb.comp(lambda t: t.matmul(ps_c2[:, :], utris[0:32, 0:32], cnt2[:, :],
                                                  start=True, stop=True)))
        kb.step(vector=kb.comp(lambda v: v.tensor_tensor(adjT[:, :], ps_c2[:, :],
                                                         seg8[0:32, :], op=ALU.add)))
        for i in range(NT):
            kb.step(tensor=kb.comp(lambda t, _i=i: t.matmul(
                ps_bc[:, :], eh32[:, _i * 128:(_i + 1) * 128], adjT[:, :],
                start=True, stop=True)))
            kb.step(vector=kb.comp(lambda v, _i=i: v.tensor_scalar(
                oh_sb[:, :], iota8[:, :], fe2[:, _i:_i + 1], None, op0=ALU.is_equal)))
            kb.step(vector=kb.comp(lambda v: v.tensor_tensor(t8[:, :], oh_sb[:, :],
                                                             ps_bc[:, :], op=ALU.mult)))
            kb.step(vector=kb.comp(lambda v, _i=i: v.tensor_reduce(
                slot[:, _i:_i + 1], t8[:, :], axis=AXX, op=ALU.add)))
            kb.step(vector=kb.comp(lambda v: v.tensor_tensor(t8[:, :], oh_sb[:, :],
                                                             cap8[:, :], op=ALU.mult)))
            kb.step(vector=kb.comp(lambda v: v.tensor_reduce(capr[:, :], t8[:, :],
                                                             axis=AXX, op=ALU.add)))
            kb.step(vector=kb.comp(lambda v, _i=i: v.tensor_tensor(
                slot[:, _i:_i + 1], slot[:, _i:_i + 1], rw[:, _i:_i + 1], op=ALU.add)))
            kb.step(vector=kb.comp(lambda v, _i=i: v.tensor_scalar(
                capr[:, :], rw[:, _i:_i + 1], capr[:, 0:1], None, op0=ALU.is_ge)))
            kb.step(vector=kb.comp(lambda v: v.tensor_scalar(
                capr[:, :], capr[:, :], float(1 << 29), None, op0=ALU.mult)))
            kb.step(vector=kb.comp(lambda v, _i=i: v.tensor_tensor(
                slot[:, _i:_i + 1], slot[:, _i:_i + 1], capr[:, :], op=ALU.add)))
        kb.step(vector=kb.comp(lambda v: v.tensor_copy(sloti[:, :], slot[:, :])))
        for i in range(NT):
            kb.step(sync=kb.dma("sync", sb3(lnst[:, :], 128), fm(lnT, i * 128, 128)))
            for g in range(2):
                def trg(t, _g=g):
                    for c in range(4):
                        last = t.transpose(ps_tx[:, c * 128:(c + 1) * 128],
                                           lnst[:, (_g * 4 + c) * 128:(_g * 4 + c + 1) * 128],
                                           idb[:, :])
                    return last

                def cpg(v, _g=g):
                    for c in range(4):
                        last = v.tensor_copy(lt2[:, (_g * 4 + c) * 128:(_g * 4 + c + 1) * 128],
                                             ps_tx[:, c * 128:(c + 1) * 128])
                    return last
                kb.step(tensor=kb.comp(trg))
                kb.step(vector=kb.comp(cpg))

            def sc1(e, semh, _i=i):
                e.indirect_dma_start(
                    out=ln_sort, out_offset=IOffs(ap=sloti[:, _i:_i + 1], axis=0),
                    in_=lt2[:, :], in_offset=None,
                    bounds_check=NSLOT - 1, oob_is_err=False).then_inc(semh, 16)
            kb.step(gpsimd=("dma", 16, sc1))

        w1 = kb.sbuf("d3w1", [128, DCH * 4096], BF16)
        w2 = kb.sbuf("d3w2", [128, 32 * D], BF16)
        b1 = kb.sbuf("d3b1", [128, 32], F32)
        b2r = kb.sbuf("d3b2r", [1, D], F32)
        b2bc = kb.sbuf("d3b2b", [128, D], F32)
        lnsp = kb.sbuf("d3lsp", [128, DCH * 512], BF16)
        h_sb = kb.sbuf("d3h", [128, 32 * 512], BF16)
        ytk2 = [kb.sbuf("d3y%d" % k, [128, D], BF16) for k in range(2)]
        ones_r = kb.sbuf("d3or", [1, 128], F32)
        ps_hp = [kb.psum("d3ph%d" % k, [128, 512]) for k in range(2)]
        ps_yp = [kb.psum("d3py%d" % k, [128, 512]) for k in range(2)]
        ps_b2 = kb.psum("d3pb2", [128, 512])
        y_sort = scr["d_ysort"]
        kb.step(vector=kb.comp(lambda v: v.memset(ones_r[:], 1.0)))
        for e in range(8):
            kb.step(gpsimd=kb.dmas([
                (sb3(w1[:, :], 4096), w["L1f_W1"][e].rearrange("(c p) j -> p c j", p=128)),
                (w2[:, :].rearrange("p (c f) -> p c f", f=D),
                 w["L1f_W2"][e].rearrange("(c p) f -> p c f", p=128)),
            ]), sync=kb.dmas([
                (b1[:, :], w["L1f_b1"][e]),
                (b2r[:, :], w["L1f_b2r"][e].rearrange("(a b) -> a b", a=1)),
            ]))
            for half in range(2):
                kb.step(tensor=kb.comp(lambda t, _h=half: t.matmul(
                    ps_b2[:, :], ones_r[:, :], b2r[:, _h * 512:(_h + 1) * 512],
                    start=True, stop=True)))
                kb.step(vector=kb.comp(lambda v, _h=half: v.tensor_copy(
                    b2bc[:, _h * 512:(_h + 1) * 512], ps_b2[:, :])))
            segl = SEG[e + 1] - SEG[e]
            spans = [(SEG[e] + o, min(512, segl - o)) for o in range(0, segl, 512)]
            for (s0, sl) in spans:
                ntile = sl // 128
                for j in range(ntile):
                    kb.step(sync=kb.dma("sync", lt2[:, :],
                                        ln_sort[s0 + j * 128:s0 + (j + 1) * 128, :]))
                    for g in range(2):
                        def trg2(t, _g=g):
                            for c in range(4):
                                last = t.transpose(ps_tx[:, c * 128:(c + 1) * 128],
                                                   lt2[:, (_g * 4 + c) * 128:(_g * 4 + c + 1) * 128],
                                                   idb[:, :])
                            return last

                        def cpg2(v, _g=g, _j=j, _sl=sl):
                            for c in range(4):
                                cc = _g * 4 + c
                                last = v.tensor_copy(
                                    lnsp[:, cc * _sl + _j * 128: cc * _sl + (_j + 1) * 128],
                                    ps_tx[:, c * 128:(c + 1) * 128])
                            return last
                        kb.step(tensor=kb.comp(trg2))
                        kb.step(vector=kb.comp(cpg2))
                def mmh(t, _j, _sl=sl):
                    for c in range(DCH):
                        last = t.matmul(ps_hp[_j % 2][:, 0:_sl],
                                        w1[:, c * 4096 + _j * 128: c * 4096 + _j * 128 + 128],
                                        lnsp[:, c * _sl:(c + 1) * _sl],
                                        start=(c == 0), stop=(c == DCH - 1))
                    return last

                def evh(s, _j, _sl=sl):
                    return s.activation(h_sb[:, _j * _sl:(_j + 1) * _sl], ps_hp[_j % 2][:, 0:_sl],
                                        AF.Relu, bias=b1[:, _j:_j + 1])
                for j in range(33):
                    parts = {}
                    if j < 32:
                        parts["tensor"] = kb.comp(lambda t, _j=j, _f=mmh: _f(t, _j))
                    if j > 0:
                        parts["scalar"] = kb.comp(lambda s, _j=j - 1, _f=evh: _f(s, _j))
                    kb.step(**parts)
                def mmy(t, _q, _sl=sl):
                    _j, _h = _q // 2, _q % 2
                    for jc in range(32):
                        last = t.matmul(ps_yp[_q % 2][:, :],
                                        h_sb[:, jc * _sl + _j * 128: jc * _sl + (_j + 1) * 128],
                                        w2[:, jc * D + _h * 512: jc * D + (_h + 1) * 512],
                                        start=(jc == 0), stop=(jc == 31))
                    return last

                def evy(v, _q):
                    _j, _h = _q // 2, _q % 2
                    return v.tensor_tensor(ytk2[_j % 2][:, _h * 512:(_h + 1) * 512],
                                           ps_yp[_q % 2][:, :],
                                           b2bc[:, _h * 512:(_h + 1) * 512], op=ALU.add)
                nq = 2 * ntile
                for q in range(nq + 2):
                    parts = {}
                    if q < nq:
                        parts["tensor"] = kb.comp(lambda t, _q=q, _f=mmy: _f(t, _q))
                    if 0 < q <= nq:
                        parts["vector"] = kb.comp(lambda v, _q=q - 1, _f=evy: _f(v, _q))
                    if q > 1 and (q - 2) % 2 == 1:
                        _j = (q - 2) // 2
                        parts["gpsimd"] = kb.dma("gpsimd",
                                                 y_sort[s0 + _j * 128:s0 + (_j + 1) * 128, :],
                                                 ytk2[_j % 2][:, :])
                    kb.step(**parts)

    srow_h = scr["d_srow"]

    def _d3u():
        y_sort = scr["d_ysort"]
        idf2 = kb.sbuf("duid", [128, 128], F32)
        slot_row = kb.sbuf("duslr", [1, TOWN], F32)
        slot_bc = kb.sbuf("duslb", [128, TOWN], F32)
        slotc47 = kb.sbuf("dus47", [128, 47], F32)
        bnc = kb.sbuf("dubnc", [32, 128], F32)
        oh_all = kb.sbuf("duoha", [128, 47 * 512], BF16)
        ysb = kb.sbuf("duys", [128, 24 * D], BF16)
        x3t2 = kb.sbuf("dux3", [128, DCH * 512], F32)
        o2 = kb.sbuf("duo2", [128, DCH * 512], F32)
        ones_r = kb.sbuf("duor", [1, 128], F32)
        ps_bn2 = kb.psum("dupbn", [32, 128])
        ps_u = [kb.psum("dupu%d" % fc, [128, 512]) for fc in range(7)]
        kb.step(sync=kb.dmas([(slotc47[:, :], w["slotcol47"]),
                              (idf2[:, :], w["ident_f32"])]),
                vector=kb.comp(lambda v: v.memset(ones_r[:], 1.0)))
        kb.step(tensor=kb.comp(lambda t: t.transpose(ps_bn2[:, :], slot[:, :], idf2[:, :])))
        kb.step(vector=kb.comp(lambda v: v.tensor_copy(bnc[:, :], ps_bn2[:, :])))
        kb.step(sync=kb.dma("sync", srow_h.ap(), bnc[:, :]))
        kb.step(sync=kb.dma("sync", slot_row[:, :],
                            srow_h.ap().rearrange("i p -> (i p)").rearrange("(a b) -> a b", a=1)))
        for jj in range(TOWN // 512):
            kb.step(tensor=kb.comp(lambda t, _j=jj: t.matmul(
                ps_u[0][:, :], ones_r[:, :], slot_row[:, _j * 512:(_j + 1) * 512],
                start=True, stop=True)))
            kb.step(vector=kb.comp(lambda v, _j=jj: v.tensor_copy(
                slot_bc[:, _j * 512:(_j + 1) * 512], ps_u[0][:, :])))
        SBLK = [(0, 24), (24, 23)]
        for sp in range(TOWN // 512):
            t0 = sp * 512

            def bld(v, _sp=sp):
                for sc in range(47):
                    last = v.tensor_scalar(oh_all[:, sc * 512:(sc + 1) * 512],
                                           slot_bc[:, _sp * 512:(_sp + 1) * 512],
                                           slotc47[:, sc:sc + 1], None, op0=ALU.is_equal)
                return last
            kb.step(vector=kb.comp(bld),
                    gpsimd=kb.dma("gpsimd", sb3(x3t2[:, :], 512), fm(x3own, t0, 512)))
            for (c0, nch) in SBLK:
                kb.step(sync=kb.dma("sync",
                                    ysb[:, :].rearrange("p (c f) -> p c f", f=D)[:, 0:nch, :],
                                    y_sort[c0 * 128:(c0 + nch) * 128, :]
                                    .rearrange("(c p) f -> p c f", p=128)))
                for fc in range(7):
                    def mmu2(t, _fc=fc, _c0=c0, _n=nch):
                        for sc in range(_n):
                            last = t.matmul(ps_u[_fc][:, :],
                                            ysb[:, sc * D + _fc * 128:sc * D + (_fc + 1) * 128],
                                            oh_all[:, (_c0 + sc) * 512:(_c0 + sc + 1) * 512],
                                            start=(_c0 == 0 and sc == 0),
                                            stop=(_c0 == 24 and sc == _n - 1))
                        return last
                    kb.step(tensor=kb.comp(mmu2))
            def ev7(v):
                for fc in range(7):
                    last = v.tensor_tensor(o2[:, fc * 512:(fc + 1) * 512], ps_u[fc][:, :],
                                           x3t2[:, fc * 512:(fc + 1) * 512], op=ALU.add)
                return last
            kb.step(vector=kb.comp(ev7))
            for (c0, nch) in SBLK:
                kb.step(sync=kb.dma("sync",
                                    ysb[:, :].rearrange("p (c f) -> p c f", f=D)[:, 0:nch, :],
                                    y_sort[c0 * 128:(c0 + nch) * 128, :]
                                    .rearrange("(c p) f -> p c f", p=128)))
                def mmu3(t, _c0=c0, _n=nch):
                    for sc in range(_n):
                        last = t.matmul(ps_u[0][:, :],
                                        ysb[:, sc * D + 7 * 128:sc * D + 8 * 128],
                                        oh_all[:, (_c0 + sc) * 512:(_c0 + sc + 1) * 512],
                                        start=(_c0 == 0 and sc == 0),
                                        stop=(_c0 == 24 and sc == _n - 1))
                    return last
                kb.step(tensor=kb.comp(mmu3))
            kb.step(vector=kb.comp(lambda v: v.tensor_tensor(
                o2[:, 7 * 512:8 * 512], ps_u[0][:, :], x3t2[:, 7 * 512:8 * 512], op=ALU.add)))
            kb.step(sync=kb.dma("sync", fm(out_tok, t0, 512), sb3(o2[:, :], 512)))
    with kb.scope():
        _d3()
    with kb.scope():
        _d3u()



def build_program(stages=("A", "D"), x3_is_input=False, debug=False):
    nc = bass.Bass("TRN2", target_bir_lowering=False, debug=False)
    w = {
        "L0a_wqkvT": nc.dram_tensor("L0a_wqkvT", [D, 3072], F32, kind="ExternalInput").ap(),
        "L0a_woutT": nc.dram_tensor("L0a_woutT", [D, D], F32, kind="ExternalInput").ap(),
        "L0a_bout": nc.dram_tensor("L0a_bout", [128, DCH], F32, kind="ExternalInput").ap(),
        "L1f_rwT": nc.dram_tensor("L1f_rwT", [D, 8], F32, kind="ExternalInput").ap(),
        "L1f_W1": nc.dram_tensor("L1f_W1", [8, D, 4096], F32, kind="ExternalInput").ap(),
        "L1f_W2": nc.dram_tensor("L1f_W2", [8, 4096, D], F32, kind="ExternalInput").ap(),
        "L1f_b1": nc.dram_tensor("L1f_b1", [8, 128, 32], F32, kind="ExternalInput").ap(),
        "L1f_b2": nc.dram_tensor("L1f_b2", [8, 128, 8], F32, kind="ExternalInput").ap(),
        "ident_f32": nc.dram_tensor("ident_f32", [128, 128], F32, kind="ExternalInput").ap(),
        "iota8_big": nc.dram_tensor("iota8_big", [128, 8], F32, kind="ExternalInput").ap(),
        "own_hf": nc.dram_tensor("own_hf", [1, 1], I32, kind="ExternalInput").ap(),
    }
    w.update({
        "L0f_wrc": nc.dram_tensor("L0f_wrc", [128, DCH], F32, kind="ExternalInput").ap(),
        "L0f_wqkvT": nc.dram_tensor("L0f_wqkvT", [D, 3072], F32, kind="ExternalInput").ap(),
        "L0f_woutT": nc.dram_tensor("L0f_woutT", [D, D], F32, kind="ExternalInput").ap(),
        "L0f_g1c": nc.dram_tensor("L0f_g1c", [128, DCH], F32, kind="ExternalInput").ap(),
        "L0f_g2c": nc.dram_tensor("L0f_g2c", [128, DCH], F32, kind="ExternalInput").ap(),
        "L0f_w1T": nc.dram_tensor("L0f_w1T", [D, DFF0], F32, kind="ExternalInput").ap(),
        "L0f_w2T": nc.dram_tensor("L0f_w2T", [D, DFF0], F32, kind="ExternalInput").ap(),
        "L0f_w3T": nc.dram_tensor("L0f_w3T", [DFF0, D], F32, kind="ExternalInput").ap(),
        "iota512": nc.dram_tensor("iota512", [128, NSEL], F32, kind="ExternalInput").ap(),
        "mask512": nc.dram_tensor("mask512", [NSEL, NSEL], F32, kind="ExternalInput").ap(),
        "slotcol4": nc.dram_tensor("slotcol4", [128, 4], F32, kind="ExternalInput").ap(),
        "L1a_wqkT": nc.dram_tensor("L1a_wqkT", [D, D], F32, kind="ExternalInput").ap(),
        "L1a_wvT": nc.dram_tensor("L1a_wvT", [D, D], F32, kind="ExternalInput").ap(),
        "L1a_wogT": nc.dram_tensor("L1a_wogT", [D, D], F32, kind="ExternalInput").ap(),
        "L1a_wifT": nc.dram_tensor("L1a_wifT", [D, 16], F32, kind="ExternalInput").ap(),
        "L1a_bif": nc.dram_tensor("L1a_bif", [128, T // 128 * 16], F32, kind="ExternalInput").ap(),
        "L1a_lnwc": nc.dram_tensor("L1a_lnwc", [128, DCH], F32, kind="ExternalInput").ap(),
        "L1a_lnbc": nc.dram_tensor("L1a_lnbc", [128, DCH], F32, kind="ExternalInput").ap(),
        "L1a_woutT": nc.dram_tensor("L1a_woutT", [D, D], F32, kind="ExternalInput").ap(),
        "utri128": nc.dram_tensor("utri128", [128, 128], F32, kind="ExternalInput").ap(),
        "eh8": nc.dram_tensor("eh8", [8, 8 * 128], F32, kind="ExternalInput").ap(),
        "eh32": nc.dram_tensor("eh32", [32, 32 * 128], F32, kind="ExternalInput").ap(),
        "utris128": nc.dram_tensor("utris128", [128, 128], F32, kind="ExternalInput").ap(),
        "seg8c": nc.dram_tensor("seg8c", [128, 8], F32, kind="ExternalInput").ap(),
        "cap8c": nc.dram_tensor("cap8c", [128, 8], F32, kind="ExternalInput").ap(),
        "tokid32i": nc.dram_tensor("tokid32i", [128, 32], I32, kind="ExternalInput").ap(),
        "slotcol47": nc.dram_tensor("slotcol47", [128, 47], F32, kind="ExternalInput").ap(),
        "iota8c": nc.dram_tensor("iota8c", [128, 8], F32, kind="ExternalInput").ap(),
        "L1f_b2r": nc.dram_tensor("L1f_b2r", [8, D], F32, kind="ExternalInput").ap(),
        "maskU8": nc.dram_tensor("maskU8", [128, 8 * 128], F32, kind="ExternalInput").ap(),
    })
    xT = nc.dram_tensor("xT", [D, T], F32, kind="ExternalInput").ap()
    out_tok = nc.dram_tensor("outT", [D, TOWN], F32, kind="ExternalOutput").ap()
    x1T = nc.dram_tensor("x1T", [D, T], F32,
                         kind="ExternalOutput" if "A" in stages else "Internal").ap()
    x2T = nc.dram_tensor("x2T", [D, T], F32,
                         kind="ExternalOutput" if "B" in stages else "Internal").ap()
    x3T = nc.dram_tensor("x3T", [D, T], F32,
                         kind="ExternalOutput" if "C" in stages else "Internal").ap()
    lnT = nc.dram_tensor("lnT", [D, T], BF16).ap()
    scratch = {
        "k_tok": nc.dram_tensor("k_tok", [T, D], BF16).ap(),
        "v_tok": nc.dram_tensor("v_tok", [T, D], BF16).ap(),
        "ksum": nc.dram_tensor("ksum", [D], F32),
        "d_ln": nc.dram_tensor("d_ln", [D, TOWN], BF16).ap(),
        "d_acc": nc.dram_tensor("d_acc", [D, TOWN], F32).ap(),
        "d_fe": nc.dram_tensor("d_fe", [TOWN], F32,
                               kind="ExternalOutput" if debug else "Internal"),
        "d_pt": nc.dram_tensor("d_pt", [128, TOWN // 128], F32),
        "d_x3own": nc.dram_tensor("d_x3own", [D, TOWN], F32).ap(),
        "d_lnsort": nc.dram_tensor("d_lnsort", [6016, D], BF16).ap(),
        "d_ysort": nc.dram_tensor("d_ysort", [6016, D], BF16).ap(),
        "d_srow": nc.dram_tensor("d_srow", [32, 128], F32),
        "b_lntok": nc.dram_tensor("b_lntok", [T, D], BF16).ap(),
        "b_row": nc.dram_tensor("b_row", [T // 128, 128], F32),
        "c_qkT": nc.dram_tensor("c_qkT", [D, T], BF16).ap(),
        "c_sogT": nc.dram_tensor("c_sogT", [D, T], BF16).ap(),
        "c_ktok": nc.dram_tensor("c_ktok", [T, 512], BF16).ap(),
        "c_vtok": nc.dram_tensor("c_vtok", [T, D], BF16).ap(),
        "c_if": nc.dram_tensor("c_if", [T, 16], F32).ap(),
        "c_htok": nc.dram_tensor("c_htok", [T, D], BF16).ap(),
    }
    if debug:
        scratch["dbg"] = nc.dram_tensor("dbg", [128, 256], F32, kind="ExternalOutput").ap()
    kb = KB(nc)
    for v in (1e-5, 1e-6):
        kb.c(v)
    kb.init_consts_step()
    if "A" in stages:
        with kb.scope():
            emit_layernorm(kb, lambda t0, tt: fm(xT, t0, tt), lnT, "ln0")
        with kb.scope():
            emit_l0_attn(kb, xT, lnT, x1T, w, scratch)
    if "B" in stages:
        xb = xT if "A" not in stages else x1T
        with kb.scope():
            emit_layernorm(kb, lambda t0, tt: fm(xb, t0, tt), lnT, "ln1")
        with kb.scope():
            emit_l0_ffn(kb, xb, lnT, x2T, w, scratch)
    if "C" in stages:
        if "B" in stages:
            xc = x2T
        elif "A" in stages:
            xc = x1T
        else:
            xc = xT
        with kb.scope():
            emit_layernorm(kb, lambda t0, tt: fm(xc, t0, tt), lnT, "ln2")
        with kb.scope():
            emit_l1_attn(kb, xc, lnT, x3T, w, scratch)
    if "D" in stages:
        if x3_is_input:
            x3 = xT
        elif "C" in stages:
            x3 = x3T
        elif "B" in stages:
            x3 = x2T
        else:
            x3 = x1T
        with kb.scope():
            emit_l1_ffn(kb, x3, out_tok, w, scratch)
    kb.final_wait()
    kb.build()
    return nc


def prep_weights(inputs):
    iota8_big = (np.arange(8, dtype=np.float32) + BIGI)[None, :].repeat(128, 0)
    return {
        "L0a_wqkvT": np.ascontiguousarray(inputs["L0_attn_W_qkv"].T),
        "L0a_woutT": np.ascontiguousarray(inputs["L0_attn_W_out"].T),
        "L0a_bout": np.ascontiguousarray(inputs["L0_attn_b_out"].reshape(DCH, 128).T),
        "L1f_rwT": np.ascontiguousarray(inputs["L1_ffn_router_w"].T),
        "L1f_W1": np.ascontiguousarray(inputs["L1_ffn_W1"]),
        "L1f_W2": np.ascontiguousarray(inputs["L1_ffn_W2"]),
        "L1f_b1": np.ascontiguousarray(inputs["L1_ffn_b1"].reshape(8, 32, 128).transpose(0, 2, 1)),
        "L1f_b2": np.ascontiguousarray(inputs["L1_ffn_b2"].reshape(8, 8, 128).transpose(0, 2, 1)),
        "ident_f32": np.eye(128, dtype=np.float32),
        "iota8_big": np.ascontiguousarray(iota8_big),
        "L0f_wrc": np.ascontiguousarray(inputs["L0_ffn_w_router"].reshape(DCH, 128).T),
        "L0f_wqkvT": np.ascontiguousarray(inputs["L0_ffn_W_qkv"].T),
        "L0f_woutT": np.ascontiguousarray(inputs["L0_ffn_W_out"].T),
        "L0f_g1c": np.ascontiguousarray(inputs["L0_ffn_g1"].reshape(DCH, 128).T),
        "L0f_g2c": np.ascontiguousarray(inputs["L0_ffn_g2"].reshape(DCH, 128).T),
        "L0f_w1T": _padT(inputs["L0_ffn_W1"]),
        "L0f_w2T": _padT(inputs["L0_ffn_W2"]),
        "L0f_w3T": _padC(inputs["L0_ffn_W3"]),
        "iota512": np.tile(np.arange(NSEL, dtype=np.float32)[None, :], (128, 1)),
        "mask512": np.where(np.triu(np.ones((NSEL, NSEL), bool), 1), -1e9, 0.0).astype(np.float32),
        "slotcol4": (np.arange(4, dtype=np.float32)[None, :] * 128
                     + np.arange(128, dtype=np.float32)[:, None]),
        "L1a_wqkT": np.ascontiguousarray(np.concatenate(
            [inputs["L1_attn_Wq"].T, inputs["L1_attn_Wk"].T], axis=1)),
        "L1a_wvT": np.ascontiguousarray(inputs["L1_attn_Wv"].T),
        "L1a_wogT": np.ascontiguousarray(inputs["L1_attn_Wog"].T),
        "L1a_wifT": np.ascontiguousarray(np.concatenate(
            [inputs["L1_attn_Wi"].T, inputs["L1_attn_Wf"].T], axis=1)),
        "L1a_bif": np.ascontiguousarray(np.tile(np.concatenate(
            [inputs["L1_attn_bi"], inputs["L1_attn_bf"]]).astype(np.float32),
            (128, T // 128))),
        "L1a_lnwc": np.ascontiguousarray(inputs["L1_attn_ln_w"].reshape(DCH, 128).T),
        "L1a_lnbc": np.ascontiguousarray(inputs["L1_attn_ln_b"].reshape(DCH, 128).T),
        "L1a_woutT": np.ascontiguousarray(inputs["L1_attn_Wout"].T),
        "utri128": np.triu(np.ones((128, 128), np.float32)),
        "eh8": np.ascontiguousarray(np.concatenate(
            [np.where(np.arange(8)[:, None] == h, 1.0, 0.0).astype(np.float32)
             .repeat(128, axis=1).reshape(8, 128) for h in range(8)], axis=1)),
        "eh32": np.ascontiguousarray(np.concatenate(
            [np.where(np.arange(32)[:, None] == i, 1.0, 0.0).astype(np.float32)
             .repeat(128, axis=1).reshape(32, 128) for i in range(32)], axis=1)),
        "utris128": np.triu(np.ones((128, 128), np.float32), 1),
        "seg8c": np.tile(np.array([0, 1536] + [1536 + 640 * e for e in range(1, 7)],
                                  np.float32)[None, :], (128, 1)),
        "cap8c": np.tile(np.array([1536] + [640] * 7, np.float32)[None, :], (128, 1)),
        "tokid32i": (np.arange(128, dtype=np.int32)[:, None]
                     + 128 * np.arange(32, dtype=np.int32)[None, :]),
        "slotcol47": (np.arange(128, dtype=np.float32)[:, None]
                      + 128 * np.arange(47, dtype=np.float32)[None, :]),
        "iota8c": np.tile(np.arange(8, dtype=np.float32)[None, :], (128, 1)),
        "L1f_b2r": np.ascontiguousarray(inputs["L1_ffn_b2"]),
        "maskU8": np.ascontiguousarray(np.tile(
            np.where(np.arange(128)[:, None] <= np.arange(128)[None, :], 0.0, -1e9)
            .astype(np.float32), (1, 8))),
    }


def _padT(wm):
    out = np.zeros((D, DFF0), np.float32)
    out[:, :wm.shape[0]] = wm.T
    return out


def _padC(wm):
    out = np.zeros((DFF0, D), np.float32)
    out[:wm.shape[1], :] = wm.T
    return out


def kernel(**inputs):
    x = inputs["L0_attn_x"]
    nc = build_program(stages=("A", "B", "C", "D"))
    wmaps = prep_weights(inputs)
    in_maps = []
    for c in range(8):
        b, hf = c // 2, c % 2
        m = dict(wmaps)
        m["xT"] = np.ascontiguousarray(x[b].T)
        m["own_hf"] = np.array([[hf]], np.int32)
        in_maps.append(m)
    res = run_bass_kernel_spmd(nc, in_maps, core_ids=list(range(8)))
    out = np.empty((4, T, D), np.float32)
    for c in range(8):
        b, hf = c // 2, c % 2
        sl = slice(hf * TOWN, (hf + 1) * TOWN)
        out[b, sl, :] = res.results[c]["outT"].T
    return out


# version 11
# speedup vs baseline: 1.0864x; 1.0864x over previous
import sys
import numpy as np
from contextlib import ExitStack, contextmanager

sys.path.insert(0, "/opt/trn_rl_repo")
import concourse.bass as bass
import concourse.mybir as mybir
from concourse.bass_utils import run_bass_kernel_spmd

F32 = mybir.dt.float32
BF16 = mybir.dt.bfloat16
I32 = mybir.dt.int32
AF = mybir.ActivationFunctionType
ALU = mybir.AluOpType
AXX = mybir.AxisListType.X
IOffs = bass.IndirectOffsetOnAxis

ENGINES = ("sync", "scalar", "vector", "tensor", "gpsimd")

D = 1024
DCH = D // 128
T = 8192
B = 4



class KB:

    def __init__(self, nc):
        self.nc = nc
        self.cmds = {e: [] for e in ENGINES}
        self.es = ExitStack()
        self.scopes = [self.es]
        self.bars = {k: self.es.enter_context(nc.semaphore("bar_" + k)) for k in "cdg"}
        self.cnt = {k: 0 for k in "cdg"}
        self.last = {e: {k: 0 for k in "cdg"} for e in ENGINES}
        self._consts = {}
        self._const_sb = self.es.enter_context(nc.sbuf_tensor("kconsts", [128, 16], F32))

    def c(self, value, p=128):
        if value not in self._consts:
            assert len(self._consts) < 16
            self._consts[value] = len(self._consts)
        i = self._consts[value]
        return self._const_sb[0:p, i:i + 1]

    def init_consts_step(self):
        def fn(v):
            last = None
            for val, i in self._consts.items():
                last = v.memset(self._const_sb[:, i:i + 1], float(val))
            return last if last is not None else v.nop()
        self.step(vector=self.comp(fn))

    def sbuf(self, name, shape, dt):
        return self.scopes[-1].enter_context(self.nc.sbuf_tensor(name, shape, dt))

    def psum(self, name, shape, dt=F32):
        return self.scopes[-1].enter_context(self.nc.psum_tensor(name, shape, dt))

    @contextmanager
    def scope(self):
        es = ExitStack()
        self.scopes.append(es)
        try:
            yield
        finally:
            es.close()
            self.scopes.pop()

    def step(self, **parts):
        starts = dict(self.cnt)
        for eng, (kind, incs, fn) in parts.items():
            waits = [(k, starts[k]) for k in "cdg"
                     if starts[k] > self.last[eng][k]]
            semk = "c" if kind == "comp" else ("g" if eng == "gpsimd" else "d")
            def cmd(engobj, _fn=fn, _w=waits, _sh=self.bars[semk]):
                for k, val in _w:
                    engobj.wait_ge(self.bars[k], val)
                _fn(engobj, _sh)
            self.cmds[eng].append(cmd)
            for k, val in waits:
                self.last[eng][k] = val
            self.cnt[semk] += incs

    def final_wait(self):
        tot = dict(self.cnt)
        def cmd(engobj, _t=tot):
            for k in "cdg":
                if _t[k]:
                    engobj.wait_ge(self.bars[k], _t[k])
        self.cmds["sync"].append(cmd)

    @staticmethod
    def _r(ap):
        return ap() if callable(ap) else ap

    def dma(self, eng, out, in_, slow=False):
        def fn(e, semh, _o=out, _i=in_):
            e.dma_start(out=self._r(_o), in_=self._r(_i),
                        allow_slow_non_contiguous=slow).then_inc(semh, 16)
        return ("dma", 16, fn)

    def dmas(self, pairs):
        def fn(e, semh, _p=pairs):
            for _o, _i in _p:
                e.dma_start(out=self._r(_o), in_=self._r(_i)).then_inc(semh, 16)
        return ("dma", 16 * len(pairs), fn)

    def comp(self, fn):
        def wrap(e, semh, _fn=fn):
            last = _fn(e)
            last.then_inc(semh, 1)
        return ("comp", 1, wrap)

    def build(self):
        block = self.es.enter_context(self.nc.Block())
        for e in ENGINES:
            cmds = self.cmds[e]
            def run(engobj, _cmds=cmds):
                for fn in _cmds:
                    fn(engobj)
            getattr(block, e)(run)
        self.es.close()


def fm(hbm_ap, t0, tt):
    return hbm_ap[:, t0:t0 + tt].rearrange("(c p) t -> p c t", p=128)


def sb3(sb_ap, tt):
    return sb_ap.rearrange("p (c t) -> p c t", t=tt)


def fm_rows(hbm_ap, d0, d1, t0, tt):
    return hbm_ap[d0:d1, t0:t0 + tt].rearrange("(c p) t -> p c t", p=128)



def emit_layernorm(kb, src_fn, dst_hbm, nm, ntok=T, eps=1e-5):
    TT = 1024
    x_db = [kb.sbuf(nm + "x%d" % k, [128, DCH * TT], F32) for k in range(2)]
    sq_sb = kb.sbuf(nm + "q", [128, DCH * TT], F32)
    y_sb = kb.sbuf(nm + "y", [128, DCH * TT], BF16)
    st_sb = kb.sbuf(nm + "s", [1, 2 * TT], F32)
    ones_f = kb.sbuf(nm + "1", [128, 1], F32)
    ones_r = kb.sbuf(nm + "r", [1, 128], F32)
    ps_sum = kb.psum(nm + "psu", [1, TT])
    ps_sq = kb.psum(nm + "psq", [1, TT])
    ps_m = kb.psum(nm + "pm", [128, TT])
    ps_a = kb.psum(nm + "pa", [128, TT])

    kb.step(vector=kb.comp(lambda v: (v.memset(ones_f[:], 1.0),
                                      v.memset(ones_r[:], 1.0))[-1]))

    m_row = st_sb[0:1, 0:TT]
    a_row = st_sb[0:1, TT:2 * TT]

    ntile_ln = ntok // TT
    kb.step(sync=kb.dma("sync", sb3(x_db[0][:, :], TT), src_fn(0, TT)))
    for i in range(ntile_ln):
        t0 = i * TT
        x_sb = x_db[i % 2]
        parts = {"scalar": kb.comp(lambda s, _x=x_sb: s.activation(sq_sb[:, :], _x[:, :],
                                                                   AF.Square))}
        if i + 1 < ntile_ln:
            parts["sync"] = kb.dma("sync", sb3(x_db[(i + 1) % 2][:, :], TT),
                                   src_fn((i + 1) * TT, TT))
        kb.step(**parts)

        def sums(t, _x=x_sb):
            for c in range(DCH):
                for hh in range(2):
                    t.matmul(ps_sum[:, hh * 512:(hh + 1) * 512], ones_f[:, :],
                             _x[:, c * TT + hh * 512: c * TT + (hh + 1) * 512],
                             start=(c == 0), stop=(c == DCH - 1))
            for c in range(DCH):
                for hh in range(2):
                    last = t.matmul(ps_sq[:, hh * 512:(hh + 1) * 512], ones_f[:, :],
                                    sq_sb[:, c * TT + hh * 512: c * TT + (hh + 1) * 512],
                                    start=(c == 0), stop=(c == DCH - 1))
            return last
        kb.step(tensor=kb.comp(sums))

        kb.step(vector=kb.comp(lambda v: (
            v.tensor_scalar(m_row, ps_sum[:, :], 1.0 / D, None, op0=ALU.mult),
            v.tensor_scalar(a_row, ps_sq[:, :], 1.0 / D, None, op0=ALU.mult))[-1]))
        kb.step(vector=kb.comp(lambda v: v.tensor_tensor(ps_sum[0:1, :], m_row, m_row,
                                                         op=ALU.mult)))
        kb.step(vector=kb.comp(lambda v: v.tensor_tensor(a_row, a_row, ps_sum[0:1, :],
                                                         op=ALU.subtract)))
        kb.step(scalar=kb.comp(lambda s: s.activation(a_row, a_row, AF.Ln, bias=kb.c(eps, p=1))))
        kb.step(scalar=kb.comp(lambda s: s.activation(a_row, a_row, AF.Exp, scale=-0.5)))

        def bcast(t):
            for hh in range(2):
                t.matmul(ps_m[:, hh * 512:(hh + 1) * 512], ones_r[:, :],
                         m_row[0:1, hh * 512:(hh + 1) * 512], start=True, stop=True)
            for hh in range(2):
                last = t.matmul(ps_a[:, hh * 512:(hh + 1) * 512], ones_r[:, :],
                                a_row[0:1, hh * 512:(hh + 1) * 512], start=True, stop=True)
            return last
        kb.step(tensor=kb.comp(bcast))

        def apply1(v, _x=x_sb):
            for c in range(DCH):
                sl = slice(c * TT, (c + 1) * TT)
                last = v.tensor_tensor(sq_sb[:, sl], _x[:, sl], ps_m[:, :], op=ALU.subtract)
            return last

        def apply2(v):
            for c in range(DCH):
                sl = slice(c * TT, (c + 1) * TT)
                last = v.tensor_tensor(y_sb[:, sl], sq_sb[:, sl], ps_a[:, :], op=ALU.mult)
            return last
        kb.step(vector=kb.comp(apply1))
        kb.step(vector=kb.comp(apply2))
        kb.step(sync=kb.dma("sync", fm(dst_hbm, t0, TT), sb3(y_sb[:, :], TT)))



def emit_l0_attn(kb, xT, lnT, x1T, w, scratch):
    nc = kb.nc
    wqkv = kb.sbuf("a_wqkv", [128, DCH * 3072], BF16)
    wout = kb.sbuf("a_wout", [128, DCH * D], BF16)
    bout = kb.sbuf("a_bout", [128, DCH], F32)
    ones_b = kb.sbuf("a_1b", [128, 1], BF16)
    ones_rb = kb.sbuf("a_1rb", [1, 128], F32)
    kb.step(
        gpsimd=kb.dmas([
            (sb3(wqkv[:, :], 3072), w["L0a_wqkvT"].rearrange("(c p) e -> p c e", p=128)),
            (sb3(wout[:, :], D), w["L0a_woutT"].rearrange("(c p) e -> p c e", p=128)),
        ]),
        scalar=kb.dma("scalar", bout[:, :], w["L0a_bout"]),
        vector=kb.comp(lambda v: (v.memset(ones_b[:], 1.0),
                                  v.memset(ones_rb[:], 1.0))[-1]),
    )
    k_tok, v_tok, ksum_h = scratch["k_tok"], scratch["v_tok"], scratch["ksum"]
    kv_f = kb.sbuf("a3_kv", [128, DCH * D], F32)
    kv_b = kb.sbuf("a3_kvb", [128, DCH * D], BF16)

    a2 = kb.scopes
    kb.scopes.append(ExitStack())
    TTILE = 128
    lt = kb.sbuf("a2_lt", [128, DCH * TTILE], BF16)
    k_sb = kb.sbuf("a2_k", [128, D], BF16)
    v_sb = kb.sbuf("a2_v", [128, D], BF16)
    ps_k = [kb.psum("a2_pk%d" % h, [128, 512]) for h in range(2)]
    ps_v = [kb.psum("a2_pv%d" % h, [128, 512]) for h in range(2)]
    ps_ks = kb.psum("a2_ks", [1, D])
    ks_row = kb.sbuf("a2_ksr", [1, D], F32)
    ntile = T // TTILE
    for i in range(ntile):
        t0 = i * TTILE
        kb.step(sync=kb.dma("sync", sb3(lt[:, :], TTILE), fm(lnT, t0, TTILE)))

        def mmkv(t, _i=i):
            for h in range(2):
                for c in range(DCH):
                    t.matmul(ps_k[h][:, :], lt[:, c * TTILE:(c + 1) * TTILE],
                             wqkv[:, c * 3072 + 1024 + h * 512: c * 3072 + 1024 + (h + 1) * 512],
                             start=(c == 0), stop=(c == DCH - 1))
            for h in range(2):
                for c in range(DCH):
                    last = t.matmul(ps_v[h][:, :], lt[:, c * TTILE:(c + 1) * TTILE],
                                    wqkv[:, c * 3072 + 2048 + h * 512: c * 3072 + 2048 + (h + 1) * 512],
                                    start=(c == 0), stop=(c == DCH - 1))
            return last
        kb.step(tensor=kb.comp(mmkv))

        def evac(s, _i=i):
            s.activation(k_sb[:, 0:512], ps_k[0][:, :], AF.Relu)
            s.activation(k_sb[:, 512:1024], ps_k[1][:, :], AF.Relu)
            s.copy(v_sb[:, 0:512], ps_v[0][:, :])
            return s.copy(v_sb[:, 512:1024], ps_v[1][:, :])
        def keps(v, _i=i):
            return v.tensor_scalar(k_sb[:, :], k_sb[:, :], 1e-6, None, op0=ALU.add)
        kb.step(scalar=kb.comp(evac))
        kb.step(vector=kb.comp(keps))

        def ksmm(t, _i=i):
            t.matmul(ps_ks[0:1, 0:512], ones_b[:, :], k_sb[:, 0:512],
                     start=(_i == 0), stop=(_i == ntile - 1))
            return t.matmul(ps_ks[0:1, 512:1024], ones_b[:, :], k_sb[:, 512:1024],
                            start=(_i == 0), stop=(_i == ntile - 1))
        kb.step(
            tensor=kb.comp(ksmm),
            sync=kb.dma("sync", k_tok[t0:t0 + TTILE, :], k_sb[:, :]),
            gpsimd=kb.dma("gpsimd", v_tok[t0:t0 + TTILE, :], v_sb[:, :]),
        )
    kb.step(vector=kb.comp(lambda v: v.tensor_copy(ks_row[:, :], ps_ks[:, :])))
    kb.step(sync=kb.dma("sync", ksum_h.ap(), ks_row[:, :]))

    kb.scopes.pop().close()
    kb.scopes.append(ExitStack())
    kt = kb.sbuf("a3_kt", [128, D], BF16)
    vt = kb.sbuf("a3_vt", [128, 512], BF16)
    ps_kv = [kb.psum("a3_p%d" % c, [128, 512]) for c in range(DCH)]
    for half in range(2):
        e0 = half * 512
        for i in range(ntile):
            t0 = i * TTILE
            kb.step(sync=kb.dma("sync", kt[:, :], k_tok[t0:t0 + TTILE, :]),
                    gpsimd=kb.dma("gpsimd", vt[:, :], v_tok[t0:t0 + TTILE, e0:e0 + 512]))

            def mmkv2(t, _i=i):
                for c in range(DCH):
                    last = t.matmul(ps_kv[c][:, :], kt[:, c * 128:(c + 1) * 128], vt[:, :],
                                    start=(_i == 0), stop=(_i == ntile - 1))
                return last
            kb.step(tensor=kb.comp(mmkv2))

        def evkv(v, _e0=e0):
            for c in range(DCH):
                v.tensor_copy(kv_f[:, c * D + _e0: c * D + _e0 + 512], ps_kv[c][:, :])
                last = v.tensor_copy(kv_b[:, c * D + _e0: c * D + _e0 + 512], ps_kv[c][:, :])
            return last
        kb.step(vector=kb.comp(evkv))

    kb.scopes.pop().close()
    kb.scopes.append(ExitStack())
    TQ = 512
    ln_db = [kb.sbuf("a4_ln%d" % k, [128, DCH * TQ], BF16) for k in range(2)]
    x_db = [kb.sbuf("a4_xd%d" % k, [128, DCH * TQ], F32) for k in range(2)]
    q_sb = kb.sbuf("a4_q", [128, DCH * TQ], BF16)
    at_sb = kb.sbuf("a4_at", [128, DCH * TQ], BF16)
    x1_sb = kb.sbuf("a4_x1", [128, DCH * TQ], F32)
    z_row = kb.sbuf("a4_z", [1, TQ], F32)
    zbc_sb = kb.sbuf("a4_zb", [128, TQ], F32)
    ksum_c = kb.sbuf("a4_ks", [128, DCH], BF16)
    ps_qp = [kb.psum("a4_pq%d" % k, [128, TQ]) for k in range(2)]
    ps_z = kb.psum("a4_pz", [1, TQ])
    ps_zb = kb.psum("a4_pzb", [128, TQ])
    ps_ap = [kb.psum("a4_pa%d" % k, [128, TQ]) for k in range(2)]
    ps_yp = [kb.psum("a4_py%d" % k, [128, TQ]) for k in range(2)]
    kb.step(gpsimd=kb.dma("gpsimd", ksum_c[:, :],
                          ksum_h.ap().rearrange("(c p) -> p c", p=128), slow=True))

    NT4 = T // TQ
    kb.step(sync=kb.dma("sync", sb3(ln_db[0][:, :], TQ), fm(lnT, 0, TQ)),
            gpsimd=kb.dma("gpsimd", sb3(x_db[0][:, :], TQ), fm(xT, 0, TQ)))
    for i in range(NT4):
        t0 = i * TQ
        ln_t = ln_db[i % 2]
        x_t = x_db[i % 2]
        def mmq(t, _ec, _ln=ln_t):
            for c in range(DCH):
                last = t.matmul(ps_qp[_ec % 2][:, :],
                                wqkv[:, c * 3072 + _ec * 128: c * 3072 + _ec * 128 + 128],
                                _ln[:, c * TQ:(c + 1) * TQ],
                                start=(c == 0), stop=(c == DCH - 1))
            return last

        def evq(s, _ec):
            return s.activation(q_sb[:, _ec * TQ:(_ec + 1) * TQ], ps_qp[_ec % 2][:, :], AF.Relu)
        for ec in range(DCH + 1):
            parts = {}
            if ec < DCH:
                parts["tensor"] = kb.comp(lambda t, _e=ec, _f=mmq: _f(t, _e))
            if ec > 0:
                parts["scalar"] = kb.comp(lambda s, _e=ec - 1, _f=evq: _f(s, _e))
            if ec == 0 and i + 1 < NT4:
                parts["sync"] = kb.dma("sync", sb3(ln_db[(i + 1) % 2][:, :], TQ),
                                       fm(lnT, (i + 1) * TQ, TQ))
                parts["gpsimd"] = kb.dma("gpsimd", sb3(x_db[(i + 1) % 2][:, :], TQ),
                                         fm(xT, (i + 1) * TQ, TQ))
            kb.step(**parts)
        kb.step(vector=kb.comp(lambda v: v.tensor_scalar(q_sb[:, :], q_sb[:, :], 1e-6, None, op0=ALU.add)))
        def mmz(t):
            for c in range(DCH):
                last = t.matmul(ps_z[:, :], ksum_c[:, c:c + 1], q_sb[:, c * TQ:(c + 1) * TQ],
                                start=(c == 0), stop=(c == DCH - 1))
            return last
        kb.step(tensor=kb.comp(mmz))
        kb.step(vector=kb.comp(lambda v: v.tensor_scalar(z_row[:, :], ps_z[:, :], 1e-6,
                                                         None, op0=ALU.add)))
        kb.step(vector=kb.comp(lambda v: v.reciprocal(z_row[:, :], z_row[:, :])))
        kb.step(tensor=kb.comp(lambda t: t.matmul(ps_zb[:, :], ones_rb[:, :], z_row[:, :],
                                                  start=True, stop=True)))
        kb.step(vector=kb.comp(lambda v: v.tensor_copy(zbc_sb[:, :], ps_zb[:, :])))
        def mma(t, _ec):
            for c in range(DCH):
                last = t.matmul(ps_ap[_ec % 2][:, :],
                                kv_b[:, c * D + _ec * 128: c * D + _ec * 128 + 128],
                                q_sb[:, c * TQ:(c + 1) * TQ],
                                start=(c == 0), stop=(c == DCH - 1))
            return last

        def eva(v, _ec):
            return v.scalar_tensor_tensor(at_sb[:, _ec * TQ:(_ec + 1) * TQ],
                                          ps_ap[_ec % 2][:, :], 1.0, zbc_sb[:, :],
                                          op0=ALU.mult, op1=ALU.mult)
        for ec in range(DCH + 1):
            parts = {}
            if ec < DCH:
                parts["tensor"] = kb.comp(lambda t, _e=ec, _f=mma: _f(t, _e))
            if ec > 0:
                parts["vector"] = kb.comp(lambda v, _e=ec - 1, _f=eva: _f(v, _e))
            kb.step(**parts)
        def mmyy(t, _fc):
            for ec in range(DCH):
                last = t.matmul(ps_yp[_fc % 2][:, :],
                                wout[:, ec * D + _fc * 128: ec * D + _fc * 128 + 128],
                                at_sb[:, ec * TQ:(ec + 1) * TQ],
                                start=(ec == 0), stop=(ec == DCH - 1))
            return last

        def evyy(v, _fc, _x=x_t):
            return v.scalar_tensor_tensor(x1_sb[:, _fc * TQ:(_fc + 1) * TQ],
                                          ps_yp[_fc % 2][:, :], bout[:, _fc:_fc + 1],
                                          _x[:, _fc * TQ:(_fc + 1) * TQ],
                                          op0=ALU.add, op1=ALU.add)
        for fc in range(DCH + 1):
            parts = {}
            if fc < DCH:
                parts["tensor"] = kb.comp(lambda t, _x=fc, _f=mmyy: _f(t, _x))
            if fc > 0:
                parts["vector"] = kb.comp(lambda v, _x=fc - 1, _f=evyy: _f(v, _x))
            kb.step(**parts)
        kb.step(sync=kb.dma("sync", fm(x1T, t0, TQ), sb3(x1_sb[:, :], TQ)))
    kb.scopes.pop().close()



NSEL = 512
DFF0 = 2816


def emit_l0_ffn(kb, x1T, lnT, x2T, w, scr):
    nc = kb.nc
    NT = T // 128
    ln_tok = scr["b_lntok"]
    b_row = scr["b_row"]

    stok = kb.sbuf("b_stok", [128, NT], F32)
    rank_tok = kb.sbuf("b_rank", [128, NT], F32)
    xsel_tok = kb.sbuf("b_xselt", [128, 4 * D], BF16)
    xselT = kb.sbuf("b_xselT", [128, DCH * NSEL], BF16)
    x_attnT = kb.sbuf("b_xat", [128, DCH * NSEL], BF16)
    delta_tok = kb.sbuf("b_dtok", [128, 4 * D], BF16)
    id128b = kb.sbuf("b_idb", [128, 128], BF16)
    id128f = kb.sbuf("b_idf", [128, 128], F32)
    ones_c = kb.sbuf("b_1c", [128, 1], F32)
    ones_r = kb.sbuf("b_1r", [1, 128], F32)
    kb.step(sync=kb.dmas([(id128f[:, :], w["ident_f32"])]),
            vector=kb.comp(lambda v: (v.memset(ones_c[:], 1.0),
                                      v.memset(ones_r[:], 1.0))[-1]))
    ones_cb = kb.sbuf("b_1cb", [128, 1], BF16)
    kb.step(vector=kb.comp(lambda v: (v.tensor_copy(id128b[:, :], id128f[:, :]),
                                      v.memset(ones_cb[:], 1.0))[-1]))

    def _b1():
        x_sb = kb.sbuf("b1x", [128, DCH * 128], F32)
        sq_sb = kb.sbuf("b1q", [128, DCH * 128], F32)
        wrc = kb.sbuf("b1wr", [128, DCH], F32)
        msum = kb.sbuf("b1ms", [128, NT], F32)
        qsum = kb.sbuf("b1qs", [128, NT], F32)
        sraw = kb.sbuf("b1sr", [128, NT], F32)
        m_all = kb.sbuf("b1m", [128, NT], F32)
        nvar = kb.sbuf("b1nv", [128, NT], F32)
        a_all = kb.sbuf("b1a", [128, NT], F32)
        t1 = kb.sbuf("b1t1", [128, NT], F32)
        swrow = kb.sbuf("b1sw", [1, DCH], F32)
        swc = kb.sbuf("b1swc", [128, 1], F32)
        ps_m = kb.psum("b1pm", [128, 1])
        ps_q = kb.psum("b1pq", [128, 1])
        ps_s = kb.psum("b1ps", [128, 1])
        ps_sw = kb.psum("b1psw", [1, DCH])
        ps_swc = kb.psum("b1pswc", [128, 1])
        kb.step(sync=kb.dma("sync", wrc[:, :], w["L0f_wrc"]))
        kb.step(tensor=kb.comp(lambda t: t.matmul(ps_sw[:, :], ones_c[:, :], wrc[:, :],
                                                  start=True, stop=True)))
        kb.step(vector=kb.comp(lambda v: v.tensor_reduce(swrow[0:1, 0:1], ps_sw[:, :],
                                                         axis=AXX, op=ALU.add)))
        kb.step(tensor=kb.comp(lambda t: t.matmul(ps_swc[:, :], ones_r[:, :], swrow[0:1, 0:1],
                                                  start=True, stop=True)))
        kb.step(vector=kb.comp(lambda v: v.tensor_copy(swc[:, :], ps_swc[:, :])))
        for i in range(NT):
            kb.step(sync=kb.dma("sync", sb3(x_sb[:, :], 128), fm(x1T, i * 128, 128)))
            kb.step(scalar=kb.comp(lambda s: s.activation(sq_sb[:, :], x_sb[:, :], AF.Square)))

            def mm(t):
                for c in range(DCH):
                    t.matmul(ps_m[:, :], x_sb[:, c * 128:(c + 1) * 128], ones_c[:, :],
                             start=(c == 0), stop=(c == DCH - 1))
                for c in range(DCH):
                    t.matmul(ps_q[:, :], sq_sb[:, c * 128:(c + 1) * 128], ones_c[:, :],
                             start=(c == 0), stop=(c == DCH - 1))
                for c in range(DCH):
                    last = t.matmul(ps_s[:, :], x_sb[:, c * 128:(c + 1) * 128],
                                    wrc[:, c:c + 1],
                                    start=(c == 0), stop=(c == DCH - 1))
                return last
            kb.step(tensor=kb.comp(mm))

            def collect(v, _i=i):
                v.tensor_copy(msum[:, _i:_i + 1], ps_m[:, :])
                v.tensor_copy(qsum[:, _i:_i + 1], ps_q[:, :])
                return v.tensor_copy(sraw[:, _i:_i + 1], ps_s[:, :])
            kb.step(vector=kb.comp(collect))

        def vstep(fn):
            kb.step(vector=kb.comp(fn))
        vstep(lambda v: (v.tensor_scalar(m_all[:, :], msum[:, :], 1.0 / D, None, op0=ALU.mult),
                         v.tensor_scalar(qsum[:, :], qsum[:, :], 1.0 / D, None, op0=ALU.mult))[-1])
        vstep(lambda v: v.tensor_tensor(nvar[:, :], m_all[:, :], m_all[:, :], op=ALU.mult))
        vstep(lambda v: v.tensor_tensor(nvar[:, :], nvar[:, :], qsum[:, :], op=ALU.subtract))
        kb.step(scalar=kb.comp(lambda s: s.activation(a_all[:, :], nvar[:, :], AF.Ln,
                                                      scale=-1.0, bias=kb.c(1e-5))))
        kb.step(scalar=kb.comp(lambda s: s.activation(a_all[:, :], a_all[:, :], AF.Exp,
                                                      scale=-0.5)))
        vstep(lambda v: v.tensor_scalar(t1[:, :], m_all[:, :], swc[:, 0:1], None, op0=ALU.mult))
        vstep(lambda v: v.tensor_tensor(t1[:, :], sraw[:, :], t1[:, :], op=ALU.subtract))
        vstep(lambda v: v.tensor_tensor(stok[:, :], t1[:, :], a_all[:, :], op=ALU.mult))

    with kb.scope():
        _b1()

    def _b1b():
        lt = kb.sbuf("b1b_lt", [128, DCH * 128], BF16)
        tt_sb = kb.sbuf("b1b_tt", [128, D], BF16)
        ps_t = kb.psum("b1b_pt", [128, D], BF16)
        for i in range(NT):
            kb.step(sync=kb.dma("sync", sb3(lt[:, :], 128), fm(lnT, i * 128, 128)))

            def trs(t):
                for c in range(DCH):
                    last = t.transpose(ps_t[:, c * 128:(c + 1) * 128],
                                       lt[:, c * 128:(c + 1) * 128], id128b[:, :])
                return last
            kb.step(tensor=kb.comp(trs))
            kb.step(vector=kb.comp(lambda v: v.tensor_copy(tt_sb[:, :], ps_t[:, :])))
            kb.step(sync=kb.dma("sync", ln_tok[i * 128:(i + 1) * 128, :], tt_sb[:, :]))

    with kb.scope():
        _b1b()

    def _b2():
        iota512 = kb.sbuf("b2i512", [128, NSEL], F32)
        pt_sb = kb.sbuf("b2pt", [128, NSEL], BF16)
        lt_tok = kb.sbuf("b2lt", [128, D], BF16)
        kb.step(sync=kb.dma("sync", iota512[:, :], w["iota512"]))
        with kb.scope():
            bigbc = kb.sbuf("b2bc", [128, T], F32)
            row = kb.sbuf("b2row", [1, T], F32)
            junk = kb.sbuf("b2jk", [128, T], BF16)
            bounce = kb.sbuf("b2bn", [NT, 128], F32)
            ps_bn = kb.psum("b2pbn", [NT, 128])
            ps_bc = kb.psum("b2pbc", [128, 512])

            def bounce_out(src_sb):
                kb.step(tensor=kb.comp(lambda t: t.transpose(ps_bn[:, :], src_sb, id128f[:, :])))
                kb.step(vector=kb.comp(lambda v: v.tensor_copy(bounce[:, :], ps_bn[:, :])))
                kb.step(sync=kb.dma("sync", b_row.ap(), bounce[:, :]))

            bounce_out(stok[:, :])
            kb.step(sync=kb.dma("sync", row[:, :],
                                b_row.ap().rearrange("i p -> (i p)").rearrange("(a b) -> a b", a=1)))
            for j in range(T // 512):
                kb.step(tensor=kb.comp(lambda t, _j=j: t.matmul(
                    ps_bc[:, :], ones_r[:, :], row[:, _j * 512:(_j + 1) * 512],
                    start=True, stop=True)))
                kb.step(vector=kb.comp(lambda v, _j=j: v.tensor_copy(
                    bigbc[:, _j * 512:(_j + 1) * 512], ps_bc[:, :])))

            for i in range(NT):
                kb.step(vector=kb.comp(lambda v, _i=i: v.tensor_scalar(
                    junk[:, :], bigbc[:, :], stok[:, _i:_i + 1], 0.0,
                    op0=ALU.is_gt, op1=ALU.add, accum_out=rank_tok[:, _i:_i + 1])))
            bounce_out(rank_tok[:, :])

        kb.scopes.append(ExitStack())
        ps_g = [kb.psum("b2pg%d" % sc, [128, D]) for sc in range(4)]
        for i in range(NT):
            kb.step(sync=kb.dma("sync", lt_tok[:, :], ln_tok[i * 128:(i + 1) * 128, :]),
                    vector=kb.comp(lambda v, _i=i: v.tensor_scalar(
                        pt_sb[:, :], iota512[:, :], rank_tok[:, _i:_i + 1], None,
                        op0=ALU.is_equal)))

            def mmg(t, _i=i):
                for sc in range(4):
                    for half in range(2):
                        last = t.matmul(ps_g[sc][:, half * 512:(half + 1) * 512],
                                        pt_sb[:, sc * 128:(sc + 1) * 128],
                                        lt_tok[:, half * 512:(half + 1) * 512],
                                        start=(_i == 0), stop=(_i == NT - 1))
                return last
            kb.step(tensor=kb.comp(mmg))

        def evg(v):
            for sc in range(4):
                last = v.tensor_copy(xsel_tok[:, sc * D:(sc + 1) * D], ps_g[sc][:, :])
            return last
        kb.step(vector=kb.comp(evg))
        kb.scopes.pop().close()
        ps_tx = kb.psum("b2ptx", [128, 512], BF16)
        for dc in range(DCH):
            def trx(t, _dc=dc):
                for sc in range(4):
                    last = t.transpose(ps_tx[:, sc * 128:(sc + 1) * 128],
                                       xsel_tok[:, sc * D + _dc * 128: sc * D + _dc * 128 + 128],
                                       id128b[:, :])
                return last
            kb.step(tensor=kb.comp(trx))
            kb.step(vector=kb.comp(lambda v, _dc=dc: v.tensor_copy(
                xselT[:, _dc * NSEL:(_dc + 1) * NSEL], ps_tx[:, :])))

    with kb.scope():
        _b2()

    def _b3a():
        wqkv = kb.sbuf("b3wq", [128, DCH * 3072], BF16)
        wout = kb.sbuf("b3wo", [128, DCH * D], BF16)
        g1c = kb.sbuf("b3g1", [128, DCH], F32)
        mask = kb.sbuf("b3mk", [128, 4 * NSEL], F32)
        sq_sb = kb.sbuf("b3sq", [128, DCH * NSEL], F32)
        rr_bc = kb.sbuf("b3rr", [128, NSEL], F32)
        rowbuf = kb.sbuf("b3row", [1, NSEL], F32)
        h1T = kb.sbuf("b3h1", [128, DCH * NSEL], BF16)
        qkvT = kb.sbuf("b3qkv", [128, 24 * NSEL], BF16)
        vtok = kb.sbuf("b3vt", [128, 4 * D], BF16)
        s4_sb = kb.sbuf("b3s4", [128, 4 * NSEL], F32)
        cols = kb.sbuf("b3c", [128, 8], F32)
        a4_sb = kb.sbuf("b3a4", [128, 4 * NSEL], BF16)
        at4_sb = kb.sbuf("b3at4", [128, 4 * NSEL], BF16)
        oT = kb.sbuf("b3oT", [128, DCH * NSEL], BF16)
        ps_a = kb.psum("b3pa", [128, NSEL])
        ps_b = kb.psum("b3pb", [128, NSEL])
        ps_s4 = [ps_a, ps_b, kb.psum("b3ps2", [128, NSEL]), kb.psum("b3ps3", [128, NSEL])]
        ps_at2 = kb.psum("b3pat", [128, 2 * NSEL], BF16)
        ps_o = kb.psum("b3po", [128, NSEL])
        kb.step(gpsimd=kb.dmas([
            (sb3(wqkv[:, :], 3072), w["L0f_wqkvT"].rearrange("(c p) e -> p c e", p=128)),
            (sb3(wout[:, :], D), w["L0f_woutT"].rearrange("(c p) e -> p c e", p=128)),
        ]), sync=kb.dmas([
            (g1c[:, :], w["L0f_g1c"]),
            (mask[:, :].rearrange("p (q k) -> p q k", k=NSEL),
             w["mask512"].rearrange("(q p) k -> p q k", p=128)),
        ]))

        def rms_rows(src_T, gcol_sb, out_T, eps):
            kb.step(scalar=kb.comp(lambda s: s.activation(sq_sb[:, :], src_T, AF.Square)))

            def mmq(t):
                for c in range(DCH):
                    last = t.matmul(ps_r[:, :], ones_c[:, :],
                                    sq_sb[:, c * NSEL:(c + 1) * NSEL],
                                    start=(c == 0), stop=(c == DCH - 1))
                return last
            kb.step(tensor=kb.comp(mmq))
            kb.step(scalar=kb.comp(lambda s: s.activation(rowbuf[:, :], ps_r[:, :], AF.Ln,
                                                          scale=1.0 / D, bias=kb.c(eps, p=1))))
            kb.step(scalar=kb.comp(lambda s: s.activation(rowbuf[:, :], rowbuf[:, :], AF.Exp,
                                                          scale=-0.5)))
            kb.step(tensor=kb.comp(lambda t: t.matmul(ps_rr[:, :], ones_r[:, :], rowbuf[:, :],
                                                      start=True, stop=True)))
            kb.step(vector=kb.comp(lambda v: v.tensor_copy(rr_bc[:, :], ps_rr[:, :])))

            def app(v):
                for c in range(DCH):
                    sl = slice(c * NSEL, (c + 1) * NSEL)
                    last = v.scalar_tensor_tensor(out_T[:, sl], src_T[:, sl],
                                                  gcol_sb[:, c:c + 1], rr_bc[:, :],
                                                  op0=ALU.mult, op1=ALU.mult)
                return last
            kb.step(vector=kb.comp(app))

        kb.scopes.append(ExitStack())
        ps_r = kb.psum("b3pr", [1, NSEL])
        ps_rr = kb.psum("b3prr", [128, NSEL])
        rms_rows(xselT[:, :], g1c, h1T, 1e-6)
        kb.scopes.pop().close()

        def mmp(t, _ec):
            for c in range(DCH):
                last = t.matmul(ps_a[:, :] if _ec % 2 == 0 else ps_b[:, :],
                                wqkv[:, c * 3072 + _ec * 128: c * 3072 + _ec * 128 + 128],
                                h1T[:, c * NSEL:(c + 1) * NSEL],
                                start=(c == 0), stop=(c == DCH - 1))
            return last

        def evp(s, _ec):
            return s.activation(qkvT[:, _ec * NSEL:(_ec + 1) * NSEL],
                                ps_a[:, :] if _ec % 2 == 0 else ps_b[:, :],
                                AF.Copy, scale=(0.125 if _ec < 8 else 1.0))
        for ec in range(25):
            parts = {}
            if ec < 24:
                parts["tensor"] = kb.comp(lambda t, _e=ec, _f=mmp: _f(t, _e))
            if ec > 0:
                parts["scalar"] = kb.comp(lambda s, _e=ec - 1, _f=evp: _f(s, _e))
            kb.step(**parts)

        for sc in range(4):
            for half in range(2):
                def mmv(t, _sc=sc, _h=half):
                    for c in range(DCH):
                        last = t.matmul(ps_a[:, :],
                                        h1T[:, c * NSEL + _sc * 128: c * NSEL + _sc * 128 + 128],
                                        wqkv[:, c * 3072 + 2048 + _h * 512: c * 3072 + 2048 + (_h + 1) * 512],
                                        start=(c == 0), stop=(c == DCH - 1))
                    return last
                kb.step(tensor=kb.comp(mmv))
                kb.step(vector=kb.comp(lambda v, _sc=sc, _h=half: v.tensor_copy(
                    vtok[:, _sc * D + _h * 512: _sc * D + (_h + 1) * 512], ps_a[:, :])))

        for h in range(16):
            ec_q = h // 2
            ec_k = 8 + h // 2
            pb = 64 * (h % 2)
            k_ap = qkvT[pb:pb + 64, ec_k * NSEL:(ec_k + 1) * NSEL]

            def mmS4(t, _h=h, _pb=pb, _eq=ec_q, _k=k_ap):
                for qc in range(4):
                    q_ap = qkvT[_pb:_pb + 64, _eq * NSEL + qc * 128: _eq * NSEL + qc * 128 + 128]
                    last = t.matmul(ps_s4[qc][:, :], q_ap, _k, start=True, stop=True)
                return last
            kb.step(tensor=kb.comp(mmS4))
            kb.step(vector=kb.comp(lambda v: [
                v.tensor_tensor(s4_sb[:, qc * NSEL:(qc + 1) * NSEL], ps_s4[qc][:, :],
                                mask[:, qc * NSEL:(qc + 1) * NSEL], op=ALU.add)
                for qc in range(4)][-1]))
            kb.step(vector=kb.comp(lambda v: [
                v.tensor_reduce(cols[:, qc:qc + 1], s4_sb[:, qc * NSEL:(qc + 1) * NSEL],
                                axis=AXX, op=ALU.max) for qc in range(4)][-1]))
            kb.step(vector=kb.comp(lambda v: [
                v.tensor_scalar(s4_sb[:, qc * NSEL:(qc + 1) * NSEL],
                                s4_sb[:, qc * NSEL:(qc + 1) * NSEL],
                                cols[:, qc:qc + 1], None, op0=ALU.subtract)
                for qc in range(4)][-1]))
            kb.step(scalar=kb.comp(lambda s: s.activation(s4_sb[:, :], s4_sb[:, :], AF.Exp)))
            kb.step(vector=kb.comp(lambda v: [
                v.tensor_reduce(cols[:, 4 + qc:5 + qc], s4_sb[:, qc * NSEL:(qc + 1) * NSEL],
                                axis=AXX, op=ALU.add) for qc in range(4)][-1]))
            kb.step(vector=kb.comp(lambda v: v.reciprocal(cols[:, 4:8], cols[:, 4:8])))
            kb.step(vector=kb.comp(lambda v: [
                v.tensor_scalar(a4_sb[:, qc * NSEL:(qc + 1) * NSEL],
                                s4_sb[:, qc * NSEL:(qc + 1) * NSEL],
                                cols[:, 4 + qc:5 + qc], None, op0=ALU.mult)
                for qc in range(4)][-1]))

            for r in range(2):
                def tra4(t, _r=r):
                    for qq in range(2):
                        qc = 2 * _r + qq
                        for kc in range(4):
                            last = t.transpose(
                                ps_at2[:, qq * NSEL + kc * 128: qq * NSEL + kc * 128 + 128],
                                a4_sb[:, qc * NSEL + kc * 128: qc * NSEL + kc * 128 + 128],
                                id128b[:, :])
                    return last
                kb.step(tensor=kb.comp(tra4))
                kb.step(vector=kb.comp(lambda v, _r=r: v.tensor_copy(
                    at4_sb[:, _r * 2 * NSEL:(_r + 1) * 2 * NSEL], ps_at2[:, :])))

            def mmo4(t, _h=h, _pb=pb):
                for qc in range(4):
                    for kc in range(4):
                        last = t.matmul(ps_o[_pb:_pb + 64, qc * 128:(qc + 1) * 128],
                                        vtok[:, kc * D + _h * 64: kc * D + _h * 64 + 64],
                                        at4_sb[:, qc * NSEL + kc * 128: qc * NSEL + kc * 128 + 128],
                                        start=(kc == 0), stop=(kc == 3))
                return last
            kb.step(tensor=kb.comp(mmo4))
            if h % 2 == 1:
                hp = h // 2
                kb.step(vector=kb.comp(lambda v, _hp=hp: v.tensor_copy(
                    oT[:, _hp * NSEL:(_hp + 1) * NSEL], ps_o[:, :])))

        for fc in range(DCH):
            def mmw(t, _fc=fc):
                for ec in range(DCH):
                    last = t.matmul(ps_a[:, :],
                                    wout[:, ec * D + _fc * 128: ec * D + _fc * 128 + 128],
                                    oT[:, ec * NSEL:(ec + 1) * NSEL],
                                    start=(ec == 0), stop=(ec == DCH - 1))
                return last
            kb.step(tensor=kb.comp(mmw))
            kb.step(vector=kb.comp(lambda v, _fc=fc: v.tensor_copy(
                x_attnT[:, _fc * NSEL:(_fc + 1) * NSEL], ps_a[:, :])))

    with kb.scope():
        _b3a()

    def _b3b():
        w1T = kb.sbuf("b4w1", [128, DCH * DFF0], BF16)
        w2T = kb.sbuf("b4w2", [128, DCH * DFF0], BF16)
        w3fc2 = [kb.sbuf("b4w3%d" % k, [128, 22 * 128], BF16) for k in range(2)]
        g2c = kb.sbuf("b4g2", [128, DCH], F32)
        s2T = kb.sbuf("b4s2", [128, DCH * NSEL], BF16)
        sq_sb = kb.sbuf("b4sq", [128, DCH * NSEL], BF16)
        rr_bc = kb.sbuf("b4rr", [128, NSEL], F32)
        rowbuf = kb.sbuf("b4row", [1, NSEL], F32)
        h2T = kb.sbuf("b4h2", [128, DCH * NSEL], BF16)
        uT = kb.sbuf("b4u", [128, 22 * NSEL], BF16)
        tsi2 = [kb.sbuf("b4si%d" % k, [128, NSEL], F32) for k in range(2)]
        dT = kb.sbuf("b4dT", [128, DCH * NSEL], BF16)
        ps_r = kb.psum("b4pr", [1, NSEL])
        ps_rr = kb.psum("b4prr", [128, NSEL])
        ps_1p = [kb.psum("b4p1%d" % k, [128, NSEL]) for k in range(2)]
        ps_2p = [kb.psum("b4p2%d" % k, [128, NSEL]) for k in range(3)]
        ps_tx = kb.psum("b4ptx", [128, 512], BF16)
        kb.step(gpsimd=kb.dmas([
            (sb3(w1T[:, :], DFF0), w["L0f_w1T"].rearrange("(c p) e -> p c e", p=128)),
            (sb3(w2T[:, :], DFF0), w["L0f_w2T"].rearrange("(c p) e -> p c e", p=128)),
        ]), sync=kb.dma("sync", g2c[:, :], w["L0f_g2c"]))

        kb.step(vector=kb.comp(lambda v: [
            v.tensor_tensor(s2T[:, c * NSEL:(c + 1) * NSEL],
                            xselT[:, c * NSEL:(c + 1) * NSEL],
                            x_attnT[:, c * NSEL:(c + 1) * NSEL], op=ALU.add)
            for c in range(DCH)][-1]))

        def rms_rows2(src_T, gcol_sb, out_T, eps):
            kb.step(scalar=kb.comp(lambda s: s.activation(sq_sb[:, :], src_T, AF.Square)))

            def mmq(t):
                for c in range(DCH):
                    last = t.matmul(ps_r[:, :], ones_cb[:, :],
                                    sq_sb[:, c * NSEL:(c + 1) * NSEL],
                                    start=(c == 0), stop=(c == DCH - 1))
                return last
            kb.step(tensor=kb.comp(mmq))
            kb.step(scalar=kb.comp(lambda s: s.activation(rowbuf[:, :], ps_r[:, :], AF.Ln,
                                                          scale=1.0 / D, bias=kb.c(eps, p=1))))
            kb.step(scalar=kb.comp(lambda s: s.activation(rowbuf[:, :], rowbuf[:, :], AF.Exp,
                                                          scale=-0.5)))
            kb.step(tensor=kb.comp(lambda t: t.matmul(ps_rr[:, :], ones_r[:, :], rowbuf[:, :],
                                                      start=True, stop=True)))
            kb.step(vector=kb.comp(lambda v: v.tensor_copy(rr_bc[:, :], ps_rr[:, :])))

            def app(v):
                for c in range(DCH):
                    sl = slice(c * NSEL, (c + 1) * NSEL)
                    last = v.scalar_tensor_tensor(out_T[:, sl], src_T[:, sl],
                                                  gcol_sb[:, c:c + 1], rr_bc[:, :],
                                                  op0=ALU.mult, op1=ALU.mult)
                return last
            kb.step(vector=kb.comp(app))

        rms_rows2(s2T[:, :], g2c, h2T, 1e-6)

        def mmu(t, _jc):
            for c in range(DCH):
                t.matmul(ps_1p[_jc % 2][:, :],
                         w1T[:, c * DFF0 + _jc * 128: c * DFF0 + _jc * 128 + 128],
                         h2T[:, c * NSEL:(c + 1) * NSEL],
                         start=(c == 0), stop=(c == DCH - 1))
            for c in range(DCH):
                last = t.matmul(ps_2p[_jc % 3][:, :],
                                w2T[:, c * DFF0 + _jc * 128: c * DFF0 + _jc * 128 + 128],
                                h2T[:, c * NSEL:(c + 1) * NSEL],
                                start=(c == 0), stop=(c == DCH - 1))
            return last

        def evsi(s, _jc):
            return s.activation(tsi2[_jc % 2][:, :], ps_1p[_jc % 2][:, :], AF.Silu)

        def evmu(v, _jc):
            return v.tensor_tensor(uT[:, _jc * NSEL:(_jc + 1) * NSEL], tsi2[_jc % 2][:, :],
                                   ps_2p[_jc % 3][:, :], op=ALU.mult)
        for jc in range(24):
            parts = {}
            if jc < 22:
                parts["tensor"] = kb.comp(lambda t, _j=jc, _f=mmu: _f(t, _j))
            if 0 < jc <= 22:
                parts["scalar"] = kb.comp(lambda s, _j=jc - 1, _f=evsi: _f(s, _j))
            if jc > 1:
                parts["vector"] = kb.comp(lambda v, _j=jc - 2, _f=evmu: _f(v, _j))
            kb.step(**parts)

        for fc in range(DCH):
            kb.step(gpsimd=kb.dma("gpsimd",
                                  w3fc2[fc % 2][:, :].rearrange("p (c f) -> p c f", f=128),
                                  w["L0f_w3T"][:, fc * 128:(fc + 1) * 128]
                                  .rearrange("(c p) f -> p c f", p=128)))

            def mmf(t, _fc=fc):
                for hc in range(22):
                    last = t.matmul(ps_1p[_fc % 2][:, :],
                                    w3fc2[_fc % 2][:, :][:, hc * 128:(hc + 1) * 128],
                                    uT[:, hc * NSEL:(hc + 1) * NSEL],
                                    start=(hc == 0), stop=(hc == 21))
                return last
            kb.step(tensor=kb.comp(mmf))
            kb.step(vector=kb.comp(lambda v, _fc=fc: v.tensor_tensor(
                dT[:, _fc * NSEL:(_fc + 1) * NSEL], ps_1p[_fc % 2][:, :],
                x_attnT[:, _fc * NSEL:(_fc + 1) * NSEL], op=ALU.add)))

        for sc in range(4):
            for dc in range(DCH):
                kb.step(tensor=kb.comp(lambda t, _sc=sc, _dc=dc: t.transpose(
                    ps_tx[:, (_dc % 4) * 128:((_dc % 4) + 1) * 128],
                    dT[:, _dc * NSEL + _sc * 128: _dc * NSEL + _sc * 128 + 128],
                    id128b[:, :])))
                kb.step(vector=kb.comp(lambda v, _sc=sc, _dc=dc: v.tensor_copy(
                    delta_tok[:, _sc * D + _dc * 128: _sc * D + (_dc + 1) * 128],
                    ps_tx[:, (_dc % 4) * 128:((_dc % 4) + 1) * 128])))

    with kb.scope():
        _b3b()

    def _b4():
        rbc = kb.sbuf("b5rbc", [128, T], F32)
        row = kb.sbuf("b5row", [1, T], F32)
        slotc = kb.sbuf("b5slc", [128, 4], F32)
        p_sc = kb.sbuf("b5p", [128, 4 * 512], BF16)
        x1_t = kb.sbuf("b5x1", [128, DCH * 512], F32)
        ln_t = kb.sbuf("b5ln", [128, DCH * 512], BF16)
        x2_t = kb.sbuf("b5x2", [128, DCH * 512], F32)
        ps_bc = kb.psum("b5pbc", [128, 512])
        ps_y = kb.psum("b5py", [128, 512])
        kb.step(sync=kb.dma("sync", slotc[:, :], w["slotcol4"]))
        kb.step(sync=kb.dma("sync", row[:, :],
                            b_row.ap().rearrange("i p -> (i p)").rearrange("(a b) -> a b", a=1)))
        for j in range(T // 512):
            kb.step(tensor=kb.comp(lambda t, _j=j: t.matmul(
                ps_bc[:, :], ones_r[:, :], row[:, _j * 512:(_j + 1) * 512],
                start=True, stop=True)))
            kb.step(vector=kb.comp(lambda v, _j=j: v.tensor_copy(
                rbc[:, _j * 512:(_j + 1) * 512], ps_bc[:, :])))
        for sp in range(T // 512):
            t0 = sp * 512
            kb.step(sync=kb.dmas([
                (sb3(x1_t[:, :], 512), fm(x1T, t0, 512)),
                (sb3(ln_t[:, :], 512), fm(lnT, t0, 512)),
            ]), vector=kb.comp(lambda v, _sp=sp: [
                v.tensor_scalar(p_sc[:, sc * 512:(sc + 1) * 512],
                                rbc[:, _sp * 512:(_sp + 1) * 512], slotc[:, sc:sc + 1],
                                None, op0=ALU.is_equal)
                for sc in range(4)][-1]))
            for dc in range(DCH):
                def mms(t, _dc=dc):
                    for sc in range(4):
                        last = t.matmul(ps_y[:, :],
                                        delta_tok[:, sc * D + _dc * 128: sc * D + (_dc + 1) * 128],
                                        p_sc[:, sc * 512:(sc + 1) * 512],
                                        start=(sc == 0), stop=(sc == 3))
                    return last
                kb.step(tensor=kb.comp(mms))

                def evs1(v, _dc=dc):
                    sl = slice(_dc * 512, (_dc + 1) * 512)
                    return v.tensor_tensor(x2_t[:, sl], ps_y[:, :], x1_t[:, sl], op=ALU.add)
                def evs2(v, _dc=dc):
                    sl = slice(_dc * 512, (_dc + 1) * 512)
                    return v.tensor_tensor(x2_t[:, sl], x2_t[:, sl], ln_t[:, sl], op=ALU.add)
                kb.step(vector=kb.comp(evs1))
                kb.step(vector=kb.comp(evs2))
            kb.step(sync=kb.dma("sync", fm(x2T, t0, 512), sb3(x2_t[:, :], 512)))

    with kb.scope():
        _b4()



NH1 = 8
DQK = 64
DV1 = 128
CL = 128
NCH = T // CL


def emit_l1_attn(kb, x2T, lnT, x3T, w, scr):
    nc = kb.nc
    qkT_h = scr["c_qkT"]
    sogT_h = scr["c_sogT"]
    ktok_h = scr["c_ktok"]
    vtok_h = scr["c_vtok"]
    if_h = scr["c_if"]
    htok_h = scr["c_htok"]

    ones_c = kb.sbuf("c_1c", [128, 1], F32)
    ones_cb = kb.sbuf("c_1cb", [128, 1], BF16)
    ones_r = kb.sbuf("c_1r", [1, 128], F32)
    id128f = kb.sbuf("c_idf", [128, 128], F32)
    id128b = kb.sbuf("c_idb", [128, 128], BF16)
    kb.step(sync=kb.dma("sync", id128f[:, :], w["ident_f32"]),
            vector=kb.comp(lambda v: (v.memset(ones_c[:], 1.0), v.memset(ones_cb[:], 1.0),
                                      v.memset(ones_r[:], 1.0))[-1]))
    kb.step(vector=kb.comp(lambda v: v.tensor_copy(id128b[:, :], id128f[:, :])))

    def _c1a():
        TQ = 512
        wqk = kb.sbuf("c1wqk", [128, DCH * D], BF16)
        wv = kb.sbuf("c1wv", [128, DCH * D], BF16)
        wog = kb.sbuf("c1wog", [128, DCH * D], BF16)
        ln_t = kb.sbuf("c1ln", [128, DCH * TQ], BF16)
        o_sb2 = [kb.sbuf("c1o%d" % k, [128, TQ], BF16) for k in range(2)]
        kt_sb = kb.sbuf("c1kt", [128, 512], BF16)
        vt_sb = kb.sbuf("c1vt", [128, D], BF16)
        ps_ab = [kb.psum("c1pa%d" % k, [128, TQ]) for k in range(2)]
        ps_b = kb.psum("c1pb", [128, TQ])
        kb.step(gpsimd=kb.dmas([
            (sb3(wqk[:, :], D), w["L1a_wqkT"].rearrange("(c p) e -> p c e", p=128)),
            (sb3(wv[:, :], D), w["L1a_wvT"].rearrange("(c p) e -> p c e", p=128)),
            (sb3(wog[:, :], D), w["L1a_wogT"].rearrange("(c p) e -> p c e", p=128)),
        ]))
        for i in range(T // TQ):
            t0 = i * TQ
            kb.step(sync=kb.dma("sync", sb3(ln_t[:, :], TQ), fm(lnT, t0, TQ)))
            def mmc1(t, _j):
                wsrc, ec = (wqk, _j) if _j < DCH else (wog, _j - DCH)
                for c in range(DCH):
                    last = t.matmul(ps_ab[_j % 2][:, :],
                                    wsrc[:, c * D + ec * 128: c * D + ec * 128 + 128],
                                    ln_t[:, c * TQ:(c + 1) * TQ],
                                    start=(c == 0), stop=(c == DCH - 1))
                return last

            def evc1(s, _j):
                if _j < DCH:
                    return s.activation(o_sb2[_j % 2][:, :], ps_ab[_j % 2][:, :], AF.Copy,
                                        scale=(0.125 if _j < 4 else 1.0))
                return s.activation(o_sb2[_j % 2][:, :], ps_ab[_j % 2][:, :], AF.Sigmoid)

            def stc1(kbx, _j, _t0):
                ec = _j if _j < DCH else _j - DCH
                dst = qkT_h if _j < DCH else sogT_h
                return kb.dma("scalar", dst[ec * 128:(ec + 1) * 128, _t0:_t0 + TQ],
                              o_sb2[_j % 2][:, :])
            for j in range(2 * DCH + 2):
                parts = {}
                if j < 2 * DCH:
                    parts["tensor"] = kb.comp(lambda t, _j=j, _f=mmc1: _f(t, _j))
                if 0 < j <= 2 * DCH:
                    parts["scalar"] = kb.comp(lambda s, _j=j - 1, _f=evc1: _f(s, _j))
                if 1 < j:
                    parts["gpsimd"] = stc1(kb, j - 2, t0)
                kb.step(**parts)
            for sub in range(4):
                def mmkt(t, _sub=sub):
                    for c in range(DCH):
                        last = t.matmul(ps_ab[0][:, :],
                                        ln_t[:, c * TQ + _sub * 128: c * TQ + (_sub + 1) * 128],
                                        wqk[:, c * D + 512: c * D + 1024],
                                        start=(c == 0), stop=(c == DCH - 1))
                    return last
                kb.step(tensor=kb.comp(mmkt))
                kb.step(vector=kb.comp(lambda v: v.tensor_copy(kt_sb[:, :], ps_ab[0][:, :])))
                kb.step(sync=kb.dma("sync", ktok_h[t0 + sub * 128:t0 + (sub + 1) * 128, :], kt_sb[:, :]))
                for half in range(2):
                    def mmvt(t, _sub=sub, _h=half):
                        for c in range(DCH):
                            last = t.matmul(ps_b[:, :],
                                            ln_t[:, c * TQ + _sub * 128: c * TQ + (_sub + 1) * 128],
                                            wv[:, c * D + _h * 512: c * D + (_h + 1) * 512],
                                            start=(c == 0), stop=(c == DCH - 1))
                        return last
                    kb.step(tensor=kb.comp(mmvt))
                    kb.step(vector=kb.comp(lambda v, _h=half: v.tensor_copy(
                        vt_sb[:, _h * 512:(_h + 1) * 512], ps_b[:, :])))
                kb.step(sync=kb.dma("sync", vtok_h[t0 + sub * 128:t0 + (sub + 1) * 128, :], vt_sb[:, :]))

    with kb.scope():
        _c1a()

    def _c1b():
        x_sb = kb.sbuf("cbx", [128, DCH * 128], F32)
        sq_sb = kb.sbuf("cbq", [128, DCH * 128], F32)
        wif = kb.sbuf("cbwif", [128, DCH * 16], F32)
        msum = kb.sbuf("cbms", [128, NCH], F32)
        qsum = kb.sbuf("cbqs", [128, NCH], F32)
        raw = kb.sbuf("cbraw", [128, NCH * 16], F32)
        m_all = kb.sbuf("cbm", [128, NCH], F32)
        nvar = kb.sbuf("cbnv", [128, NCH], F32)
        a_all = kb.sbuf("cba", [128, NCH], F32)
        tmp = kb.sbuf("cbtmp", [128, NCH * 16], F32)
        bif = kb.sbuf("cbb", [128, NCH * 16], F32)
        swbc = kb.sbuf("cbsw", [128, 16], F32)
        swrow = kb.sbuf("cbswr", [1, 16], F32)
        ps_m = kb.psum("cbpm", [128, 1])
        ps_q = kb.psum("cbpq", [128, 1])
        ps_r = kb.psum("cbpr", [128, 16])
        ps_sw = kb.psum("cbpsw", [1, 16])
        ps_swb = kb.psum("cbpswb", [128, 16])
        kb.step(sync=kb.dmas([
            (sb3(wif[:, :], 16), w["L1a_wifT"].rearrange("(c p) e -> p c e", p=128)),
            (bif[:, :], w["L1a_bif"]),
        ]))
        def mmsw(t):
            for c in range(DCH):
                last = t.matmul(ps_sw[:, :], ones_c[:, :], wif[:, c * 16:(c + 1) * 16],
                                start=(c == 0), stop=(c == DCH - 1))
            return last
        kb.step(tensor=kb.comp(mmsw))
        kb.step(vector=kb.comp(lambda v: v.tensor_copy(swrow[:, :], ps_sw[:, :])))
        kb.step(tensor=kb.comp(lambda t: t.matmul(ps_swb[:, :], ones_r[:, :], swrow[:, :],
                                                  start=True, stop=True)))
        kb.step(vector=kb.comp(lambda v: v.tensor_copy(swbc[:, :], ps_swb[:, :])))
        for i in range(NCH):
            kb.step(sync=kb.dma("sync", sb3(x_sb[:, :], 128), fm(x2T, i * 128, 128)))
            kb.step(scalar=kb.comp(lambda s: s.activation(sq_sb[:, :], x_sb[:, :], AF.Square)))

            def mm(t):
                for c in range(DCH):
                    t.matmul(ps_m[:, :], x_sb[:, c * 128:(c + 1) * 128], ones_c[:, :],
                             start=(c == 0), stop=(c == DCH - 1))
                for c in range(DCH):
                    t.matmul(ps_q[:, :], sq_sb[:, c * 128:(c + 1) * 128], ones_c[:, :],
                             start=(c == 0), stop=(c == DCH - 1))
                for c in range(DCH):
                    last = t.matmul(ps_r[:, :], x_sb[:, c * 128:(c + 1) * 128],
                                    wif[:, c * 16:(c + 1) * 16],
                                    start=(c == 0), stop=(c == DCH - 1))
                return last
            kb.step(tensor=kb.comp(mm))

            def collect(v, _i=i):
                v.tensor_copy(msum[:, _i:_i + 1], ps_m[:, :])
                v.tensor_copy(qsum[:, _i:_i + 1], ps_q[:, :])
                return v.tensor_copy(raw[:, _i * 16:(_i + 1) * 16], ps_r[:, :])
            kb.step(vector=kb.comp(collect))

        def vstep(fn):
            kb.step(vector=kb.comp(fn))
        vstep(lambda v: (v.tensor_scalar(m_all[:, :], msum[:, :], 1.0 / D, None, op0=ALU.mult),
                         v.tensor_scalar(qsum[:, :], qsum[:, :], 1.0 / D, None, op0=ALU.mult))[-1])
        vstep(lambda v: v.tensor_tensor(nvar[:, :], m_all[:, :], m_all[:, :], op=ALU.mult))
        vstep(lambda v: v.tensor_tensor(nvar[:, :], nvar[:, :], qsum[:, :], op=ALU.subtract))
        kb.step(scalar=kb.comp(lambda s: s.activation(a_all[:, :], nvar[:, :], AF.Ln,
                                                      scale=-1.0, bias=kb.c(1e-5))))
        kb.step(scalar=kb.comp(lambda s: s.activation(a_all[:, :], a_all[:, :], AF.Exp,
                                                      scale=-0.5)))

        def ptile(fn):
            def run(v):
                for i in range(NCH):
                    last = fn(v, i)
                return last
            vstep(run)
        ptile(lambda v, i: v.tensor_scalar(tmp[:, i * 16:(i + 1) * 16], swbc[:, :],
                                           m_all[:, i:i + 1], None, op0=ALU.mult))
        vstep(lambda v: v.tensor_tensor(tmp[:, :], raw[:, :], tmp[:, :], op=ALU.subtract))
        ptile(lambda v, i: v.tensor_scalar(tmp[:, i * 16:(i + 1) * 16], tmp[:, i * 16:(i + 1) * 16],
                                           a_all[:, i:i + 1], None, op0=ALU.mult))
        vstep(lambda v: v.tensor_tensor(tmp[:, :], tmp[:, :], bif[:, :], op=ALU.add))
        kb.step(scalar=kb.comp(lambda s: s.activation(tmp[:, :], tmp[:, :], AF.Tanh,
                                                      scale=1.0 / 15.0)))
        vstep(lambda v: v.tensor_scalar(tmp[:, :], tmp[:, :], 15.0, None, op0=ALU.mult))
        kb.step(sync=kb.dma("sync",
                            if_h.rearrange("(i p) f -> p i f", p=128),
                            tmp[:, :].rearrange("p (i f) -> p i f", f=16)))

    with kb.scope():
        _c1b()

    def _c2():
        qk_db = [kb.sbuf("c2qk%d" % k, [64, 16 * CL], BF16) for k in range(2)]
        ktk_db = [kb.sbuf("c2kt%d" % k, [128, 512], BF16) for k in range(2)]
        vtk_db = [kb.sbuf("c2vt%d" % k, [128, D], BF16) for k in range(2)]
        ift_db = [kb.sbuf("c2if%d" % k, [128, 16], F32) for k in range(2)]
        floc = kb.sbuf("c2fl", [128, 8], F32)
        g_tok = kb.sbuf("c2g", [128, 8], F32)
        gT = kb.sbuf("c2gT", [8, 128], F32)
        flT = kb.sbuf("c2flT", [8, 128], F32)
        rmA = kb.sbuf("c2rmA", [8, 128], F32)
        rmB = kb.sbuf("c2rmB", [8, 128], F32)
        dT = kb.sbuf("c2dT", [8, 128], F32)
        d_tok = kb.sbuf("c2dt", [128, 8], F32)
        m_tok = kb.sbuf("c2mt", [128, 8], F32)
        em_tok = kb.sbuf("c2em", [128, 8], F32)
        ea_tok = kb.sbuf("c2ea", [128, 8], F32)
        e_tok = kb.sbuf("c2et", [128, 8], F32)
        acol = kb.sbuf("c2ac", [8, 1], F32)
        arow = kb.sbuf("c2ar", [1, 8], F32)
        dlrow = kb.sbuf("c2dlr", [1, 8], F32)
        eacol = kb.sbuf("c2eac", [8, 1], F32)
        earow = kb.sbuf("c2ear", [1, 8], F32)
        ea64 = kb.sbuf("c2ea64", [64, 8], F32)
        t8 = kb.sbuf("c2t8", [8, 1], F32)
        utri = kb.sbuf("c2ut", [128, 128], F32)
        eh8 = kb.sbuf("c2eh", [8, 8 * 128], F32)
        masku = kb.sbuf("c2mk", [128, 8 * CL], F32)
        w_sb = kb.sbuf("c2w", [128, 8 * CL], F32)
        t1_sb = kb.sbuf("c2t1", [128, 8 * CL], F32)
        sw_sb = kb.sbuf("c2sw", [128, 8 * CL], BF16)
        o_sb = kb.sbuf("c2o", [128, 8 * CL], F32)
        num_sb = kb.sbuf("c2num", [128, 8 * CL], F32)
        h_sb = kb.sbuf("c2h", [128, 8 * CL], BF16)
        ke_sb = kb.sbuf("c2ke", [128, 512], BF16)
        qn1 = kb.sbuf("c2qn1", [128, 8], F32)
        qn2 = kb.sbuf("c2qn2", [128, 8], F32)
        dn = kb.sbuf("c2dn", [128, 8], F32)
        rcp = kb.sbuf("c2rcp", [128, 8], F32)
        cn_sb = kb.sbuf("c2cn", [64, 8 * DV1], F32)
        nn_sb = kb.sbuf("c2nn", [64, 8], F32)
        cn_bf = kb.sbuf("c2cnb", [64, 8 * DV1], BF16)
        nn_bf = kb.sbuf("c2nnb", [64, 8], BF16)
        ps_sm = kb.psum("c2psm", [128, 512])
        ps_t8 = ps_sm[0:8, 0:256]
        ps_tr = ps_sm[:, 256:264]
        ps_f = ps_sm[:, 264:272]
        ps_qn = ps_sm[:, 272:288]
        ps_b8 = ps_sm[:, 288:296]
        ps_nn = ps_sm[0:64, 296:304]
        ps_b64 = ps_sm[0:64, 304:312]
        ps_s = kb.psum("c2ps", [128, 8 * CL])
        ps_w = kb.psum("c2pw", [128, 8 * CL])
        ps_o = kb.psum("c2po", [128, 8 * CL])
        ps_cn = ps_o[0:64, :]
        kb.step(sync=kb.dmas([
            (utri[:, :], w["utri128"]),
            (masku[:, :], w["maskU8"]),
            (eh8[:, :], w["eh8"]),
        ]), vector=kb.comp(lambda v: (v.memset(cn_sb[:, :], 0.0), v.memset(nn_sb[:, :], 0.0),
                                      v.memset(cn_bf[:, :], 0.0), v.memset(nn_bf[:, :], 0.0),
                                      v.memset(acol[:, :], 0.0), v.memset(arow[:, :], 0.0))[-1]))

        def vs(fn):
            kb.step(vector=kb.comp(fn))

        def chunk_load(ch):
            t0 = ch * CL
            k = ch % 2
            return kb.dmas([
                (qk_db[k][:, :].rearrange("p (c t) -> p c t", t=CL),
                 qkT_h[:, t0:t0 + CL].rearrange("(c p) t -> p c t", p=64)),
                (ktk_db[k][:, :], ktok_h[t0:t0 + CL, :]),
                (vtk_db[k][:, :], vtok_h[t0:t0 + CL, :]),
                (ift_db[k][:, :], if_h[t0:t0 + CL, :]),
            ])
        kb.step(sync=chunk_load(0))
        for ch in range(NCH):
            t0 = ch * CL
            qk_ch = qk_db[ch % 2]
            ktk = ktk_db[ch % 2]
            vtk = vtk_db[ch % 2]
            if_t = ift_db[ch % 2]
            parts = {"tensor": kb.comp(lambda t, _if=if_t: t.matmul(
                ps_f[:, :], utri[:, :], _if[:, 8:16], start=True, stop=True))}
            if ch + 1 < NCH:
                parts["sync"] = chunk_load(ch + 1)
            kb.step(**parts)
            vs(lambda v: v.tensor_copy(floc[:, :], ps_f[:, :]))
            vs(lambda v, _if=if_t: v.tensor_tensor(g_tok[:, :], _if[:, 0:8], floc[:, :], op=ALU.subtract))
            def tr1(t):
                t.transpose(ps_t8[0:8, 0:128], g_tok[:, :], id128f[:, :])
                return t.transpose(ps_t8[0:8, 128:256], floc[:, :], id128f[:, :])
            kb.step(tensor=kb.comp(tr1))
            vs(lambda v: (v.tensor_copy(gT[:, :], ps_t8[0:8, 0:128]),
                          v.tensor_copy(flT[:, :], ps_t8[0:8, 128:256]))[-1])
            vs(lambda v: v.tensor_copy(rmA[:, :], gT[:, :]))
            cur, nxt = rmA, rmB
            s = 1
            while s < CL:
                def rmstep(v, _c=cur, _n=nxt, _s=s):
                    v.tensor_copy(_n[:, 0:_s], _c[:, 0:_s])
                    return v.tensor_tensor(_n[:, _s:], _c[:, _s:], _c[:, 0:CL - _s], op=ALU.max)
                vs(rmstep)
                cur, nxt = nxt, cur
                s *= 2
            vs(lambda v, _c=cur: v.tensor_scalar(dT[:, :], _c[:, :], acol[:, 0:1], None, op0=ALU.max))
            kb.step(tensor=kb.comp(lambda t: t.transpose(ps_tr[:, :], dT[:, :], id128f[0:8, 0:8])))
            vs(lambda v: v.tensor_copy(d_tok[:, :], ps_tr[:, :]))
            vs(lambda v: v.tensor_tensor(m_tok[:, :], floc[:, :], d_tok[:, :], op=ALU.add))
            kb.step(scalar=kb.comp(lambda s: s.activation(em_tok[:, :], m_tok[:, :], AF.Exp,
                                                          scale=-1.0)))
            kb.step(tensor=kb.comp(lambda t: t.matmul(ps_b8[:, :], ones_r[:, :], arow[:, :],
                                                      start=True, stop=True)))
            vs(lambda v: v.tensor_tensor(ea_tok[:, :], ps_b8[:, :], d_tok[:, :], op=ALU.subtract))
            kb.step(scalar=kb.comp(lambda s: s.activation(ea_tok[:, :], ea_tok[:, :], AF.Exp)))
            def bcD(t):
                for h in range(NH1):
                    last = t.matmul(ps_w[:, h * CL:(h + 1) * CL], eh8[:, h * 128:(h + 1) * 128],
                                    dT[:, :], start=True, stop=True)
                return last
            kb.step(tensor=kb.comp(bcD))

            def wsub(v):
                for h in range(NH1):
                    last = v.tensor_scalar(t1_sb[:, h * CL:(h + 1) * CL],
                                           ps_w[:, h * CL:(h + 1) * CL],
                                           g_tok[:, h:h + 1], None, op0=ALU.subtract)
                return last
            vs(wsub)
            vs(lambda v: v.tensor_tensor(t1_sb[:, :], t1_sb[:, :], masku[:, :], op=ALU.subtract))
            kb.step(scalar=kb.comp(lambda s: s.activation(w_sb[:, :], t1_sb[:, :], AF.Exp,
                                                          scale=-1.0)))
            def mmS(t, _qk=qk_ch):
                for h in range(NH1):
                    last = t.matmul(ps_s[:, h * CL:(h + 1) * CL],
                                    _qk[:, (8 + h) * CL:(9 + h) * CL],
                                    _qk[:, h * CL:(h + 1) * CL],
                                    start=True, stop=True)
                return last
            kb.step(tensor=kb.comp(mmS))
            vs(lambda v: v.tensor_tensor(sw_sb[:, :], ps_s[:, :], w_sb[:, :], op=ALU.mult))

            def mmO(t, _vt=vtk):
                for h in range(NH1):
                    t.matmul(ps_o[:, h * CL:(h + 1) * CL], sw_sb[:, h * CL:(h + 1) * CL],
                             _vt[:, h * DV1:(h + 1) * DV1], start=True, stop=True)
                for h in range(NH1):
                    last = t.matmul(ps_qn[:, h:h + 1], sw_sb[:, h * CL:(h + 1) * CL],
                                    ones_cb[:, :], start=True, stop=True)
                return last
            kb.step(tensor=kb.comp(mmO))

            def mmQC(t, _qk=qk_ch):
                for h in range(NH1):
                    t.matmul(ps_w[:, h * CL:(h + 1) * CL],
                             _qk[:, h * CL:(h + 1) * CL],
                             cn_bf[:, h * DV1:(h + 1) * DV1], start=True, stop=True)
                for h in range(NH1):
                    last = t.matmul(ps_qn[:, 8 + h:9 + h],
                                    _qk[:, h * CL:(h + 1) * CL],
                                    nn_bf[:, h:h + 1], start=True, stop=True)
                return last
            kb.step(tensor=kb.comp(mmQC))
            vs(lambda v: v.tensor_copy(o_sb[:, :], ps_o[:, :]))

            def addinter(v):
                for h in range(NH1):
                    last = v.scalar_tensor_tensor(num_sb[:, h * CL:(h + 1) * CL],
                                                  ps_w[:, h * CL:(h + 1) * CL],
                                                  ea_tok[:, h:h + 1],
                                                  o_sb[:, h * CL:(h + 1) * CL],
                                                  op0=ALU.mult, op1=ALU.add)
                return last
            vs(addinter)
            vs(lambda v: v.tensor_copy(qn1[:, :], ps_qn[:, 0:8]))
            vs(lambda v: v.tensor_tensor(qn2[:, :], ea_tok[:, :], ps_qn[:, 8:16], op=ALU.mult))
            vs(lambda v: v.tensor_tensor(qn1[:, :], qn1[:, :], qn2[:, :], op=ALU.add))
            kb.step(scalar=kb.comp(lambda s: s.activation(qn1[:, :], qn1[:, :], AF.Abs)))
            vs(lambda v: v.tensor_tensor(dn[:, :], qn1[:, :], em_tok[:, :], op=ALU.max))
            vs(lambda v: v.tensor_scalar(dn[:, :], dn[:, :], 1e-6, None, op0=ALU.add))
            vs(lambda v: v.reciprocal(rcp[:, :], dn[:, :]))

            def hdiv(v):
                for h in range(NH1):
                    last = v.tensor_scalar(h_sb[:, h * CL:(h + 1) * CL],
                                           num_sb[:, h * CL:(h + 1) * CL],
                                           rcp[:, h:h + 1], None, op0=ALU.mult)
                return last
            vs(hdiv)
            kb.step(sync=kb.dma("sync", htok_h[t0:t0 + CL, :], h_sb[:, :]))
            kb.step(tensor=kb.comp(lambda t: t.transpose(ps_t8[0:1, 0:8], dT[:, 127:128],
                                                         id128f[0:8, 0:8])))
            vs(lambda v: v.tensor_copy(dlrow[:, :], ps_t8[0:1, 0:8]))
            kb.step(tensor=kb.comp(lambda t: t.matmul(ps_b8[:, :], ones_r[:, :], dlrow[:, :],
                                                      start=True, stop=True)))
            vs(lambda v: v.tensor_tensor(e_tok[:, :], g_tok[:, :], ps_b8[:, :], op=ALU.subtract))
            kb.step(scalar=kb.comp(lambda s: s.activation(e_tok[:, :], e_tok[:, :], AF.Exp)))

            def kemul(v, _kt=ktk):
                for h in range(NH1):
                    last = v.tensor_scalar(ke_sb[:, h * DQK:(h + 1) * DQK],
                                           _kt[:, h * DQK:(h + 1) * DQK],
                                           e_tok[:, h:h + 1], None, op0=ALU.mult)
                return last
            vs(kemul)

            def mmCN(t, _vt=vtk):
                for h in range(NH1):
                    t.matmul(ps_cn[:, h * DV1:(h + 1) * DV1], ke_sb[:, h * DQK:(h + 1) * DQK],
                             _vt[:, h * DV1:(h + 1) * DV1], start=True, stop=True)
                for h in range(NH1):
                    last = t.matmul(ps_nn[:, h:h + 1], ke_sb[:, h * DQK:(h + 1) * DQK],
                                    ones_cb[:, :], start=True, stop=True)
                return last
            kb.step(tensor=kb.comp(mmCN))
            vs(lambda v: v.tensor_tensor(t8[:, :], acol[:, :], dT[:, 127:128], op=ALU.subtract))
            kb.step(scalar=kb.comp(lambda s: s.activation(t8[:, :], t8[:, :], AF.Exp)))
            vs(lambda v: v.tensor_copy(eacol[:, :], t8[:, :]))
            kb.step(tensor=kb.comp(lambda t: t.transpose(ps_t8[0:1, 0:8], eacol[:, :],
                                                         id128f[0:8, 0:8])))
            vs(lambda v: v.tensor_copy(earow[:, :], ps_t8[0:1, 0:8]))
            kb.step(tensor=kb.comp(lambda t: t.matmul(ps_b64[:, :], ones_r[0:1, 0:64],
                                                      earow[:, :], start=True, stop=True)))
            vs(lambda v: v.tensor_copy(ea64[:, :], ps_b64[:, :]))

            def cnup(v):
                for h in range(NH1):
                    last = v.scalar_tensor_tensor(cn_sb[:, h * DV1:(h + 1) * DV1],
                                                  cn_sb[:, h * DV1:(h + 1) * DV1],
                                                  ea64[:, h:h + 1],
                                                  ps_cn[:, h * DV1:(h + 1) * DV1],
                                                  op0=ALU.mult, op1=ALU.add)
                return last
            vs(cnup)

            def nnup(v):
                for h in range(NH1):
                    last = v.scalar_tensor_tensor(nn_sb[:, h:h + 1], nn_sb[:, h:h + 1],
                                                  ea64[:, h:h + 1], ps_nn[:, h:h + 1],
                                                  op0=ALU.mult, op1=ALU.add)
                return last
            vs(nnup)
            vs(lambda v: (v.tensor_copy(cn_bf[:, :], cn_sb[:, :]),
                          v.tensor_copy(nn_bf[:, :], nn_sb[:, :]))[-1])
            vs(lambda v: v.tensor_tensor(acol[:, :], dT[:, 127:128], flT[:, 127:128], op=ALU.add))
            kb.step(tensor=kb.comp(lambda t: t.transpose(ps_t8[0:1, 0:8], acol[:, :],
                                                         id128f[0:8, 0:8])))
            vs(lambda v: v.tensor_copy(arow[:, :], ps_t8[0:1, 0:8]))

    with kb.scope():
        _c2()

    def _c3():
        TQ = 512
        wout = kb.sbuf("c3wo", [128, DCH * D], BF16)
        lnw = kb.sbuf("c3lw", [128, DCH], F32)
        lnb = kb.sbuf("c3lb", [128, DCH], F32)
        h_tok = kb.sbuf("c3ht", [128, D], BF16)
        hT = kb.sbuf("c3hT", [128, DCH * TQ], BF16)
        sq_sb = kb.sbuf("c3sq", [128, DCH * TQ], F32)
        sog = kb.sbuf("c3sog", [128, DCH * TQ], BF16)
        x2_t = kb.sbuf("c3x2", [128, DCH * TQ], F32)
        hout = kb.sbuf("c3ho", [128, DCH * TQ], BF16)
        rows = kb.sbuf("c3rw", [1, 2 * TQ], F32)
        mbc = kb.sbuf("c3mb", [128, TQ], F32)
        abc = kb.sbuf("c3ab", [128, TQ], F32)
        x3_t = kb.sbuf("c3x3", [128, DCH * TQ], F32)
        ps_tx = kb.psum("c3ptx", [128, 512], BF16)
        ps_s = kb.psum("c3psu", [1, TQ])
        ps_q = kb.psum("c3psq", [1, TQ])
        ps_m = kb.psum("c3pm", [128, TQ])
        ps_a = kb.psum("c3pa", [128, TQ])
        ps_yq = [kb.psum("c3py%d" % k, [128, TQ]) for k in range(2)]
        kb.step(gpsimd=kb.dma("gpsimd", sb3(wout[:, :], D),
                              w["L1a_woutT"].rearrange("(c p) e -> p c e", p=128)),
                sync=kb.dmas([(lnw[:, :], w["L1a_lnwc"]), (lnb[:, :], w["L1a_lnbc"])]))
        m_row = rows[0:1, 0:TQ]
        a_row = rows[0:1, TQ:2 * TQ]
        for i in range(T // TQ):
            t0 = i * TQ
            for sub in range(4):
                kb.step(sync=kb.dma("sync", h_tok[:, :], htok_h[t0 + sub * 128:t0 + (sub + 1) * 128, :]))

                def trh1(t, _sub=sub):
                    for c in range(4):
                        last = t.transpose(ps_tx[:, c * 128:(c + 1) * 128],
                                           h_tok[:, c * 128:(c + 1) * 128], id128b[:, :])
                    return last
                def cph1(v, _sub=sub):
                    for c in range(4):
                        last = v.tensor_copy(hT[:, c * TQ + _sub * 128: c * TQ + (_sub + 1) * 128],
                                             ps_tx[:, c * 128:(c + 1) * 128])
                    return last
                def trh2(t, _sub=sub):
                    for c in range(4, DCH):
                        last = t.transpose(ps_tx[:, (c - 4) * 128:(c - 3) * 128],
                                           h_tok[:, c * 128:(c + 1) * 128], id128b[:, :])
                    return last
                def cph2(v, _sub=sub):
                    for c in range(4, DCH):
                        last = v.tensor_copy(hT[:, c * TQ + _sub * 128: c * TQ + (_sub + 1) * 128],
                                             ps_tx[:, (c - 4) * 128:(c - 3) * 128])
                    return last
                kb.step(tensor=kb.comp(trh1))
                kb.step(vector=kb.comp(cph1))
                kb.step(tensor=kb.comp(trh2))
                kb.step(vector=kb.comp(cph2))
            kb.step(sync=kb.dmas([
                (sb3(sog[:, :], TQ), fm(sogT_h, t0, TQ)),
                (sb3(x2_t[:, :], TQ), fm(x2T, t0, TQ)),
            ]), scalar=kb.comp(lambda s: s.activation(sq_sb[:, :], hT[:, :], AF.Square)))

            def mmstat(t):
                for c in range(DCH):
                    t.matmul(ps_s[:, :], ones_cb[:, :], hT[:, c * TQ:(c + 1) * TQ],
                             start=(c == 0), stop=(c == DCH - 1))
                for c in range(DCH):
                    last = t.matmul(ps_q[:, :], ones_c[:, :], sq_sb[:, c * TQ:(c + 1) * TQ],
                                    start=(c == 0), stop=(c == DCH - 1))
                return last
            kb.step(tensor=kb.comp(mmstat))

            kb.step(vector=kb.comp(lambda v: (
                v.tensor_scalar(m_row, ps_s[:, :], 1.0 / D, None, op0=ALU.mult),
                v.tensor_scalar(a_row, ps_q[:, :], 1.0 / D, None, op0=ALU.mult))[-1]))
            kb.step(vector=kb.comp(lambda v: v.tensor_tensor(ps_s[0:1, :], m_row, m_row,
                                                             op=ALU.mult)))
            kb.step(vector=kb.comp(lambda v: v.tensor_tensor(a_row, a_row, ps_s[0:1, :],
                                                             op=ALU.subtract)))
            kb.step(scalar=kb.comp(lambda s: s.activation(a_row, a_row, AF.Ln,
                                                          bias=kb.c(1e-6, p=1))))
            kb.step(scalar=kb.comp(lambda s: s.activation(a_row, a_row, AF.Exp, scale=-0.5)))

            def bc(t):
                t.matmul(ps_m[:, :], ones_r[:, :], m_row, start=True, stop=True)
                return t.matmul(ps_a[:, :], ones_r[:, :], a_row, start=True, stop=True)
            kb.step(tensor=kb.comp(bc))
            kb.step(vector=kb.comp(lambda v: (v.tensor_copy(mbc[:, :], ps_m[:, :]),
                                              v.tensor_copy(abc[:, :], ps_a[:, :]))[-1]))

            def ap1(v):
                for c in range(DCH):
                    sl = slice(c * TQ, (c + 1) * TQ)
                    last = v.tensor_tensor(sq_sb[:, sl], hT[:, sl], mbc[:, :], op=ALU.subtract)
                return last
            def ap2(v):
                for c in range(DCH):
                    sl = slice(c * TQ, (c + 1) * TQ)
                    last = v.tensor_tensor(sq_sb[:, sl], sq_sb[:, sl], abc[:, :], op=ALU.mult)
                return last
            def ap3(v):
                for c in range(DCH):
                    sl = slice(c * TQ, (c + 1) * TQ)
                    last = v.tensor_scalar(sq_sb[:, sl], sq_sb[:, sl], lnw[:, c:c + 1],
                                           None, op0=ALU.mult)
                return last
            def ap4(v):
                for c in range(DCH):
                    sl = slice(c * TQ, (c + 1) * TQ)
                    last = v.scalar_tensor_tensor(hout[:, sl], sq_sb[:, sl], lnb[:, c:c + 1],
                                                  sog[:, sl], op0=ALU.add, op1=ALU.mult)
                return last
            kb.step(vector=kb.comp(ap1))
            kb.step(vector=kb.comp(ap2))
            kb.step(vector=kb.comp(ap3))
            kb.step(vector=kb.comp(ap4))

            def mmy3(t, _fc):
                for vc in range(DCH):
                    last = t.matmul(ps_yq[_fc % 2][:, :],
                                    wout[:, vc * D + _fc * 128: vc * D + _fc * 128 + 128],
                                    hout[:, vc * TQ:(vc + 1) * TQ],
                                    start=(vc == 0), stop=(vc == DCH - 1))
                return last

            def evy3(v, _fc):
                return v.tensor_tensor(x3_t[:, _fc * TQ:(_fc + 1) * TQ], ps_yq[_fc % 2][:, :],
                                       x2_t[:, _fc * TQ:(_fc + 1) * TQ], op=ALU.add)
            for fc in range(DCH + 1):
                parts = {}
                if fc < DCH:
                    parts["tensor"] = kb.comp(lambda t, _x=fc, _f=mmy3: _f(t, _x))
                if fc > 0:
                    parts["vector"] = kb.comp(lambda v, _x=fc - 1, _f=evy3: _f(v, _x))
                kb.step(**parts)
            kb.step(sync=kb.dma("sync", fm(x3T, t0, TQ), sb3(x3_t[:, :], TQ)))

    with kb.scope():
        _c3()



TOWN = 4096
CAP_E = 640
BIGI = 1024.0


def fm_dyn(hbm_ap, idx_sv, tt):
    return hbm_ap[:, bass.ts(idx_sv, tt)].rearrange("(c p) t -> p c t", p=128)


def emit_l1_ffn(kb, x3T, out_tok, w, scr):
    nc = kb.nc
    lnT = scr["d_ln"]
    acc_h = scr["d_acc"]
    fe_h = scr["d_fe"]
    pt_h = scr["d_pt"]
    hf_sb = kb.sbuf("d_hf", [1, 1], I32)
    slot = kb.sbuf("d_slot", [128, TOWN // 128], F32)
    sloti = kb.sbuf("d_sloti", [128, TOWN // 128], I32)
    regs = {}
    kb.step(sync=kb.dma("sync", hf_sb[:, :], w["own_hf"]))

    def loadreg(eng_name):
        def fn(e):
            regs[eng_name] = e.value_load(hf_sb[0:1, 0:1])
            return e.nop()
        return kb.comp(fn)
    kb.step(sync=loadreg("sync"))

    x3own = scr["d_x3own"]
    def cpy(e, semh):
        for q in range(4):
            e.dma_start(out=x3own[:, q * 1024:(q + 1) * 1024],
                        in_=x3T[:, bass.ts(regs["sync"] * 4 + q, 1024)]).then_inc(semh, 16)
    kb.step(sync=("dma", 64, cpy))

    def xsrc(t0, tt):
        return fm(x3own, t0, tt)

    with kb.scope():
        emit_layernorm(kb, xsrc, lnT, "ln3", ntok=TOWN)

    def _d2():
        NT = TOWN // 128
        x_sb = kb.sbuf("d2x", [128, DCH * 128], F32)
        sq_sb = kb.sbuf("d2q", [128, DCH * 128], F32)
        rwT = kb.sbuf("d2rw", [128, DCH * 8], F32)
        ones_c = kb.sbuf("d2o", [128, 1], F32)
        ones_r = kb.sbuf("d2or", [1, 128], F32)
        iota_big = kb.sbuf("d2ib", [128, 8], F32)
        srbc = kb.sbuf("d2sr", [128, 8], F32)
        msum = kb.sbuf("d2ms", [128, NT], F32)
        qsum = kb.sbuf("d2qs", [128, NT], F32)
        raw = kb.sbuf("d2raw", [128, NT * 8], F32)
        tmp = kb.sbuf("d2tmp", [128, NT * 8], F32)
        lall = kb.sbuf("d2l", [128, NT * 8], F32)
        m_all = kb.sbuf("d2m", [128, NT], F32)
        q_all = kb.sbuf("d2qq", [128, NT], F32)
        mm2 = kb.sbuf("d2m2", [128, NT], F32)
        nvar = kb.sbuf("d2nv", [128, NT], F32)
        a_all = kb.sbuf("d2a", [128, NT], F32)
        mx_all = kb.sbuf("d2mx", [128, NT], F32)
        sm_all = kb.sbuf("d2sm", [128, NT], F32)
        rc_all = kb.sbuf("d2rc", [128, NT], F32)
        pm_all = kb.sbuf("d2pm_", [128, NT], F32)
        ptil = kb.sbuf("d2pt", [128, NT], F32)
        band = kb.sbuf("d2bd", [128, NT], F32)
        ecol = kb.sbuf("d2ec", [128, NT], F32)
        fe_sb = kb.sbuf("d2fe", [128, NT], F32)
        pbc = kb.sbuf("d2pb", [128, TOWN], F32)
        prow = kb.sbuf("d2pr", [1, TOWN], F32)
        junk = kb.sbuf("d2jk", [128, TOWN], BF16)
        r1 = kb.sbuf("d2r1", [128, NT], F32)
        r2 = kb.sbuf("d2r2", [128, NT], F32)
        sr_col = kb.sbuf("d2src", [8, 1], F32)
        sr_row = kb.sbuf("d2srw", [1, 8], F32)
        id128 = kb.sbuf("d2id", [128, 128], F32)
        fe_t = kb.sbuf("d2fet", [NT, 128], F32)
        ps_m = kb.psum("d2pm", [128, 1])
        ps_q = kb.psum("d2pq", [128, 1])
        ps_r = kb.psum("d2praw", [128, 8])
        ps_sr = kb.psum("d2psr", [8, 1])
        ps_srr = kb.psum("d2psrr", [1, 8])
        ps_bc = kb.psum("d2pbc", [128, 512])
        ps_fe = kb.psum("d2pfe", [NT, 128])

        kb.step(sync=kb.dmas([
            (sb3(rwT[:, :], 8), w["L1f_rwT"].rearrange("(c p) e -> p c e", p=128)),
            (id128[:, :], w["ident_f32"]),
            (iota_big[:, :], w["iota8_big"]),
        ]), vector=kb.comp(lambda v: (v.memset(ones_c[:], 1.0),
                                      v.memset(ones_r[:], 1.0))[-1]))

        def srmm(t):
            for c in range(DCH):
                last = t.matmul(ps_sr[:, :], rwT[:, c * 8:(c + 1) * 8], ones_c[:, :],
                                start=(c == 0), stop=(c == DCH - 1))
            return last
        kb.step(tensor=kb.comp(srmm))
        kb.step(vector=kb.comp(lambda v: v.tensor_copy(sr_col[:, :], ps_sr[:, :])))
        kb.step(tensor=kb.comp(lambda t: t.transpose(ps_srr[:, :], sr_col[:, :], id128[0:8, 0:8])))
        kb.step(vector=kb.comp(lambda v: v.tensor_copy(sr_row[:, :], ps_srr[:, :])))
        kb.step(tensor=kb.comp(lambda t: t.matmul(ps_r[:, :], ones_r[:, :], sr_row[:, :],
                                                  start=True, stop=True)))
        kb.step(vector=kb.comp(lambda v: v.tensor_copy(srbc[:, :], ps_r[:, :])))

        for i in range(NT):
            kb.step(sync=kb.dma("sync", sb3(x_sb[:, :], 128), xsrc(i * 128, 128)))
            kb.step(scalar=kb.comp(lambda s: s.activation(sq_sb[:, :], x_sb[:, :], AF.Square)))

            def mm(t):
                for c in range(DCH):
                    t.matmul(ps_m[:, :], x_sb[:, c * 128:(c + 1) * 128], ones_c[:, :],
                             start=(c == 0), stop=(c == DCH - 1))
                for c in range(DCH):
                    t.matmul(ps_q[:, :], sq_sb[:, c * 128:(c + 1) * 128], ones_c[:, :],
                             start=(c == 0), stop=(c == DCH - 1))
                for c in range(DCH):
                    last = t.matmul(ps_r[:, :], x_sb[:, c * 128:(c + 1) * 128],
                                    rwT[:, c * 8:(c + 1) * 8],
                                    start=(c == 0), stop=(c == DCH - 1))
                return last
            kb.step(tensor=kb.comp(mm))

            def collect(v, _i=i):
                v.tensor_copy(msum[:, _i:_i + 1], ps_m[:, :])
                v.tensor_copy(qsum[:, _i:_i + 1], ps_q[:, :])
                return v.tensor_copy(raw[:, _i * 8:(_i + 1) * 8], ps_r[:, :])
            kb.step(vector=kb.comp(collect))

        def vstep(fn):
            kb.step(vector=kb.comp(fn))

        def per_tile(fn):
            def run(v):
                for i in range(NT):
                    last = fn(v, i)
                return last
            vstep(run)

        vstep(lambda v: (v.tensor_scalar(m_all[:, :], msum[:, :], 1.0 / D, None, op0=ALU.mult),
                         v.tensor_scalar(q_all[:, :], qsum[:, :], 1.0 / D, None, op0=ALU.mult))[-1])
        vstep(lambda v: v.tensor_tensor(mm2[:, :], m_all[:, :], m_all[:, :], op=ALU.mult))
        vstep(lambda v: v.tensor_tensor(nvar[:, :], mm2[:, :], q_all[:, :], op=ALU.subtract))
        kb.step(scalar=kb.comp(lambda s: s.activation(a_all[:, :], nvar[:, :], AF.Ln,
                                                      scale=-1.0, bias=kb.c(1e-5))))
        kb.step(scalar=kb.comp(lambda s: s.activation(a_all[:, :], a_all[:, :], AF.Exp, scale=-0.5)))
        per_tile(lambda v, i: v.tensor_scalar(tmp[:, i * 8:(i + 1) * 8], srbc[:, :],
                                              m_all[:, i:i + 1], None, op0=ALU.mult))
        vstep(lambda v: v.tensor_tensor(lall[:, :], raw[:, :], tmp[:, :], op=ALU.subtract))
        per_tile(lambda v, i: v.tensor_scalar(lall[:, i * 8:(i + 1) * 8], lall[:, i * 8:(i + 1) * 8],
                                              a_all[:, i:i + 1], None, op0=ALU.mult))
        per_tile(lambda v, i: v.tensor_reduce(mx_all[:, i:i + 1], lall[:, i * 8:(i + 1) * 8],
                                              axis=AXX, op=ALU.max))
        per_tile(lambda v, i: v.tensor_scalar(lall[:, i * 8:(i + 1) * 8], lall[:, i * 8:(i + 1) * 8],
                                              mx_all[:, i:i + 1], None, op0=ALU.subtract))
        kb.step(scalar=kb.comp(lambda s: s.activation(lall[:, :], lall[:, :], AF.Exp)))
        per_tile(lambda v, i: v.tensor_reduce(sm_all[:, i:i + 1], lall[:, i * 8:(i + 1) * 8],
                                              axis=AXX, op=ALU.add))
        vstep(lambda v: v.reciprocal(rc_all[:, :], sm_all[:, :]))
        per_tile(lambda v, i: v.tensor_scalar(lall[:, i * 8:(i + 1) * 8], lall[:, i * 8:(i + 1) * 8],
                                              rc_all[:, i:i + 1], None, op0=ALU.mult))
        per_tile(lambda v, i: v.tensor_reduce(pm_all[:, i:i + 1], lall[:, i * 8:(i + 1) * 8],
                                              axis=AXX, op=ALU.max))
        per_tile(lambda v, i: v.tensor_scalar(tmp[:, i * 8:(i + 1) * 8], lall[:, i * 8:(i + 1) * 8],
                                              pm_all[:, i:i + 1], None, op0=ALU.is_equal))
        vstep(lambda v: v.tensor_scalar(tmp[:, :], tmp[:, :], BIGI, None, op0=ALU.mult))
        per_tile(lambda v, i: v.tensor_tensor(tmp[:, i * 8:(i + 1) * 8], iota_big[:, :],
                                              tmp[:, i * 8:(i + 1) * 8], op=ALU.subtract))
        per_tile(lambda v, i: v.tensor_reduce(ecol[:, i:i + 1], tmp[:, i * 8:(i + 1) * 8],
                                              axis=AXX, op=ALU.min))
        vstep(lambda v: v.tensor_scalar(ptil[:, :], ecol[:, :], 2.0, None, op0=ALU.mult))
        vstep(lambda v: v.tensor_scalar(band[:, :], ptil[:, :], 2.0, None, op0=ALU.add))
        vstep(lambda v: v.tensor_tensor(ptil[:, :], ptil[:, :], pm_all[:, :], op=ALU.add))

        kb.step(sync=kb.dma("sync", pt_h.ap(), ptil[:, :]))
        kb.step(sync=kb.dma("sync", prow[:, :],
                            pt_h.ap().rearrange("p i -> (p i)").rearrange("(a b) -> a b", a=1)))
        for j in range(8):
            kb.step(tensor=kb.comp(lambda t, _j=j: t.matmul(
                ps_bc[:, :], ones_r[:, :], prow[:, _j * 512:(_j + 1) * 512],
                start=True, stop=True)))
            kb.step(vector=kb.comp(lambda v, _j=j: v.tensor_copy(
                pbc[:, _j * 512:(_j + 1) * 512], ps_bc[:, :])))

        for i in range(NT):
            kb.step(vector=kb.comp(lambda v, _i=i: v.tensor_scalar(
                junk[:, :], pbc[:, :], ptil[:, _i:_i + 1], 0.0,
                op0=ALU.is_gt, op1=ALU.add, accum_out=r1[:, _i:_i + 1])))
            kb.step(vector=kb.comp(lambda v, _i=i: v.tensor_scalar(
                junk[:, :], pbc[:, :], band[:, _i:_i + 1], 0.0,
                op0=ALU.is_ge, op1=ALU.add, accum_out=r2[:, _i:_i + 1])))

        vstep(lambda v: v.tensor_tensor(r1[:, :], r1[:, :], r2[:, :], op=ALU.subtract))
        vstep(lambda v: v.tensor_scalar(r1[:, :], r1[:, :], float(CAP_E), None, op0=ALU.is_lt))
        vstep(lambda v: v.tensor_tensor(fe_sb[:, :], ecol[:, :], r1[:, :], op=ALU.mult))
        kb.step(tensor=kb.comp(lambda t: t.transpose(ps_fe[:, :], fe_sb[:, :], id128[:, :])))
        kb.step(vector=kb.comp(lambda v: v.tensor_copy(fe_t[:, :], ps_fe[:, :])))
        kb.step(sync=kb.dma("sync", fe_h.ap().rearrange("(i p) -> i p", p=128), fe_t[:, :]))

    with kb.scope():
        _d2()

    S0C = 1536
    SEG = [0, S0C] + [S0C + CAP_E * e for e in range(1, 8)]
    NSLOT = SEG[8]

    def _d3():
        NT = TOWN // 128
        ln_sort = scr["d_lnsort"]
        idb = kb.sbuf("d3idb", [128, 128], BF16)
        idf = kb.sbuf("d3idf", [128, 128], F32)
        iota8 = kb.sbuf("d3io8", [128, 8], F32)
        utris = kb.sbuf("d3uts", [128, 128], F32)
        eh32 = kb.sbuf("d3eh", [32, 32 * 128], F32)
        seg8 = kb.sbuf("d3sg", [128, 8], F32)
        cap8 = kb.sbuf("d3cp", [128, 8], F32)
        tokid = kb.sbuf("d3tk", [128, NT], I32)
        ones_c = kb.sbuf("d3oc", [128, 1], F32)
        fe2 = kb.sbuf("d3fe", [128, NT], F32)
        fe_rows = kb.sbuf("d3fer", [32, 128], F32)
        lnst = kb.sbuf("d3lns", [128, DCH * 128], BF16)
        lt2 = kb.sbuf("d3lt2", [128, D], BF16)
        oh_sb = kb.sbuf("d3oh", [128, 8], F32)
        t8 = kb.sbuf("d3t8", [128, 8], F32)
        rw = kb.sbuf("d3rw", [128, NT], F32)
        capr = kb.sbuf("d3cr", [128, 1], F32)
        cntT = kb.sbuf("d3cT", [8, 32], F32)
        cnt2 = kb.sbuf("d3c2", [32, 8], F32)
        adjT = kb.sbuf("d3aj", [32, 8], F32)
        zln = kb.sbuf("d3z", [128, D], BF16)
        bigi = kb.sbuf("d3bg", [128, 64], I32)
        ps_tx = kb.psum("d3ptx", [128, 512], BF16)
        ps_sm = kb.psum("d3psm", [128, 128])
        ps_c = ps_sm[0:8, 0:32]
        ps_p = ps_sm[:, 32:40]
        ps_c2 = ps_sm[0:32, 40:48]
        ps_bc = ps_sm[:, 48:56]
        ps_fe2 = ps_sm[:, 56:88]
        kb.step(sync=kb.dmas([
            (idf[:, :], w["ident_f32"]),
            (utris[:, :], w["utris128"]),
            (eh32[:, :], w["eh32"]),
            (seg8[:, :], w["seg8c"]),
            (cap8[:, :], w["cap8c"]),
            (tokid[:, :], w["tokid32i"]),
            (iota8[:, :], w["iota8c"]),
        ]), gpsimd=kb.dma("gpsimd", fe_rows[:, :],
                          fe_h.ap().rearrange("(i p) -> i p", p=128)),
           vector=kb.comp(lambda v: (v.memset(ones_c[:], 1.0), v.memset(zln[:, :], 0.0),
                                     v.memset(bigi[:, :], float(1 << 29)))[-1]))
        kb.step(tensor=kb.comp(lambda t: t.transpose(ps_fe2[:, :], fe_rows[:, :],
                                                     idf[0:32, 0:32])))
        kb.step(vector=kb.comp(lambda v: v.tensor_copy(fe2[:, :], ps_fe2[:, :])))
        kb.step(vector=kb.comp(lambda v: v.tensor_copy(idb[:, :], idf[:, :])))
        for i in range(NSLOT // 128):
            kb.step(gpsimd=kb.dma("gpsimd", ln_sort[i * 128:(i + 1) * 128, :], zln[:, :]))
        for i in range(NT):
            kb.step(vector=kb.comp(lambda v, _i=i: v.tensor_scalar(
                oh_sb[:, :], iota8[:, :], fe2[:, _i:_i + 1], None, op0=ALU.is_equal)))

            def mmc(t, _i=i):
                t.matmul(ps_c[:, _i:_i + 1], oh_sb[:, :], ones_c[:, :], start=True, stop=True)
                return t.matmul(ps_p[:, :], utris[:, :], oh_sb[:, :], start=True, stop=True)
            kb.step(tensor=kb.comp(mmc))
            kb.step(vector=kb.comp(lambda v: v.tensor_tensor(t8[:, :], oh_sb[:, :],
                                                             ps_p[:, :], op=ALU.mult)))
            kb.step(vector=kb.comp(lambda v, _i=i: v.tensor_reduce(
                rw[:, _i:_i + 1], t8[:, :], axis=AXX, op=ALU.add)))
        kb.step(vector=kb.comp(lambda v: v.tensor_copy(cntT[:, :], ps_c[:, :])))
        kb.step(tensor=kb.comp(lambda t: t.transpose(ps_c2[:, :], cntT[:, :], idf[0:8, 0:8])))
        kb.step(vector=kb.comp(lambda v: v.tensor_copy(cnt2[:, :], ps_c2[:, :])))
        kb.step(tensor=kb.comp(lambda t: t.matmul(ps_c2[:, :], utris[0:32, 0:32], cnt2[:, :],
                                                  start=True, stop=True)))
        kb.step(vector=kb.comp(lambda v: v.tensor_tensor(adjT[:, :], ps_c2[:, :],
                                                         seg8[0:32, :], op=ALU.add)))
        for i in range(NT):
            kb.step(tensor=kb.comp(lambda t, _i=i: t.matmul(
                ps_bc[:, :], eh32[:, _i * 128:(_i + 1) * 128], adjT[:, :],
                start=True, stop=True)))
            kb.step(vector=kb.comp(lambda v, _i=i: v.tensor_scalar(
                oh_sb[:, :], iota8[:, :], fe2[:, _i:_i + 1], None, op0=ALU.is_equal)))
            kb.step(vector=kb.comp(lambda v: v.tensor_tensor(t8[:, :], oh_sb[:, :],
                                                             ps_bc[:, :], op=ALU.mult)))
            kb.step(vector=kb.comp(lambda v, _i=i: v.tensor_reduce(
                slot[:, _i:_i + 1], t8[:, :], axis=AXX, op=ALU.add)))
            kb.step(vector=kb.comp(lambda v: v.tensor_tensor(t8[:, :], oh_sb[:, :],
                                                             cap8[:, :], op=ALU.mult)))
            kb.step(vector=kb.comp(lambda v: v.tensor_reduce(capr[:, :], t8[:, :],
                                                             axis=AXX, op=ALU.add)))
            kb.step(vector=kb.comp(lambda v, _i=i: v.tensor_tensor(
                slot[:, _i:_i + 1], slot[:, _i:_i + 1], rw[:, _i:_i + 1], op=ALU.add)))
            kb.step(vector=kb.comp(lambda v, _i=i: v.tensor_scalar(
                capr[:, :], rw[:, _i:_i + 1], capr[:, 0:1], None, op0=ALU.is_ge)))
            kb.step(vector=kb.comp(lambda v: v.tensor_scalar(
                capr[:, :], capr[:, :], float(1 << 29), None, op0=ALU.mult)))
            kb.step(vector=kb.comp(lambda v, _i=i: v.tensor_tensor(
                slot[:, _i:_i + 1], slot[:, _i:_i + 1], capr[:, :], op=ALU.add)))
        kb.step(vector=kb.comp(lambda v: v.tensor_copy(sloti[:, :], slot[:, :])))
        for i in range(NT):
            kb.step(sync=kb.dma("sync", sb3(lnst[:, :], 128), fm(lnT, i * 128, 128)))
            for g in range(2):
                def trg(t, _g=g):
                    for c in range(4):
                        last = t.transpose(ps_tx[:, c * 128:(c + 1) * 128],
                                           lnst[:, (_g * 4 + c) * 128:(_g * 4 + c + 1) * 128],
                                           idb[:, :])
                    return last

                def cpg(v, _g=g):
                    for c in range(4):
                        last = v.tensor_copy(lt2[:, (_g * 4 + c) * 128:(_g * 4 + c + 1) * 128],
                                             ps_tx[:, c * 128:(c + 1) * 128])
                    return last
                kb.step(tensor=kb.comp(trg))
                kb.step(vector=kb.comp(cpg))

            def sc1(e, semh, _i=i):
                e.indirect_dma_start(
                    out=ln_sort, out_offset=IOffs(ap=sloti[:, _i:_i + 1], axis=0),
                    in_=lt2[:, :], in_offset=None,
                    bounds_check=NSLOT - 1, oob_is_err=False).then_inc(semh, 16)
            kb.step(gpsimd=("dma", 16, sc1))

        w1 = kb.sbuf("d3w1", [128, DCH * 4096], BF16)
        w2 = kb.sbuf("d3w2", [128, 32 * D], BF16)
        b1 = kb.sbuf("d3b1", [128, 32], F32)
        b2r = kb.sbuf("d3b2r", [1, D], F32)
        b2bc = kb.sbuf("d3b2b", [128, D], F32)
        lnsp = kb.sbuf("d3lsp", [128, DCH * 512], BF16)
        h_sb = kb.sbuf("d3h", [128, 32 * 512], BF16)
        ytk2 = [kb.sbuf("d3y%d" % k, [128, D], BF16) for k in range(2)]
        ones_r = kb.sbuf("d3or", [1, 128], F32)
        ps_hp = [kb.psum("d3ph%d" % k, [128, 512]) for k in range(2)]
        ps_yp = [kb.psum("d3py%d" % k, [128, 512]) for k in range(2)]
        ps_b2 = kb.psum("d3pb2", [128, 512])
        y_sort = scr["d_ysort"]
        kb.step(vector=kb.comp(lambda v: v.memset(ones_r[:], 1.0)))
        for e in range(8):
            kb.step(gpsimd=kb.dmas([
                (sb3(w1[:, :], 4096), w["L1f_W1"][e].rearrange("(c p) j -> p c j", p=128)),
                (w2[:, :].rearrange("p (c f) -> p c f", f=D),
                 w["L1f_W2"][e].rearrange("(c p) f -> p c f", p=128)),
            ]), sync=kb.dmas([
                (b1[:, :], w["L1f_b1"][e]),
                (b2r[:, :], w["L1f_b2r"][e].rearrange("(a b) -> a b", a=1)),
            ]))
            for half in range(2):
                kb.step(tensor=kb.comp(lambda t, _h=half: t.matmul(
                    ps_b2[:, :], ones_r[:, :], b2r[:, _h * 512:(_h + 1) * 512],
                    start=True, stop=True)))
                kb.step(vector=kb.comp(lambda v, _h=half: v.tensor_copy(
                    b2bc[:, _h * 512:(_h + 1) * 512], ps_b2[:, :])))
            segl = SEG[e + 1] - SEG[e]
            spans = [(SEG[e] + o, min(512, segl - o)) for o in range(0, segl, 512)]
            for (s0, sl) in spans:
                ntile = sl // 128
                for j in range(ntile):
                    kb.step(sync=kb.dma("sync", lt2[:, :],
                                        ln_sort[s0 + j * 128:s0 + (j + 1) * 128, :]))
                    for g in range(2):
                        def trg2(t, _g=g):
                            for c in range(4):
                                last = t.transpose(ps_tx[:, c * 128:(c + 1) * 128],
                                                   lt2[:, (_g * 4 + c) * 128:(_g * 4 + c + 1) * 128],
                                                   idb[:, :])
                            return last

                        def cpg2(v, _g=g, _j=j, _sl=sl):
                            for c in range(4):
                                cc = _g * 4 + c
                                last = v.tensor_copy(
                                    lnsp[:, cc * _sl + _j * 128: cc * _sl + (_j + 1) * 128],
                                    ps_tx[:, c * 128:(c + 1) * 128])
                            return last
                        kb.step(tensor=kb.comp(trg2))
                        kb.step(vector=kb.comp(cpg2))
                def mmh(t, _j, _sl=sl):
                    for c in range(DCH):
                        last = t.matmul(ps_hp[_j % 2][:, 0:_sl],
                                        w1[:, c * 4096 + _j * 128: c * 4096 + _j * 128 + 128],
                                        lnsp[:, c * _sl:(c + 1) * _sl],
                                        start=(c == 0), stop=(c == DCH - 1))
                    return last

                def evh(s, _j, _sl=sl):
                    return s.activation(h_sb[:, _j * _sl:(_j + 1) * _sl], ps_hp[_j % 2][:, 0:_sl],
                                        AF.Relu, bias=b1[:, _j:_j + 1])
                for j in range(33):
                    parts = {}
                    if j < 32:
                        parts["tensor"] = kb.comp(lambda t, _j=j, _f=mmh: _f(t, _j))
                    if j > 0:
                        parts["scalar"] = kb.comp(lambda s, _j=j - 1, _f=evh: _f(s, _j))
                    kb.step(**parts)
                def mmy(t, _q, _sl=sl):
                    _j, _h = _q // 2, _q % 2
                    for jc in range(32):
                        last = t.matmul(ps_yp[_q % 2][:, :],
                                        h_sb[:, jc * _sl + _j * 128: jc * _sl + (_j + 1) * 128],
                                        w2[:, jc * D + _h * 512: jc * D + (_h + 1) * 512],
                                        start=(jc == 0), stop=(jc == 31))
                    return last

                def evy(v, _q):
                    _j, _h = _q // 2, _q % 2
                    return v.tensor_tensor(ytk2[_j % 2][:, _h * 512:(_h + 1) * 512],
                                           ps_yp[_q % 2][:, :],
                                           b2bc[:, _h * 512:(_h + 1) * 512], op=ALU.add)
                nq = 2 * ntile
                for q in range(nq + 2):
                    parts = {}
                    if q < nq:
                        parts["tensor"] = kb.comp(lambda t, _q=q, _f=mmy: _f(t, _q))
                    if 0 < q <= nq:
                        parts["vector"] = kb.comp(lambda v, _q=q - 1, _f=evy: _f(v, _q))
                    if q > 1 and (q - 2) % 2 == 1:
                        _j = (q - 2) // 2
                        parts["gpsimd"] = kb.dma("gpsimd",
                                                 y_sort[s0 + _j * 128:s0 + (_j + 1) * 128, :],
                                                 ytk2[_j % 2][:, :])
                    kb.step(**parts)

    srow_h = scr["d_srow"]

    def _d3u():
        y_sort = scr["d_ysort"]
        idf2 = kb.sbuf("duid", [128, 128], F32)
        slot_row = kb.sbuf("duslr", [1, TOWN], F32)
        slot_bc = kb.sbuf("duslb", [128, TOWN], F32)
        slotc47 = kb.sbuf("dus47", [128, 47], F32)
        bnc = kb.sbuf("dubnc", [32, 128], F32)
        oh_all = kb.sbuf("duoha", [128, 47 * 512], BF16)
        ysb = kb.sbuf("duys", [128, 24 * D], BF16)
        x3t2 = kb.sbuf("dux3", [128, DCH * 512], F32)
        o2 = kb.sbuf("duo2", [128, DCH * 512], F32)
        ones_r = kb.sbuf("duor", [1, 128], F32)
        kb.scopes.append(ExitStack())
        ps_bn2 = kb.psum("dupbn", [32, 128])
        kb.step(sync=kb.dmas([(slotc47[:, :], w["slotcol47"]),
                              (idf2[:, :], w["ident_f32"])]),
                vector=kb.comp(lambda v: v.memset(ones_r[:], 1.0)))
        kb.step(tensor=kb.comp(lambda t: t.transpose(ps_bn2[:, :], slot[:, :], idf2[:, :])))
        kb.step(vector=kb.comp(lambda v: v.tensor_copy(bnc[:, :], ps_bn2[:, :])))
        kb.scopes.pop().close()
        ps_u = [kb.psum("dupu%d" % fc, [128, 512]) for fc in range(8)]
        kb.step(sync=kb.dma("sync", srow_h.ap(), bnc[:, :]))
        kb.step(sync=kb.dma("sync", slot_row[:, :],
                            srow_h.ap().rearrange("i p -> (i p)").rearrange("(a b) -> a b", a=1)))
        for jj in range(TOWN // 512):
            kb.step(tensor=kb.comp(lambda t, _j=jj: t.matmul(
                ps_u[0][:, :], ones_r[:, :], slot_row[:, _j * 512:(_j + 1) * 512],
                start=True, stop=True)))
            kb.step(vector=kb.comp(lambda v, _j=jj: v.tensor_copy(
                slot_bc[:, _j * 512:(_j + 1) * 512], ps_u[0][:, :])))
        SBLK = [(0, 24), (24, 23)]
        for sp in range(TOWN // 512):
            t0 = sp * 512

            def bld(v, _sp=sp):
                for sc in range(47):
                    last = v.tensor_scalar(oh_all[:, sc * 512:(sc + 1) * 512],
                                           slot_bc[:, _sp * 512:(_sp + 1) * 512],
                                           slotc47[:, sc:sc + 1], None, op0=ALU.is_equal)
                return last
            kb.step(vector=kb.comp(bld),
                    gpsimd=kb.dma("gpsimd", sb3(x3t2[:, :], 512), fm(x3own, t0, 512)))
            for (c0, nch) in SBLK:
                kb.step(sync=kb.dma("sync",
                                    ysb[:, :].rearrange("p (c f) -> p c f", f=D)[:, 0:nch, :],
                                    y_sort[c0 * 128:(c0 + nch) * 128, :]
                                    .rearrange("(c p) f -> p c f", p=128)))
                for fc in range(8):
                    def mmu2(t, _fc=fc, _c0=c0, _n=nch):
                        for sc in range(_n):
                            last = t.matmul(ps_u[_fc][:, :],
                                            ysb[:, sc * D + _fc * 128:sc * D + (_fc + 1) * 128],
                                            oh_all[:, (_c0 + sc) * 512:(_c0 + sc + 1) * 512],
                                            start=(_c0 == 0 and sc == 0),
                                            stop=(_c0 == 24 and sc == _n - 1))
                        return last
                    kb.step(tensor=kb.comp(mmu2))

            def ev8(v):
                for fc in range(8):
                    last = v.tensor_tensor(o2[:, fc * 512:(fc + 1) * 512], ps_u[fc][:, :],
                                           x3t2[:, fc * 512:(fc + 1) * 512], op=ALU.add)
                return last
            kb.step(vector=kb.comp(ev8))
            kb.step(sync=kb.dma("sync", fm(out_tok, t0, 512), sb3(o2[:, :], 512)))
    with kb.scope():
        _d3()
    with kb.scope():
        _d3u()



def build_program(stages=("A", "D"), x3_is_input=False, debug=False):
    nc = bass.Bass("TRN2", target_bir_lowering=False, debug=False)
    w = {
        "L0a_wqkvT": nc.dram_tensor("L0a_wqkvT", [D, 3072], F32, kind="ExternalInput").ap(),
        "L0a_woutT": nc.dram_tensor("L0a_woutT", [D, D], F32, kind="ExternalInput").ap(),
        "L0a_bout": nc.dram_tensor("L0a_bout", [128, DCH], F32, kind="ExternalInput").ap(),
        "L1f_rwT": nc.dram_tensor("L1f_rwT", [D, 8], F32, kind="ExternalInput").ap(),
        "L1f_W1": nc.dram_tensor("L1f_W1", [8, D, 4096], F32, kind="ExternalInput").ap(),
        "L1f_W2": nc.dram_tensor("L1f_W2", [8, 4096, D], F32, kind="ExternalInput").ap(),
        "L1f_b1": nc.dram_tensor("L1f_b1", [8, 128, 32], F32, kind="ExternalInput").ap(),
        "L1f_b2": nc.dram_tensor("L1f_b2", [8, 128, 8], F32, kind="ExternalInput").ap(),
        "ident_f32": nc.dram_tensor("ident_f32", [128, 128], F32, kind="ExternalInput").ap(),
        "iota8_big": nc.dram_tensor("iota8_big", [128, 8], F32, kind="ExternalInput").ap(),
        "own_hf": nc.dram_tensor("own_hf", [1, 1], I32, kind="ExternalInput").ap(),
    }
    w.update({
        "L0f_wrc": nc.dram_tensor("L0f_wrc", [128, DCH], F32, kind="ExternalInput").ap(),
        "L0f_wqkvT": nc.dram_tensor("L0f_wqkvT", [D, 3072], F32, kind="ExternalInput").ap(),
        "L0f_woutT": nc.dram_tensor("L0f_woutT", [D, D], F32, kind="ExternalInput").ap(),
        "L0f_g1c": nc.dram_tensor("L0f_g1c", [128, DCH], F32, kind="ExternalInput").ap(),
        "L0f_g2c": nc.dram_tensor("L0f_g2c", [128, DCH], F32, kind="ExternalInput").ap(),
        "L0f_w1T": nc.dram_tensor("L0f_w1T", [D, DFF0], F32, kind="ExternalInput").ap(),
        "L0f_w2T": nc.dram_tensor("L0f_w2T", [D, DFF0], F32, kind="ExternalInput").ap(),
        "L0f_w3T": nc.dram_tensor("L0f_w3T", [DFF0, D], F32, kind="ExternalInput").ap(),
        "iota512": nc.dram_tensor("iota512", [128, NSEL], F32, kind="ExternalInput").ap(),
        "mask512": nc.dram_tensor("mask512", [NSEL, NSEL], F32, kind="ExternalInput").ap(),
        "slotcol4": nc.dram_tensor("slotcol4", [128, 4], F32, kind="ExternalInput").ap(),
        "L1a_wqkT": nc.dram_tensor("L1a_wqkT", [D, D], F32, kind="ExternalInput").ap(),
        "L1a_wvT": nc.dram_tensor("L1a_wvT", [D, D], F32, kind="ExternalInput").ap(),
        "L1a_wogT": nc.dram_tensor("L1a_wogT", [D, D], F32, kind="ExternalInput").ap(),
        "L1a_wifT": nc.dram_tensor("L1a_wifT", [D, 16], F32, kind="ExternalInput").ap(),
        "L1a_bif": nc.dram_tensor("L1a_bif", [128, T // 128 * 16], F32, kind="ExternalInput").ap(),
        "L1a_lnwc": nc.dram_tensor("L1a_lnwc", [128, DCH], F32, kind="ExternalInput").ap(),
        "L1a_lnbc": nc.dram_tensor("L1a_lnbc", [128, DCH], F32, kind="ExternalInput").ap(),
        "L1a_woutT": nc.dram_tensor("L1a_woutT", [D, D], F32, kind="ExternalInput").ap(),
        "utri128": nc.dram_tensor("utri128", [128, 128], F32, kind="ExternalInput").ap(),
        "eh8": nc.dram_tensor("eh8", [8, 8 * 128], F32, kind="ExternalInput").ap(),
        "eh32": nc.dram_tensor("eh32", [32, 32 * 128], F32, kind="ExternalInput").ap(),
        "utris128": nc.dram_tensor("utris128", [128, 128], F32, kind="ExternalInput").ap(),
        "seg8c": nc.dram_tensor("seg8c", [128, 8], F32, kind="ExternalInput").ap(),
        "cap8c": nc.dram_tensor("cap8c", [128, 8], F32, kind="ExternalInput").ap(),
        "tokid32i": nc.dram_tensor("tokid32i", [128, 32], I32, kind="ExternalInput").ap(),
        "slotcol47": nc.dram_tensor("slotcol47", [128, 47], F32, kind="ExternalInput").ap(),
        "iota8c": nc.dram_tensor("iota8c", [128, 8], F32, kind="ExternalInput").ap(),
        "L1f_b2r": nc.dram_tensor("L1f_b2r", [8, D], F32, kind="ExternalInput").ap(),
        "maskU8": nc.dram_tensor("maskU8", [128, 8 * 128], F32, kind="ExternalInput").ap(),
    })
    xT = nc.dram_tensor("xT", [D, T], F32, kind="ExternalInput").ap()
    out_tok = nc.dram_tensor("outT", [D, TOWN], F32, kind="ExternalOutput").ap()
    x1T = nc.dram_tensor("x1T", [D, T], F32,
                         kind="ExternalOutput" if "A" in stages else "Internal").ap()
    x2T = nc.dram_tensor("x2T", [D, T], F32,
                         kind="ExternalOutput" if "B" in stages else "Internal").ap()
    x3T = nc.dram_tensor("x3T", [D, T], F32,
                         kind="ExternalOutput" if "C" in stages else "Internal").ap()
    lnT = nc.dram_tensor("lnT", [D, T], BF16).ap()
    scratch = {
        "k_tok": nc.dram_tensor("k_tok", [T, D], BF16).ap(),
        "v_tok": nc.dram_tensor("v_tok", [T, D], BF16).ap(),
        "ksum": nc.dram_tensor("ksum", [D], F32),
        "d_ln": nc.dram_tensor("d_ln", [D, TOWN], BF16).ap(),
        "d_acc": nc.dram_tensor("d_acc", [D, TOWN], F32).ap(),
        "d_fe": nc.dram_tensor("d_fe", [TOWN], F32,
                               kind="ExternalOutput" if debug else "Internal"),
        "d_pt": nc.dram_tensor("d_pt", [128, TOWN // 128], F32),
        "d_x3own": nc.dram_tensor("d_x3own", [D, TOWN], F32).ap(),
        "d_lnsort": nc.dram_tensor("d_lnsort", [6016, D], BF16).ap(),
        "d_ysort": nc.dram_tensor("d_ysort", [6016, D], BF16).ap(),
        "d_srow": nc.dram_tensor("d_srow", [32, 128], F32),
        "b_lntok": nc.dram_tensor("b_lntok", [T, D], BF16).ap(),
        "b_row": nc.dram_tensor("b_row", [T // 128, 128], F32),
        "c_qkT": nc.dram_tensor("c_qkT", [D, T], BF16).ap(),
        "c_sogT": nc.dram_tensor("c_sogT", [D, T], BF16).ap(),
        "c_ktok": nc.dram_tensor("c_ktok", [T, 512], BF16).ap(),
        "c_vtok": nc.dram_tensor("c_vtok", [T, D], BF16).ap(),
        "c_if": nc.dram_tensor("c_if", [T, 16], F32).ap(),
        "c_htok": nc.dram_tensor("c_htok", [T, D], BF16).ap(),
    }
    if debug:
        scratch["dbg"] = nc.dram_tensor("dbg", [128, 256], F32, kind="ExternalOutput").ap()
    kb = KB(nc)
    for v in (1e-5, 1e-6):
        kb.c(v)
    kb.init_consts_step()
    if "A" in stages:
        with kb.scope():
            emit_layernorm(kb, lambda t0, tt: fm(xT, t0, tt), lnT, "ln0")
        with kb.scope():
            emit_l0_attn(kb, xT, lnT, x1T, w, scratch)
    if "B" in stages:
        xb = xT if "A" not in stages else x1T
        with kb.scope():
            emit_layernorm(kb, lambda t0, tt: fm(xb, t0, tt), lnT, "ln1")
        with kb.scope():
            emit_l0_ffn(kb, xb, lnT, x2T, w, scratch)
    if "C" in stages:
        if "B" in stages:
            xc = x2T
        elif "A" in stages:
            xc = x1T
        else:
            xc = xT
        with kb.scope():
            emit_layernorm(kb, lambda t0, tt: fm(xc, t0, tt), lnT, "ln2")
        with kb.scope():
            emit_l1_attn(kb, xc, lnT, x3T, w, scratch)
    if "D" in stages:
        if x3_is_input:
            x3 = xT
        elif "C" in stages:
            x3 = x3T
        elif "B" in stages:
            x3 = x2T
        else:
            x3 = x1T
        with kb.scope():
            emit_l1_ffn(kb, x3, out_tok, w, scratch)
    kb.final_wait()
    kb.build()
    return nc


def prep_weights(inputs):
    iota8_big = (np.arange(8, dtype=np.float32) + BIGI)[None, :].repeat(128, 0)
    return {
        "L0a_wqkvT": np.ascontiguousarray(inputs["L0_attn_W_qkv"].T),
        "L0a_woutT": np.ascontiguousarray(inputs["L0_attn_W_out"].T),
        "L0a_bout": np.ascontiguousarray(inputs["L0_attn_b_out"].reshape(DCH, 128).T),
        "L1f_rwT": np.ascontiguousarray(inputs["L1_ffn_router_w"].T),
        "L1f_W1": np.ascontiguousarray(inputs["L1_ffn_W1"]),
        "L1f_W2": np.ascontiguousarray(inputs["L1_ffn_W2"]),
        "L1f_b1": np.ascontiguousarray(inputs["L1_ffn_b1"].reshape(8, 32, 128).transpose(0, 2, 1)),
        "L1f_b2": np.ascontiguousarray(inputs["L1_ffn_b2"].reshape(8, 8, 128).transpose(0, 2, 1)),
        "ident_f32": np.eye(128, dtype=np.float32),
        "iota8_big": np.ascontiguousarray(iota8_big),
        "L0f_wrc": np.ascontiguousarray(inputs["L0_ffn_w_router"].reshape(DCH, 128).T),
        "L0f_wqkvT": np.ascontiguousarray(inputs["L0_ffn_W_qkv"].T),
        "L0f_woutT": np.ascontiguousarray(inputs["L0_ffn_W_out"].T),
        "L0f_g1c": np.ascontiguousarray(inputs["L0_ffn_g1"].reshape(DCH, 128).T),
        "L0f_g2c": np.ascontiguousarray(inputs["L0_ffn_g2"].reshape(DCH, 128).T),
        "L0f_w1T": _padT(inputs["L0_ffn_W1"]),
        "L0f_w2T": _padT(inputs["L0_ffn_W2"]),
        "L0f_w3T": _padC(inputs["L0_ffn_W3"]),
        "iota512": np.tile(np.arange(NSEL, dtype=np.float32)[None, :], (128, 1)),
        "mask512": np.where(np.triu(np.ones((NSEL, NSEL), bool), 1), -1e9, 0.0).astype(np.float32),
        "slotcol4": (np.arange(4, dtype=np.float32)[None, :] * 128
                     + np.arange(128, dtype=np.float32)[:, None]),
        "L1a_wqkT": np.ascontiguousarray(np.concatenate(
            [inputs["L1_attn_Wq"].T, inputs["L1_attn_Wk"].T], axis=1)),
        "L1a_wvT": np.ascontiguousarray(inputs["L1_attn_Wv"].T),
        "L1a_wogT": np.ascontiguousarray(inputs["L1_attn_Wog"].T),
        "L1a_wifT": np.ascontiguousarray(np.concatenate(
            [inputs["L1_attn_Wi"].T, inputs["L1_attn_Wf"].T], axis=1)),
        "L1a_bif": np.ascontiguousarray(np.tile(np.concatenate(
            [inputs["L1_attn_bi"], inputs["L1_attn_bf"]]).astype(np.float32),
            (128, T // 128))),
        "L1a_lnwc": np.ascontiguousarray(inputs["L1_attn_ln_w"].reshape(DCH, 128).T),
        "L1a_lnbc": np.ascontiguousarray(inputs["L1_attn_ln_b"].reshape(DCH, 128).T),
        "L1a_woutT": np.ascontiguousarray(inputs["L1_attn_Wout"].T),
        "utri128": np.triu(np.ones((128, 128), np.float32)),
        "eh8": np.ascontiguousarray(np.concatenate(
            [np.where(np.arange(8)[:, None] == h, 1.0, 0.0).astype(np.float32)
             .repeat(128, axis=1).reshape(8, 128) for h in range(8)], axis=1)),
        "eh32": np.ascontiguousarray(np.concatenate(
            [np.where(np.arange(32)[:, None] == i, 1.0, 0.0).astype(np.float32)
             .repeat(128, axis=1).reshape(32, 128) for i in range(32)], axis=1)),
        "utris128": np.triu(np.ones((128, 128), np.float32), 1),
        "seg8c": np.tile(np.array([0, 1536] + [1536 + 640 * e for e in range(1, 7)],
                                  np.float32)[None, :], (128, 1)),
        "cap8c": np.tile(np.array([1536] + [640] * 7, np.float32)[None, :], (128, 1)),
        "tokid32i": (np.arange(128, dtype=np.int32)[:, None]
                     + 128 * np.arange(32, dtype=np.int32)[None, :]),
        "slotcol47": (np.arange(128, dtype=np.float32)[:, None]
                      + 128 * np.arange(47, dtype=np.float32)[None, :]),
        "iota8c": np.tile(np.arange(8, dtype=np.float32)[None, :], (128, 1)),
        "L1f_b2r": np.ascontiguousarray(inputs["L1_ffn_b2"]),
        "maskU8": np.ascontiguousarray(np.tile(
            np.where(np.arange(128)[:, None] <= np.arange(128)[None, :], 0.0, -1e9)
            .astype(np.float32), (1, 8))),
    }


def _padT(wm):
    out = np.zeros((D, DFF0), np.float32)
    out[:, :wm.shape[0]] = wm.T
    return out


def _padC(wm):
    out = np.zeros((DFF0, D), np.float32)
    out[:wm.shape[1], :] = wm.T
    return out


def kernel(**inputs):
    x = inputs["L0_attn_x"]
    nc = build_program(stages=("A", "B", "C", "D"))
    wmaps = prep_weights(inputs)
    in_maps = []
    for c in range(8):
        b, hf = c // 2, c % 2
        m = dict(wmaps)
        m["xT"] = np.ascontiguousarray(x[b].T)
        m["own_hf"] = np.array([[hf]], np.int32)
        in_maps.append(m)
    res = run_bass_kernel_spmd(nc, in_maps, core_ids=list(range(8)))
    out = np.empty((4, T, D), np.float32)
    for c in range(8):
        b, hf = c // 2, c % 2
        sl = slice(hf * TOWN, (hf + 1) * TOWN)
        out[b, sl, :] = res.results[c]["outT"].T
    return out


# version 12 (final)
# speedup vs baseline: 1.1186x; 1.0296x over previous
import sys
import numpy as np
from contextlib import ExitStack, contextmanager

sys.path.insert(0, "/opt/trn_rl_repo")
import concourse.bass as bass
import concourse.mybir as mybir
from concourse.bass_utils import run_bass_kernel_spmd

F32 = mybir.dt.float32
BF16 = mybir.dt.bfloat16
I32 = mybir.dt.int32
AF = mybir.ActivationFunctionType
ALU = mybir.AluOpType
AXX = mybir.AxisListType.X
IOffs = bass.IndirectOffsetOnAxis

ENGINES = ("sync", "scalar", "vector", "tensor", "gpsimd")

D = 1024
DCH = D // 128
T = 8192
B = 4



class KB:

    def __init__(self, nc):
        self.nc = nc
        self.cmds = {e: [] for e in ENGINES}
        self.es = ExitStack()
        self.scopes = [self.es]
        self.bars = {k: self.es.enter_context(nc.semaphore("bar_" + k)) for k in "cdg"}
        self.cnt = {k: 0 for k in "cdg"}
        self.last = {e: {k: 0 for k in "cdg"} for e in ENGINES}
        self._consts = {}
        self._const_sb = self.es.enter_context(nc.sbuf_tensor("kconsts", [128, 16], F32))

    def c(self, value, p=128):
        if value not in self._consts:
            assert len(self._consts) < 16
            self._consts[value] = len(self._consts)
        i = self._consts[value]
        return self._const_sb[0:p, i:i + 1]

    def init_consts_step(self):
        def fn(v):
            last = None
            for val, i in self._consts.items():
                last = v.memset(self._const_sb[:, i:i + 1], float(val))
            return last if last is not None else v.nop()
        self.step(vector=self.comp(fn))

    def sbuf(self, name, shape, dt):
        return self.scopes[-1].enter_context(self.nc.sbuf_tensor(name, shape, dt))

    def psum(self, name, shape, dt=F32):
        return self.scopes[-1].enter_context(self.nc.psum_tensor(name, shape, dt))

    @contextmanager
    def scope(self):
        es = ExitStack()
        self.scopes.append(es)
        try:
            yield
        finally:
            es.close()
            self.scopes.pop()

    def step(self, **parts):
        starts = dict(self.cnt)
        for eng, (kind, incs, fn) in parts.items():
            waits = [(k, starts[k]) for k in "cdg"
                     if starts[k] > self.last[eng][k]]
            semk = "c" if kind == "comp" else ("g" if eng == "gpsimd" else "d")
            def cmd(engobj, _fn=fn, _w=waits, _sh=self.bars[semk]):
                for k, val in _w:
                    engobj.wait_ge(self.bars[k], val)
                _fn(engobj, _sh)
            self.cmds[eng].append(cmd)
            for k, val in waits:
                self.last[eng][k] = val
            self.cnt[semk] += incs

    def final_wait(self):
        tot = dict(self.cnt)
        def cmd(engobj, _t=tot):
            for k in "cdg":
                if _t[k]:
                    engobj.wait_ge(self.bars[k], _t[k])
        self.cmds["sync"].append(cmd)

    @staticmethod
    def _r(ap):
        return ap() if callable(ap) else ap

    def dma(self, eng, out, in_, slow=False):
        def fn(e, semh, _o=out, _i=in_):
            e.dma_start(out=self._r(_o), in_=self._r(_i),
                        allow_slow_non_contiguous=slow).then_inc(semh, 16)
        return ("dma", 16, fn)

    def dmas(self, pairs):
        def fn(e, semh, _p=pairs):
            for _o, _i in _p:
                e.dma_start(out=self._r(_o), in_=self._r(_i)).then_inc(semh, 16)
        return ("dma", 16 * len(pairs), fn)

    def comp(self, fn):
        def wrap(e, semh, _fn=fn):
            last = _fn(e)
            last.then_inc(semh, 1)
        return ("comp", 1, wrap)

    def build(self):
        block = self.es.enter_context(self.nc.Block())
        for e in ENGINES:
            cmds = self.cmds[e]
            def run(engobj, _cmds=cmds):
                for fn in _cmds:
                    fn(engobj)
            getattr(block, e)(run)
        self.es.close()


def fm(hbm_ap, t0, tt):
    return hbm_ap[:, t0:t0 + tt].rearrange("(c p) t -> p c t", p=128)


def sb3(sb_ap, tt):
    return sb_ap.rearrange("p (c t) -> p c t", t=tt)


def fm_rows(hbm_ap, d0, d1, t0, tt):
    return hbm_ap[d0:d1, t0:t0 + tt].rearrange("(c p) t -> p c t", p=128)



def emit_layernorm(kb, src_fn, dst_hbm, nm, ntok=T, eps=1e-5):
    TT = 1024
    x_db = [kb.sbuf(nm + "x%d" % k, [128, DCH * TT], F32) for k in range(2)]
    sq_sb = kb.sbuf(nm + "q", [128, DCH * TT], F32)
    y_sb = kb.sbuf(nm + "y", [128, DCH * TT], BF16)
    st_sb = kb.sbuf(nm + "s", [1, 2 * TT], F32)
    ones_f = kb.sbuf(nm + "1", [128, 1], F32)
    ones_r = kb.sbuf(nm + "r", [1, 128], F32)
    ps_sum = kb.psum(nm + "psu", [1, TT])
    ps_sq = kb.psum(nm + "psq", [1, TT])
    ps_m = kb.psum(nm + "pm", [128, TT])
    ps_a = kb.psum(nm + "pa", [128, TT])

    kb.step(vector=kb.comp(lambda v: (v.memset(ones_f[:], 1.0),
                                      v.memset(ones_r[:], 1.0))[-1]))

    m_row = st_sb[0:1, 0:TT]
    a_row = st_sb[0:1, TT:2 * TT]

    ntile_ln = ntok // TT
    kb.step(sync=kb.dma("sync", sb3(x_db[0][:, :], TT), src_fn(0, TT)))
    for i in range(ntile_ln):
        t0 = i * TT
        x_sb = x_db[i % 2]
        parts = {"scalar": kb.comp(lambda s, _x=x_sb: s.activation(sq_sb[:, :], _x[:, :],
                                                                   AF.Square))}
        if i + 1 < ntile_ln:
            parts["sync"] = kb.dma("sync", sb3(x_db[(i + 1) % 2][:, :], TT),
                                   src_fn((i + 1) * TT, TT))
        kb.step(**parts)

        def sums(t, _x=x_sb):
            for c in range(DCH):
                for hh in range(2):
                    t.matmul(ps_sum[:, hh * 512:(hh + 1) * 512], ones_f[:, :],
                             _x[:, c * TT + hh * 512: c * TT + (hh + 1) * 512],
                             start=(c == 0), stop=(c == DCH - 1))
            for c in range(DCH):
                for hh in range(2):
                    last = t.matmul(ps_sq[:, hh * 512:(hh + 1) * 512], ones_f[:, :],
                                    sq_sb[:, c * TT + hh * 512: c * TT + (hh + 1) * 512],
                                    start=(c == 0), stop=(c == DCH - 1))
            return last
        kb.step(tensor=kb.comp(sums))

        kb.step(vector=kb.comp(lambda v: (
            v.tensor_scalar(m_row, ps_sum[:, :], 1.0 / D, None, op0=ALU.mult),
            v.tensor_scalar(a_row, ps_sq[:, :], 1.0 / D, None, op0=ALU.mult))[-1]))
        kb.step(vector=kb.comp(lambda v: v.tensor_tensor(ps_sum[0:1, :], m_row, m_row,
                                                         op=ALU.mult)))
        kb.step(vector=kb.comp(lambda v: v.tensor_tensor(a_row, a_row, ps_sum[0:1, :],
                                                         op=ALU.subtract)))
        kb.step(scalar=kb.comp(lambda s: s.activation(a_row, a_row, AF.Ln, bias=kb.c(eps, p=1))))
        kb.step(scalar=kb.comp(lambda s: s.activation(a_row, a_row, AF.Exp, scale=-0.5)))

        def bcast(t):
            for hh in range(2):
                t.matmul(ps_m[:, hh * 512:(hh + 1) * 512], ones_r[:, :],
                         m_row[0:1, hh * 512:(hh + 1) * 512], start=True, stop=True)
            for hh in range(2):
                last = t.matmul(ps_a[:, hh * 512:(hh + 1) * 512], ones_r[:, :],
                                a_row[0:1, hh * 512:(hh + 1) * 512], start=True, stop=True)
            return last
        kb.step(tensor=kb.comp(bcast))

        def apply1(v, _x=x_sb):
            for c in range(DCH):
                sl = slice(c * TT, (c + 1) * TT)
                last = v.tensor_tensor(sq_sb[:, sl], _x[:, sl], ps_m[:, :], op=ALU.subtract)
            return last

        def apply2(v):
            for c in range(DCH):
                sl = slice(c * TT, (c + 1) * TT)
                last = v.tensor_tensor(y_sb[:, sl], sq_sb[:, sl], ps_a[:, :], op=ALU.mult)
            return last
        kb.step(vector=kb.comp(apply1))
        kb.step(vector=kb.comp(apply2))
        kb.step(sync=kb.dma("sync", fm(dst_hbm, t0, TT), sb3(y_sb[:, :], TT)))



def emit_l0_attn(kb, xT, lnT, x1T, w, scratch):
    nc = kb.nc
    wqkv = kb.sbuf("a_wqkv", [128, DCH * 3072], BF16)
    wout = kb.sbuf("a_wout", [128, DCH * D], BF16)
    bout = kb.sbuf("a_bout", [128, DCH], F32)
    ones_b = kb.sbuf("a_1b", [128, 1], BF16)
    ones_rb = kb.sbuf("a_1rb", [1, 128], F32)
    kb.step(
        gpsimd=kb.dmas([
            (sb3(wqkv[:, :], 3072), w["L0a_wqkvT"].rearrange("(c p) e -> p c e", p=128)),
            (sb3(wout[:, :], D), w["L0a_woutT"].rearrange("(c p) e -> p c e", p=128)),
        ]),
        scalar=kb.dma("scalar", bout[:, :], w["L0a_bout"]),
        vector=kb.comp(lambda v: (v.memset(ones_b[:], 1.0),
                                  v.memset(ones_rb[:], 1.0))[-1]),
    )
    k_tok, v_tok, ksum_h = scratch["k_tok"], scratch["v_tok"], scratch["ksum"]
    kv_f = kb.sbuf("a3_kv", [128, DCH * D], F32)
    kv_b = kb.sbuf("a3_kvb", [128, DCH * D], BF16)

    a2 = kb.scopes
    kb.scopes.append(ExitStack())
    TTILE = 128
    lt_db2 = [kb.sbuf("a2_lt%d" % k, [128, DCH * TTILE], BF16) for k in range(2)]
    k_sb = kb.sbuf("a2_k", [128, D], BF16)
    v_sb = kb.sbuf("a2_v", [128, D], BF16)
    ps_k = [kb.psum("a2_pk%d" % h, [128, 512]) for h in range(2)]
    ps_v = [kb.psum("a2_pv%d" % h, [128, 512]) for h in range(2)]
    ps_ks = kb.psum("a2_ks", [1, D])
    ks_row = kb.sbuf("a2_ksr", [1, D], F32)
    ntile = T // TTILE
    kb.step(sync=kb.dma("sync", sb3(lt_db2[0][:, :], TTILE), fm(lnT, 0, TTILE)))
    for i in range(ntile):
        t0 = i * TTILE
        lt = lt_db2[i % 2]

        def mmkv(t, _lt=lt):
            for h in range(2):
                for c in range(DCH):
                    t.matmul(ps_k[h][:, :], _lt[:, c * TTILE:(c + 1) * TTILE],
                             wqkv[:, c * 3072 + 1024 + h * 512: c * 3072 + 1024 + (h + 1) * 512],
                             start=(c == 0), stop=(c == DCH - 1))
            for h in range(2):
                for c in range(DCH):
                    last = t.matmul(ps_v[h][:, :], _lt[:, c * TTILE:(c + 1) * TTILE],
                                    wqkv[:, c * 3072 + 2048 + h * 512: c * 3072 + 2048 + (h + 1) * 512],
                                    start=(c == 0), stop=(c == DCH - 1))
            return last
        parts = {"tensor": kb.comp(mmkv)}
        if i + 1 < ntile:
            parts["sync"] = kb.dma("sync", sb3(lt_db2[(i + 1) % 2][:, :], TTILE),
                                   fm(lnT, (i + 1) * TTILE, TTILE))
        kb.step(**parts)

        def evac(s, _i=i):
            s.activation(k_sb[:, 0:512], ps_k[0][:, :], AF.Relu)
            s.activation(k_sb[:, 512:1024], ps_k[1][:, :], AF.Relu)
            s.copy(v_sb[:, 0:512], ps_v[0][:, :])
            return s.copy(v_sb[:, 512:1024], ps_v[1][:, :])
        def keps(v, _i=i):
            return v.tensor_scalar(k_sb[:, :], k_sb[:, :], 1e-6, None, op0=ALU.add)
        kb.step(scalar=kb.comp(evac))
        kb.step(vector=kb.comp(keps))

        def ksmm(t, _i=i):
            t.matmul(ps_ks[0:1, 0:512], ones_b[:, :], k_sb[:, 0:512],
                     start=(_i == 0), stop=(_i == ntile - 1))
            return t.matmul(ps_ks[0:1, 512:1024], ones_b[:, :], k_sb[:, 512:1024],
                            start=(_i == 0), stop=(_i == ntile - 1))
        kb.step(
            tensor=kb.comp(ksmm),
            sync=kb.dma("sync", k_tok[t0:t0 + TTILE, :], k_sb[:, :]),
            gpsimd=kb.dma("gpsimd", v_tok[t0:t0 + TTILE, :], v_sb[:, :]),
        )
    kb.step(vector=kb.comp(lambda v: v.tensor_copy(ks_row[:, :], ps_ks[:, :])))
    kb.step(sync=kb.dma("sync", ksum_h.ap(), ks_row[:, :]))

    kb.scopes.pop().close()
    kb.scopes.append(ExitStack())
    kt_db = [kb.sbuf("a3_kt%d" % k, [128, D], BF16) for k in range(2)]
    vt_db = [kb.sbuf("a3_vt%d" % k, [128, 512], BF16) for k in range(2)]
    ps_kv = [kb.psum("a3_p%d" % c, [128, 512]) for c in range(DCH)]
    for half in range(2):
        e0 = half * 512
        kb.step(sync=kb.dma("sync", kt_db[0][:, :], k_tok[0:TTILE, :]),
                gpsimd=kb.dma("gpsimd", vt_db[0][:, :], v_tok[0:TTILE, e0:e0 + 512]))
        for i in range(ntile):
            kt = kt_db[i % 2]
            vt = vt_db[i % 2]

            def mmkv2(t, _i=i, _kt=kt, _vt=vt):
                for c in range(DCH):
                    last = t.matmul(ps_kv[c][:, :], _kt[:, c * 128:(c + 1) * 128], _vt[:, :],
                                    start=(_i == 0), stop=(_i == ntile - 1))
                return last
            parts = {"tensor": kb.comp(mmkv2)}
            if i + 1 < ntile:
                t1 = (i + 1) * TTILE
                parts["sync"] = kb.dma("sync", kt_db[(i + 1) % 2][:, :],
                                       k_tok[t1:t1 + TTILE, :])
                parts["gpsimd"] = kb.dma("gpsimd", vt_db[(i + 1) % 2][:, :],
                                         v_tok[t1:t1 + TTILE, e0:e0 + 512])
            kb.step(**parts)

        def evkv(v, _e0=e0):
            for c in range(DCH):
                v.tensor_copy(kv_f[:, c * D + _e0: c * D + _e0 + 512], ps_kv[c][:, :])
                last = v.tensor_copy(kv_b[:, c * D + _e0: c * D + _e0 + 512], ps_kv[c][:, :])
            return last
        kb.step(vector=kb.comp(evkv))

    kb.scopes.pop().close()
    kb.scopes.append(ExitStack())
    TQ = 512
    ln_db = [kb.sbuf("a4_ln%d" % k, [128, DCH * TQ], BF16) for k in range(2)]
    x_db = [kb.sbuf("a4_xd%d" % k, [128, DCH * TQ], F32) for k in range(2)]
    q_sb = kb.sbuf("a4_q", [128, DCH * TQ], BF16)
    at_sb = kb.sbuf("a4_at", [128, DCH * TQ], BF16)
    x1_sb = kb.sbuf("a4_x1", [128, DCH * TQ], F32)
    z_row = kb.sbuf("a4_z", [1, TQ], F32)
    zbc_sb = kb.sbuf("a4_zb", [128, TQ], F32)
    ksum_c = kb.sbuf("a4_ks", [128, DCH], BF16)
    ps_qp = [kb.psum("a4_pq%d" % k, [128, TQ]) for k in range(2)]
    ps_z = kb.psum("a4_pz", [1, TQ])
    ps_zb = kb.psum("a4_pzb", [128, TQ])
    ps_ap = [kb.psum("a4_pa%d" % k, [128, TQ]) for k in range(2)]
    ps_yp = [kb.psum("a4_py%d" % k, [128, TQ]) for k in range(2)]
    kb.step(gpsimd=kb.dma("gpsimd", ksum_c[:, :],
                          ksum_h.ap().rearrange("(c p) -> p c", p=128), slow=True))

    NT4 = T // TQ
    kb.step(sync=kb.dma("sync", sb3(ln_db[0][:, :], TQ), fm(lnT, 0, TQ)),
            gpsimd=kb.dma("gpsimd", sb3(x_db[0][:, :], TQ), fm(xT, 0, TQ)))
    for i in range(NT4):
        t0 = i * TQ
        ln_t = ln_db[i % 2]
        x_t = x_db[i % 2]
        def mmq(t, _ec, _ln=ln_t):
            for c in range(DCH):
                last = t.matmul(ps_qp[_ec % 2][:, :],
                                wqkv[:, c * 3072 + _ec * 128: c * 3072 + _ec * 128 + 128],
                                _ln[:, c * TQ:(c + 1) * TQ],
                                start=(c == 0), stop=(c == DCH - 1))
            return last

        def evq(s, _ec):
            return s.activation(q_sb[:, _ec * TQ:(_ec + 1) * TQ], ps_qp[_ec % 2][:, :], AF.Relu)
        for ec in range(DCH + 1):
            parts = {}
            if ec < DCH:
                parts["tensor"] = kb.comp(lambda t, _e=ec, _f=mmq: _f(t, _e))
            if ec > 0:
                parts["scalar"] = kb.comp(lambda s, _e=ec - 1, _f=evq: _f(s, _e))
            if ec == 0 and i + 1 < NT4:
                parts["sync"] = kb.dma("sync", sb3(ln_db[(i + 1) % 2][:, :], TQ),
                                       fm(lnT, (i + 1) * TQ, TQ))
                parts["gpsimd"] = kb.dma("gpsimd", sb3(x_db[(i + 1) % 2][:, :], TQ),
                                         fm(xT, (i + 1) * TQ, TQ))
            kb.step(**parts)
        kb.step(vector=kb.comp(lambda v: v.tensor_scalar(q_sb[:, :], q_sb[:, :], 1e-6, None, op0=ALU.add)))
        def mmz(t):
            for c in range(DCH):
                last = t.matmul(ps_z[:, :], ksum_c[:, c:c + 1], q_sb[:, c * TQ:(c + 1) * TQ],
                                start=(c == 0), stop=(c == DCH - 1))
            return last
        kb.step(tensor=kb.comp(mmz))
        kb.step(vector=kb.comp(lambda v: v.tensor_scalar(z_row[:, :], ps_z[:, :], 1e-6,
                                                         None, op0=ALU.add)))
        kb.step(vector=kb.comp(lambda v: v.reciprocal(z_row[:, :], z_row[:, :])))
        kb.step(tensor=kb.comp(lambda t: t.matmul(ps_zb[:, :], ones_rb[:, :], z_row[:, :],
                                                  start=True, stop=True)))
        kb.step(vector=kb.comp(lambda v: v.tensor_copy(zbc_sb[:, :], ps_zb[:, :])))
        def mma(t, _ec):
            for c in range(DCH):
                last = t.matmul(ps_ap[_ec % 2][:, :],
                                kv_b[:, c * D + _ec * 128: c * D + _ec * 128 + 128],
                                q_sb[:, c * TQ:(c + 1) * TQ],
                                start=(c == 0), stop=(c == DCH - 1))
            return last

        def eva(v, _ec):
            return v.scalar_tensor_tensor(at_sb[:, _ec * TQ:(_ec + 1) * TQ],
                                          ps_ap[_ec % 2][:, :], 1.0, zbc_sb[:, :],
                                          op0=ALU.mult, op1=ALU.mult)
        for ec in range(DCH + 1):
            parts = {}
            if ec < DCH:
                parts["tensor"] = kb.comp(lambda t, _e=ec, _f=mma: _f(t, _e))
            if ec > 0:
                parts["vector"] = kb.comp(lambda v, _e=ec - 1, _f=eva: _f(v, _e))
            kb.step(**parts)
        def mmyy(t, _fc):
            for ec in range(DCH):
                last = t.matmul(ps_yp[_fc % 2][:, :],
                                wout[:, ec * D + _fc * 128: ec * D + _fc * 128 + 128],
                                at_sb[:, ec * TQ:(ec + 1) * TQ],
                                start=(ec == 0), stop=(ec == DCH - 1))
            return last

        def evyy(v, _fc, _x=x_t):
            return v.scalar_tensor_tensor(x1_sb[:, _fc * TQ:(_fc + 1) * TQ],
                                          ps_yp[_fc % 2][:, :], bout[:, _fc:_fc + 1],
                                          _x[:, _fc * TQ:(_fc + 1) * TQ],
                                          op0=ALU.add, op1=ALU.add)
        for fc in range(DCH + 1):
            parts = {}
            if fc < DCH:
                parts["tensor"] = kb.comp(lambda t, _x=fc, _f=mmyy: _f(t, _x))
            if fc > 0:
                parts["vector"] = kb.comp(lambda v, _x=fc - 1, _f=evyy: _f(v, _x))
            kb.step(**parts)
        kb.step(sync=kb.dma("sync", fm(x1T, t0, TQ), sb3(x1_sb[:, :], TQ)))
    kb.scopes.pop().close()



NSEL = 512
DFF0 = 2816


def emit_l0_ffn(kb, x1T, lnT, x2T, w, scr):
    nc = kb.nc
    NT = T // 128
    ln_tok = scr["b_lntok"]
    b_row = scr["b_row"]

    stok = kb.sbuf("b_stok", [128, NT], F32)
    rank_tok = kb.sbuf("b_rank", [128, NT], F32)
    xsel_tok = kb.sbuf("b_xselt", [128, 4 * D], BF16)
    xselT = kb.sbuf("b_xselT", [128, DCH * NSEL], BF16)
    x_attnT = kb.sbuf("b_xat", [128, DCH * NSEL], BF16)
    delta_tok = kb.sbuf("b_dtok", [128, 4 * D], BF16)
    id128b = kb.sbuf("b_idb", [128, 128], BF16)
    id128f = kb.sbuf("b_idf", [128, 128], F32)
    ones_c = kb.sbuf("b_1c", [128, 1], F32)
    ones_r = kb.sbuf("b_1r", [1, 128], F32)
    kb.step(sync=kb.dmas([(id128f[:, :], w["ident_f32"])]),
            vector=kb.comp(lambda v: (v.memset(ones_c[:], 1.0),
                                      v.memset(ones_r[:], 1.0))[-1]))
    ones_cb = kb.sbuf("b_1cb", [128, 1], BF16)
    kb.step(vector=kb.comp(lambda v: (v.tensor_copy(id128b[:, :], id128f[:, :]),
                                      v.memset(ones_cb[:], 1.0))[-1]))

    def _b1():
        x_sb = kb.sbuf("b1x", [128, DCH * 128], F32)
        sq_sb = kb.sbuf("b1q", [128, DCH * 128], F32)
        wrc = kb.sbuf("b1wr", [128, DCH], F32)
        msum = kb.sbuf("b1ms", [128, NT], F32)
        qsum = kb.sbuf("b1qs", [128, NT], F32)
        sraw = kb.sbuf("b1sr", [128, NT], F32)
        m_all = kb.sbuf("b1m", [128, NT], F32)
        nvar = kb.sbuf("b1nv", [128, NT], F32)
        a_all = kb.sbuf("b1a", [128, NT], F32)
        t1 = kb.sbuf("b1t1", [128, NT], F32)
        swrow = kb.sbuf("b1sw", [1, DCH], F32)
        swc = kb.sbuf("b1swc", [128, 1], F32)
        ps_m = kb.psum("b1pm", [128, 1])
        ps_q = kb.psum("b1pq", [128, 1])
        ps_s = kb.psum("b1ps", [128, 1])
        ps_sw = kb.psum("b1psw", [1, DCH])
        ps_swc = kb.psum("b1pswc", [128, 1])
        kb.step(sync=kb.dma("sync", wrc[:, :], w["L0f_wrc"]))
        kb.step(tensor=kb.comp(lambda t: t.matmul(ps_sw[:, :], ones_c[:, :], wrc[:, :],
                                                  start=True, stop=True)))
        kb.step(vector=kb.comp(lambda v: v.tensor_reduce(swrow[0:1, 0:1], ps_sw[:, :],
                                                         axis=AXX, op=ALU.add)))
        kb.step(tensor=kb.comp(lambda t: t.matmul(ps_swc[:, :], ones_r[:, :], swrow[0:1, 0:1],
                                                  start=True, stop=True)))
        kb.step(vector=kb.comp(lambda v: v.tensor_copy(swc[:, :], ps_swc[:, :])))
        for i in range(NT):
            kb.step(sync=kb.dma("sync", sb3(x_sb[:, :], 128), fm(x1T, i * 128, 128)))
            kb.step(scalar=kb.comp(lambda s: s.activation(sq_sb[:, :], x_sb[:, :], AF.Square)))

            def mm(t):
                for c in range(DCH):
                    t.matmul(ps_m[:, :], x_sb[:, c * 128:(c + 1) * 128], ones_c[:, :],
                             start=(c == 0), stop=(c == DCH - 1))
                for c in range(DCH):
                    t.matmul(ps_q[:, :], sq_sb[:, c * 128:(c + 1) * 128], ones_c[:, :],
                             start=(c == 0), stop=(c == DCH - 1))
                for c in range(DCH):
                    last = t.matmul(ps_s[:, :], x_sb[:, c * 128:(c + 1) * 128],
                                    wrc[:, c:c + 1],
                                    start=(c == 0), stop=(c == DCH - 1))
                return last
            kb.step(tensor=kb.comp(mm))

            def collect(v, _i=i):
                v.tensor_copy(msum[:, _i:_i + 1], ps_m[:, :])
                v.tensor_copy(qsum[:, _i:_i + 1], ps_q[:, :])
                return v.tensor_copy(sraw[:, _i:_i + 1], ps_s[:, :])
            kb.step(vector=kb.comp(collect))

        def vstep(fn):
            kb.step(vector=kb.comp(fn))
        vstep(lambda v: (v.tensor_scalar(m_all[:, :], msum[:, :], 1.0 / D, None, op0=ALU.mult),
                         v.tensor_scalar(qsum[:, :], qsum[:, :], 1.0 / D, None, op0=ALU.mult))[-1])
        vstep(lambda v: v.tensor_tensor(nvar[:, :], m_all[:, :], m_all[:, :], op=ALU.mult))
        vstep(lambda v: v.tensor_tensor(nvar[:, :], nvar[:, :], qsum[:, :], op=ALU.subtract))
        kb.step(scalar=kb.comp(lambda s: s.activation(a_all[:, :], nvar[:, :], AF.Ln,
                                                      scale=-1.0, bias=kb.c(1e-5))))
        kb.step(scalar=kb.comp(lambda s: s.activation(a_all[:, :], a_all[:, :], AF.Exp,
                                                      scale=-0.5)))
        vstep(lambda v: v.tensor_scalar(t1[:, :], m_all[:, :], swc[:, 0:1], None, op0=ALU.mult))
        vstep(lambda v: v.tensor_tensor(t1[:, :], sraw[:, :], t1[:, :], op=ALU.subtract))
        vstep(lambda v: v.tensor_tensor(stok[:, :], t1[:, :], a_all[:, :], op=ALU.mult))

    with kb.scope():
        _b1()

    def _b1b():
        lt = kb.sbuf("b1b_lt", [128, DCH * 128], BF16)
        tt_sb = kb.sbuf("b1b_tt", [128, D], BF16)
        ps_t = kb.psum("b1b_pt", [128, D], BF16)
        for i in range(NT):
            kb.step(sync=kb.dma("sync", sb3(lt[:, :], 128), fm(lnT, i * 128, 128)))

            def trs(t):
                for c in range(DCH):
                    last = t.transpose(ps_t[:, c * 128:(c + 1) * 128],
                                       lt[:, c * 128:(c + 1) * 128], id128b[:, :])
                return last
            kb.step(tensor=kb.comp(trs))
            kb.step(vector=kb.comp(lambda v: v.tensor_copy(tt_sb[:, :], ps_t[:, :])))
            kb.step(sync=kb.dma("sync", ln_tok[i * 128:(i + 1) * 128, :], tt_sb[:, :]))

    with kb.scope():
        _b1b()

    def _b2():
        iota512 = kb.sbuf("b2i512", [128, NSEL], F32)
        pt_sb = kb.sbuf("b2pt", [128, NSEL], BF16)
        lt_tok = kb.sbuf("b2lt", [128, D], BF16)
        kb.step(sync=kb.dma("sync", iota512[:, :], w["iota512"]))
        with kb.scope():
            bigbc = kb.sbuf("b2bc", [128, T], F32)
            row = kb.sbuf("b2row", [1, T], F32)
            junk = kb.sbuf("b2jk", [128, T], BF16)
            bounce = kb.sbuf("b2bn", [NT, 128], F32)
            ps_bn = kb.psum("b2pbn", [NT, 128])
            ps_bc = kb.psum("b2pbc", [128, 512])

            def bounce_out(src_sb):
                kb.step(tensor=kb.comp(lambda t: t.transpose(ps_bn[:, :], src_sb, id128f[:, :])))
                kb.step(vector=kb.comp(lambda v: v.tensor_copy(bounce[:, :], ps_bn[:, :])))
                kb.step(sync=kb.dma("sync", b_row.ap(), bounce[:, :]))

            bounce_out(stok[:, :])
            kb.step(sync=kb.dma("sync", row[:, :],
                                b_row.ap().rearrange("i p -> (i p)").rearrange("(a b) -> a b", a=1)))
            for j in range(T // 512):
                kb.step(tensor=kb.comp(lambda t, _j=j: t.matmul(
                    ps_bc[:, :], ones_r[:, :], row[:, _j * 512:(_j + 1) * 512],
                    start=True, stop=True)))
                kb.step(vector=kb.comp(lambda v, _j=j: v.tensor_copy(
                    bigbc[:, _j * 512:(_j + 1) * 512], ps_bc[:, :])))

            for i in range(NT):
                kb.step(vector=kb.comp(lambda v, _i=i: v.tensor_scalar(
                    junk[:, :], bigbc[:, :], stok[:, _i:_i + 1], 0.0,
                    op0=ALU.is_gt, op1=ALU.add, accum_out=rank_tok[:, _i:_i + 1])))
            bounce_out(rank_tok[:, :])

        kb.scopes.append(ExitStack())
        ps_g = [kb.psum("b2pg%d" % sc, [128, D]) for sc in range(4)]
        for i in range(NT):
            kb.step(sync=kb.dma("sync", lt_tok[:, :], ln_tok[i * 128:(i + 1) * 128, :]),
                    vector=kb.comp(lambda v, _i=i: v.tensor_scalar(
                        pt_sb[:, :], iota512[:, :], rank_tok[:, _i:_i + 1], None,
                        op0=ALU.is_equal)))

            def mmg(t, _i=i):
                for sc in range(4):
                    for half in range(2):
                        last = t.matmul(ps_g[sc][:, half * 512:(half + 1) * 512],
                                        pt_sb[:, sc * 128:(sc + 1) * 128],
                                        lt_tok[:, half * 512:(half + 1) * 512],
                                        start=(_i == 0), stop=(_i == NT - 1))
                return last
            kb.step(tensor=kb.comp(mmg))

        def evg(v):
            for sc in range(4):
                last = v.tensor_copy(xsel_tok[:, sc * D:(sc + 1) * D], ps_g[sc][:, :])
            return last
        kb.step(vector=kb.comp(evg))
        kb.scopes.pop().close()
        ps_tx = kb.psum("b2ptx", [128, 512], BF16)
        for dc in range(DCH):
            def trx(t, _dc=dc):
                for sc in range(4):
                    last = t.transpose(ps_tx[:, sc * 128:(sc + 1) * 128],
                                       xsel_tok[:, sc * D + _dc * 128: sc * D + _dc * 128 + 128],
                                       id128b[:, :])
                return last
            kb.step(tensor=kb.comp(trx))
            kb.step(vector=kb.comp(lambda v, _dc=dc: v.tensor_copy(
                xselT[:, _dc * NSEL:(_dc + 1) * NSEL], ps_tx[:, :])))

    with kb.scope():
        _b2()

    def _b3a():
        wqkv = kb.sbuf("b3wq", [128, DCH * 3072], BF16)
        wout = kb.sbuf("b3wo", [128, DCH * D], BF16)
        g1c = kb.sbuf("b3g1", [128, DCH], F32)
        mask = kb.sbuf("b3mk", [128, 4 * NSEL], F32)
        sq_sb = kb.sbuf("b3sq", [128, DCH * NSEL], F32)
        rr_bc = kb.sbuf("b3rr", [128, NSEL], F32)
        rowbuf = kb.sbuf("b3row", [1, NSEL], F32)
        h1T = kb.sbuf("b3h1", [128, DCH * NSEL], BF16)
        qkvT = kb.sbuf("b3qkv", [128, 24 * NSEL], BF16)
        vtok = kb.sbuf("b3vt", [128, 4 * D], BF16)
        s4_sb = kb.sbuf("b3s4", [128, 4 * NSEL], F32)
        cols = kb.sbuf("b3c", [128, 8], F32)
        a4_sb = kb.sbuf("b3a4", [128, 4 * NSEL], BF16)
        at4_sb = kb.sbuf("b3at4", [128, 4 * NSEL], BF16)
        oT = kb.sbuf("b3oT", [128, DCH * NSEL], BF16)
        ps_a = kb.psum("b3pa", [128, NSEL])
        ps_b = kb.psum("b3pb", [128, NSEL])
        ps_s4 = [ps_a, ps_b, kb.psum("b3ps2", [128, NSEL]), kb.psum("b3ps3", [128, NSEL])]
        ps_at2 = kb.psum("b3pat", [128, 2 * NSEL], BF16)
        ps_o = kb.psum("b3po", [128, NSEL])
        kb.step(gpsimd=kb.dmas([
            (sb3(wqkv[:, :], 3072), w["L0f_wqkvT"].rearrange("(c p) e -> p c e", p=128)),
            (sb3(wout[:, :], D), w["L0f_woutT"].rearrange("(c p) e -> p c e", p=128)),
        ]), sync=kb.dmas([
            (g1c[:, :], w["L0f_g1c"]),
            (mask[:, :].rearrange("p (q k) -> p q k", k=NSEL),
             w["mask512"].rearrange("(q p) k -> p q k", p=128)),
        ]))

        def rms_rows(src_T, gcol_sb, out_T, eps):
            kb.step(scalar=kb.comp(lambda s: s.activation(sq_sb[:, :], src_T, AF.Square)))

            def mmq(t):
                for c in range(DCH):
                    last = t.matmul(ps_r[:, :], ones_c[:, :],
                                    sq_sb[:, c * NSEL:(c + 1) * NSEL],
                                    start=(c == 0), stop=(c == DCH - 1))
                return last
            kb.step(tensor=kb.comp(mmq))
            kb.step(scalar=kb.comp(lambda s: s.activation(rowbuf[:, :], ps_r[:, :], AF.Ln,
                                                          scale=1.0 / D, bias=kb.c(eps, p=1))))
            kb.step(scalar=kb.comp(lambda s: s.activation(rowbuf[:, :], rowbuf[:, :], AF.Exp,
                                                          scale=-0.5)))
            kb.step(tensor=kb.comp(lambda t: t.matmul(ps_rr[:, :], ones_r[:, :], rowbuf[:, :],
                                                      start=True, stop=True)))
            kb.step(vector=kb.comp(lambda v: v.tensor_copy(rr_bc[:, :], ps_rr[:, :])))

            def app(v):
                for c in range(DCH):
                    sl = slice(c * NSEL, (c + 1) * NSEL)
                    last = v.scalar_tensor_tensor(out_T[:, sl], src_T[:, sl],
                                                  gcol_sb[:, c:c + 1], rr_bc[:, :],
                                                  op0=ALU.mult, op1=ALU.mult)
                return last
            kb.step(vector=kb.comp(app))

        kb.scopes.append(ExitStack())
        ps_r = kb.psum("b3pr", [1, NSEL])
        ps_rr = kb.psum("b3prr", [128, NSEL])
        rms_rows(xselT[:, :], g1c, h1T, 1e-6)
        kb.scopes.pop().close()

        def mmp(t, _ec):
            for c in range(DCH):
                last = t.matmul(ps_a[:, :] if _ec % 2 == 0 else ps_b[:, :],
                                wqkv[:, c * 3072 + _ec * 128: c * 3072 + _ec * 128 + 128],
                                h1T[:, c * NSEL:(c + 1) * NSEL],
                                start=(c == 0), stop=(c == DCH - 1))
            return last

        def evp(s, _ec):
            return s.activation(qkvT[:, _ec * NSEL:(_ec + 1) * NSEL],
                                ps_a[:, :] if _ec % 2 == 0 else ps_b[:, :],
                                AF.Copy, scale=(0.125 if _ec < 8 else 1.0))
        for ec in range(25):
            parts = {}
            if ec < 24:
                parts["tensor"] = kb.comp(lambda t, _e=ec, _f=mmp: _f(t, _e))
            if ec > 0:
                parts["scalar"] = kb.comp(lambda s, _e=ec - 1, _f=evp: _f(s, _e))
            kb.step(**parts)

        for sc in range(4):
            for half in range(2):
                def mmv(t, _sc=sc, _h=half):
                    for c in range(DCH):
                        last = t.matmul(ps_a[:, :],
                                        h1T[:, c * NSEL + _sc * 128: c * NSEL + _sc * 128 + 128],
                                        wqkv[:, c * 3072 + 2048 + _h * 512: c * 3072 + 2048 + (_h + 1) * 512],
                                        start=(c == 0), stop=(c == DCH - 1))
                    return last
                kb.step(tensor=kb.comp(mmv))
                kb.step(vector=kb.comp(lambda v, _sc=sc, _h=half: v.tensor_copy(
                    vtok[:, _sc * D + _h * 512: _sc * D + (_h + 1) * 512], ps_a[:, :])))

        for h in range(16):
            ec_q = h // 2
            ec_k = 8 + h // 2
            pb = 64 * (h % 2)
            k_ap = qkvT[pb:pb + 64, ec_k * NSEL:(ec_k + 1) * NSEL]

            def mmS4(t, _h=h, _pb=pb, _eq=ec_q, _k=k_ap):
                for qc in range(4):
                    q_ap = qkvT[_pb:_pb + 64, _eq * NSEL + qc * 128: _eq * NSEL + qc * 128 + 128]
                    last = t.matmul(ps_s4[qc][:, :], q_ap, _k, start=True, stop=True)
                return last
            kb.step(tensor=kb.comp(mmS4))
            kb.step(vector=kb.comp(lambda v: [
                v.tensor_tensor(s4_sb[:, qc * NSEL:(qc + 1) * NSEL], ps_s4[qc][:, :],
                                mask[:, qc * NSEL:(qc + 1) * NSEL], op=ALU.add)
                for qc in range(4)][-1]))
            kb.step(vector=kb.comp(lambda v: [
                v.tensor_reduce(cols[:, qc:qc + 1], s4_sb[:, qc * NSEL:(qc + 1) * NSEL],
                                axis=AXX, op=ALU.max) for qc in range(4)][-1]))
            kb.step(vector=kb.comp(lambda v: [
                v.tensor_scalar(s4_sb[:, qc * NSEL:(qc + 1) * NSEL],
                                s4_sb[:, qc * NSEL:(qc + 1) * NSEL],
                                cols[:, qc:qc + 1], None, op0=ALU.subtract)
                for qc in range(4)][-1]))
            kb.step(scalar=kb.comp(lambda s: s.activation(s4_sb[:, :], s4_sb[:, :], AF.Exp)))
            kb.step(vector=kb.comp(lambda v: [
                v.tensor_reduce(cols[:, 4 + qc:5 + qc], s4_sb[:, qc * NSEL:(qc + 1) * NSEL],
                                axis=AXX, op=ALU.add) for qc in range(4)][-1]))
            kb.step(vector=kb.comp(lambda v: v.reciprocal(cols[:, 4:8], cols[:, 4:8])))
            kb.step(vector=kb.comp(lambda v: [
                v.tensor_scalar(a4_sb[:, qc * NSEL:(qc + 1) * NSEL],
                                s4_sb[:, qc * NSEL:(qc + 1) * NSEL],
                                cols[:, 4 + qc:5 + qc], None, op0=ALU.mult)
                for qc in range(4)][-1]))

            for r in range(2):
                def tra4(t, _r=r):
                    for qq in range(2):
                        qc = 2 * _r + qq
                        for kc in range(4):
                            last = t.transpose(
                                ps_at2[:, qq * NSEL + kc * 128: qq * NSEL + kc * 128 + 128],
                                a4_sb[:, qc * NSEL + kc * 128: qc * NSEL + kc * 128 + 128],
                                id128b[:, :])
                    return last
                kb.step(tensor=kb.comp(tra4))
                kb.step(vector=kb.comp(lambda v, _r=r: v.tensor_copy(
                    at4_sb[:, _r * 2 * NSEL:(_r + 1) * 2 * NSEL], ps_at2[:, :])))

            def mmo4(t, _h=h, _pb=pb):
                for qc in range(4):
                    for kc in range(4):
                        last = t.matmul(ps_o[_pb:_pb + 64, qc * 128:(qc + 1) * 128],
                                        vtok[:, kc * D + _h * 64: kc * D + _h * 64 + 64],
                                        at4_sb[:, qc * NSEL + kc * 128: qc * NSEL + kc * 128 + 128],
                                        start=(kc == 0), stop=(kc == 3))
                return last
            kb.step(tensor=kb.comp(mmo4))
            if h % 2 == 1:
                hp = h // 2
                kb.step(vector=kb.comp(lambda v, _hp=hp: v.tensor_copy(
                    oT[:, _hp * NSEL:(_hp + 1) * NSEL], ps_o[:, :])))

        for fc in range(DCH):
            def mmw(t, _fc=fc):
                for ec in range(DCH):
                    last = t.matmul(ps_a[:, :],
                                    wout[:, ec * D + _fc * 128: ec * D + _fc * 128 + 128],
                                    oT[:, ec * NSEL:(ec + 1) * NSEL],
                                    start=(ec == 0), stop=(ec == DCH - 1))
                return last
            kb.step(tensor=kb.comp(mmw))
            kb.step(vector=kb.comp(lambda v, _fc=fc: v.tensor_copy(
                x_attnT[:, _fc * NSEL:(_fc + 1) * NSEL], ps_a[:, :])))

    with kb.scope():
        _b3a()

    def _b3b():
        w1T = kb.sbuf("b4w1", [128, DCH * DFF0], BF16)
        w2T = kb.sbuf("b4w2", [128, DCH * DFF0], BF16)
        w3fc2 = [kb.sbuf("b4w3%d" % k, [128, 22 * 128], BF16) for k in range(2)]
        g2c = kb.sbuf("b4g2", [128, DCH], F32)
        s2T = kb.sbuf("b4s2", [128, DCH * NSEL], BF16)
        sq_sb = kb.sbuf("b4sq", [128, DCH * NSEL], BF16)
        rr_bc = kb.sbuf("b4rr", [128, NSEL], F32)
        rowbuf = kb.sbuf("b4row", [1, NSEL], F32)
        h2T = kb.sbuf("b4h2", [128, DCH * NSEL], BF16)
        uT = kb.sbuf("b4u", [128, 22 * NSEL], BF16)
        tsi2 = [kb.sbuf("b4si%d" % k, [128, NSEL], F32) for k in range(2)]
        dT = kb.sbuf("b4dT", [128, DCH * NSEL], BF16)
        ps_r = kb.psum("b4pr", [1, NSEL])
        ps_rr = kb.psum("b4prr", [128, NSEL])
        ps_1p = [kb.psum("b4p1%d" % k, [128, NSEL]) for k in range(2)]
        ps_2p = [kb.psum("b4p2%d" % k, [128, NSEL]) for k in range(3)]
        ps_tx = kb.psum("b4ptx", [128, 512], BF16)
        kb.step(gpsimd=kb.dmas([
            (sb3(w1T[:, :], DFF0), w["L0f_w1T"].rearrange("(c p) e -> p c e", p=128)),
            (sb3(w2T[:, :], DFF0), w["L0f_w2T"].rearrange("(c p) e -> p c e", p=128)),
        ]), sync=kb.dma("sync", g2c[:, :], w["L0f_g2c"]))

        kb.step(vector=kb.comp(lambda v: [
            v.tensor_tensor(s2T[:, c * NSEL:(c + 1) * NSEL],
                            xselT[:, c * NSEL:(c + 1) * NSEL],
                            x_attnT[:, c * NSEL:(c + 1) * NSEL], op=ALU.add)
            for c in range(DCH)][-1]))

        def rms_rows2(src_T, gcol_sb, out_T, eps):
            kb.step(scalar=kb.comp(lambda s: s.activation(sq_sb[:, :], src_T, AF.Square)))

            def mmq(t):
                for c in range(DCH):
                    last = t.matmul(ps_r[:, :], ones_cb[:, :],
                                    sq_sb[:, c * NSEL:(c + 1) * NSEL],
                                    start=(c == 0), stop=(c == DCH - 1))
                return last
            kb.step(tensor=kb.comp(mmq))
            kb.step(scalar=kb.comp(lambda s: s.activation(rowbuf[:, :], ps_r[:, :], AF.Ln,
                                                          scale=1.0 / D, bias=kb.c(eps, p=1))))
            kb.step(scalar=kb.comp(lambda s: s.activation(rowbuf[:, :], rowbuf[:, :], AF.Exp,
                                                          scale=-0.5)))
            kb.step(tensor=kb.comp(lambda t: t.matmul(ps_rr[:, :], ones_r[:, :], rowbuf[:, :],
                                                      start=True, stop=True)))
            kb.step(vector=kb.comp(lambda v: v.tensor_copy(rr_bc[:, :], ps_rr[:, :])))

            def app(v):
                for c in range(DCH):
                    sl = slice(c * NSEL, (c + 1) * NSEL)
                    last = v.scalar_tensor_tensor(out_T[:, sl], src_T[:, sl],
                                                  gcol_sb[:, c:c + 1], rr_bc[:, :],
                                                  op0=ALU.mult, op1=ALU.mult)
                return last
            kb.step(vector=kb.comp(app))

        rms_rows2(s2T[:, :], g2c, h2T, 1e-6)

        def mmu(t, _jc):
            for c in range(DCH):
                t.matmul(ps_1p[_jc % 2][:, :],
                         w1T[:, c * DFF0 + _jc * 128: c * DFF0 + _jc * 128 + 128],
                         h2T[:, c * NSEL:(c + 1) * NSEL],
                         start=(c == 0), stop=(c == DCH - 1))
            for c in range(DCH):
                last = t.matmul(ps_2p[_jc % 3][:, :],
                                w2T[:, c * DFF0 + _jc * 128: c * DFF0 + _jc * 128 + 128],
                                h2T[:, c * NSEL:(c + 1) * NSEL],
                                start=(c == 0), stop=(c == DCH - 1))
            return last

        def evsi(s, _jc):
            return s.activation(tsi2[_jc % 2][:, :], ps_1p[_jc % 2][:, :], AF.Silu)

        def evmu(v, _jc):
            return v.tensor_tensor(uT[:, _jc * NSEL:(_jc + 1) * NSEL], tsi2[_jc % 2][:, :],
                                   ps_2p[_jc % 3][:, :], op=ALU.mult)
        for jc in range(24):
            parts = {}
            if jc < 22:
                parts["tensor"] = kb.comp(lambda t, _j=jc, _f=mmu: _f(t, _j))
            if 0 < jc <= 22:
                parts["scalar"] = kb.comp(lambda s, _j=jc - 1, _f=evsi: _f(s, _j))
            if jc > 1:
                parts["vector"] = kb.comp(lambda v, _j=jc - 2, _f=evmu: _f(v, _j))
            kb.step(**parts)

        for fc in range(DCH):
            kb.step(gpsimd=kb.dma("gpsimd",
                                  w3fc2[fc % 2][:, :].rearrange("p (c f) -> p c f", f=128),
                                  w["L0f_w3T"][:, fc * 128:(fc + 1) * 128]
                                  .rearrange("(c p) f -> p c f", p=128)))

            def mmf(t, _fc=fc):
                for hc in range(22):
                    last = t.matmul(ps_1p[_fc % 2][:, :],
                                    w3fc2[_fc % 2][:, :][:, hc * 128:(hc + 1) * 128],
                                    uT[:, hc * NSEL:(hc + 1) * NSEL],
                                    start=(hc == 0), stop=(hc == 21))
                return last
            kb.step(tensor=kb.comp(mmf))
            kb.step(vector=kb.comp(lambda v, _fc=fc: v.tensor_tensor(
                dT[:, _fc * NSEL:(_fc + 1) * NSEL], ps_1p[_fc % 2][:, :],
                x_attnT[:, _fc * NSEL:(_fc + 1) * NSEL], op=ALU.add)))

        for sc in range(4):
            for dc in range(DCH):
                kb.step(tensor=kb.comp(lambda t, _sc=sc, _dc=dc: t.transpose(
                    ps_tx[:, (_dc % 4) * 128:((_dc % 4) + 1) * 128],
                    dT[:, _dc * NSEL + _sc * 128: _dc * NSEL + _sc * 128 + 128],
                    id128b[:, :])))
                kb.step(vector=kb.comp(lambda v, _sc=sc, _dc=dc: v.tensor_copy(
                    delta_tok[:, _sc * D + _dc * 128: _sc * D + (_dc + 1) * 128],
                    ps_tx[:, (_dc % 4) * 128:((_dc % 4) + 1) * 128])))

    with kb.scope():
        _b3b()

    def _b4():
        rbc = kb.sbuf("b5rbc", [128, T], F32)
        row = kb.sbuf("b5row", [1, T], F32)
        slotc = kb.sbuf("b5slc", [128, 4], F32)
        p_sc = kb.sbuf("b5p", [128, 4 * 512], BF16)
        x1_t = kb.sbuf("b5x1", [128, DCH * 512], F32)
        ln_t = kb.sbuf("b5ln", [128, DCH * 512], BF16)
        x2_t = kb.sbuf("b5x2", [128, DCH * 512], F32)
        ps_bc = kb.psum("b5pbc", [128, 512])
        ps_y = kb.psum("b5py", [128, 512])
        kb.step(sync=kb.dma("sync", slotc[:, :], w["slotcol4"]))
        kb.step(sync=kb.dma("sync", row[:, :],
                            b_row.ap().rearrange("i p -> (i p)").rearrange("(a b) -> a b", a=1)))
        for j in range(T // 512):
            kb.step(tensor=kb.comp(lambda t, _j=j: t.matmul(
                ps_bc[:, :], ones_r[:, :], row[:, _j * 512:(_j + 1) * 512],
                start=True, stop=True)))
            kb.step(vector=kb.comp(lambda v, _j=j: v.tensor_copy(
                rbc[:, _j * 512:(_j + 1) * 512], ps_bc[:, :])))
        for sp in range(T // 512):
            t0 = sp * 512
            kb.step(sync=kb.dmas([
                (sb3(x1_t[:, :], 512), fm(x1T, t0, 512)),
                (sb3(ln_t[:, :], 512), fm(lnT, t0, 512)),
            ]), vector=kb.comp(lambda v, _sp=sp: [
                v.tensor_scalar(p_sc[:, sc * 512:(sc + 1) * 512],
                                rbc[:, _sp * 512:(_sp + 1) * 512], slotc[:, sc:sc + 1],
                                None, op0=ALU.is_equal)
                for sc in range(4)][-1]))
            for dc in range(DCH):
                def mms(t, _dc=dc):
                    for sc in range(4):
                        last = t.matmul(ps_y[:, :],
                                        delta_tok[:, sc * D + _dc * 128: sc * D + (_dc + 1) * 128],
                                        p_sc[:, sc * 512:(sc + 1) * 512],
                                        start=(sc == 0), stop=(sc == 3))
                    return last
                kb.step(tensor=kb.comp(mms))

                def evs1(v, _dc=dc):
                    sl = slice(_dc * 512, (_dc + 1) * 512)
                    return v.tensor_tensor(x2_t[:, sl], ps_y[:, :], x1_t[:, sl], op=ALU.add)
                def evs2(v, _dc=dc):
                    sl = slice(_dc * 512, (_dc + 1) * 512)
                    return v.tensor_tensor(x2_t[:, sl], x2_t[:, sl], ln_t[:, sl], op=ALU.add)
                kb.step(vector=kb.comp(evs1))
                kb.step(vector=kb.comp(evs2))
            kb.step(sync=kb.dma("sync", fm(x2T, t0, 512), sb3(x2_t[:, :], 512)))

    with kb.scope():
        _b4()



NH1 = 8
DQK = 64
DV1 = 128
CL = 128
NCH = T // CL


def emit_l1_attn(kb, x2T, lnT, x3T, w, scr):
    nc = kb.nc
    qkT_h = scr["c_qkT"]
    sogT_h = scr["c_sogT"]
    ktok_h = scr["c_ktok"]
    vtok_h = scr["c_vtok"]
    if_h = scr["c_if"]
    htok_h = scr["c_htok"]

    ones_c = kb.sbuf("c_1c", [128, 1], F32)
    ones_cb = kb.sbuf("c_1cb", [128, 1], BF16)
    ones_r = kb.sbuf("c_1r", [1, 128], F32)
    id128f = kb.sbuf("c_idf", [128, 128], F32)
    id128b = kb.sbuf("c_idb", [128, 128], BF16)
    kb.step(sync=kb.dma("sync", id128f[:, :], w["ident_f32"]),
            vector=kb.comp(lambda v: (v.memset(ones_c[:], 1.0), v.memset(ones_cb[:], 1.0),
                                      v.memset(ones_r[:], 1.0))[-1]))
    kb.step(vector=kb.comp(lambda v: v.tensor_copy(id128b[:, :], id128f[:, :])))

    def _c1a():
        TQ = 512
        wqk = kb.sbuf("c1wqk", [128, DCH * D], BF16)
        wv = kb.sbuf("c1wv", [128, DCH * D], BF16)
        wog = kb.sbuf("c1wog", [128, DCH * D], BF16)
        ln_t = kb.sbuf("c1ln", [128, DCH * TQ], BF16)
        o_sb2 = [kb.sbuf("c1o%d" % k, [128, TQ], BF16) for k in range(2)]
        kt_sb = kb.sbuf("c1kt", [128, 512], BF16)
        vt_sb = kb.sbuf("c1vt", [128, D], BF16)
        ps_ab = [kb.psum("c1pa%d" % k, [128, TQ]) for k in range(2)]
        ps_b = kb.psum("c1pb", [128, TQ])
        kb.step(gpsimd=kb.dmas([
            (sb3(wqk[:, :], D), w["L1a_wqkT"].rearrange("(c p) e -> p c e", p=128)),
            (sb3(wv[:, :], D), w["L1a_wvT"].rearrange("(c p) e -> p c e", p=128)),
            (sb3(wog[:, :], D), w["L1a_wogT"].rearrange("(c p) e -> p c e", p=128)),
        ]))
        for i in range(T // TQ):
            t0 = i * TQ
            kb.step(sync=kb.dma("sync", sb3(ln_t[:, :], TQ), fm(lnT, t0, TQ)))
            def mmc1(t, _j):
                wsrc, ec = (wqk, _j) if _j < DCH else (wog, _j - DCH)
                for c in range(DCH):
                    last = t.matmul(ps_ab[_j % 2][:, :],
                                    wsrc[:, c * D + ec * 128: c * D + ec * 128 + 128],
                                    ln_t[:, c * TQ:(c + 1) * TQ],
                                    start=(c == 0), stop=(c == DCH - 1))
                return last

            def evc1(s, _j):
                if _j < DCH:
                    return s.activation(o_sb2[_j % 2][:, :], ps_ab[_j % 2][:, :], AF.Copy,
                                        scale=(0.125 if _j < 4 else 1.0))
                return s.activation(o_sb2[_j % 2][:, :], ps_ab[_j % 2][:, :], AF.Sigmoid)

            def stc1(kbx, _j, _t0):
                ec = _j if _j < DCH else _j - DCH
                dst = qkT_h if _j < DCH else sogT_h
                return kb.dma("scalar", dst[ec * 128:(ec + 1) * 128, _t0:_t0 + TQ],
                              o_sb2[_j % 2][:, :])
            for j in range(2 * DCH + 2):
                parts = {}
                if j < 2 * DCH:
                    parts["tensor"] = kb.comp(lambda t, _j=j, _f=mmc1: _f(t, _j))
                if 0 < j <= 2 * DCH:
                    parts["scalar"] = kb.comp(lambda s, _j=j - 1, _f=evc1: _f(s, _j))
                if 1 < j:
                    parts["gpsimd"] = stc1(kb, j - 2, t0)
                kb.step(**parts)
            for sub in range(4):
                def mmkt(t, _sub=sub):
                    for c in range(DCH):
                        last = t.matmul(ps_ab[0][:, :],
                                        ln_t[:, c * TQ + _sub * 128: c * TQ + (_sub + 1) * 128],
                                        wqk[:, c * D + 512: c * D + 1024],
                                        start=(c == 0), stop=(c == DCH - 1))
                    return last
                kb.step(tensor=kb.comp(mmkt))
                kb.step(vector=kb.comp(lambda v: v.tensor_copy(kt_sb[:, :], ps_ab[0][:, :])))
                kb.step(sync=kb.dma("sync", ktok_h[t0 + sub * 128:t0 + (sub + 1) * 128, :], kt_sb[:, :]))
                for half in range(2):
                    def mmvt(t, _sub=sub, _h=half):
                        for c in range(DCH):
                            last = t.matmul(ps_b[:, :],
                                            ln_t[:, c * TQ + _sub * 128: c * TQ + (_sub + 1) * 128],
                                            wv[:, c * D + _h * 512: c * D + (_h + 1) * 512],
                                            start=(c == 0), stop=(c == DCH - 1))
                        return last
                    kb.step(tensor=kb.comp(mmvt))
                    kb.step(vector=kb.comp(lambda v, _h=half: v.tensor_copy(
                        vt_sb[:, _h * 512:(_h + 1) * 512], ps_b[:, :])))
                kb.step(sync=kb.dma("sync", vtok_h[t0 + sub * 128:t0 + (sub + 1) * 128, :], vt_sb[:, :]))

    with kb.scope():
        _c1a()

    def _c1b():
        x_sb = kb.sbuf("cbx", [128, DCH * 128], F32)
        sq_sb = kb.sbuf("cbq", [128, DCH * 128], F32)
        wif = kb.sbuf("cbwif", [128, DCH * 16], F32)
        msum = kb.sbuf("cbms", [128, NCH], F32)
        qsum = kb.sbuf("cbqs", [128, NCH], F32)
        raw = kb.sbuf("cbraw", [128, NCH * 16], F32)
        m_all = kb.sbuf("cbm", [128, NCH], F32)
        nvar = kb.sbuf("cbnv", [128, NCH], F32)
        a_all = kb.sbuf("cba", [128, NCH], F32)
        tmp = kb.sbuf("cbtmp", [128, NCH * 16], F32)
        bif = kb.sbuf("cbb", [128, NCH * 16], F32)
        swbc = kb.sbuf("cbsw", [128, 16], F32)
        swrow = kb.sbuf("cbswr", [1, 16], F32)
        ps_m = kb.psum("cbpm", [128, 1])
        ps_q = kb.psum("cbpq", [128, 1])
        ps_r = kb.psum("cbpr", [128, 16])
        ps_sw = kb.psum("cbpsw", [1, 16])
        ps_swb = kb.psum("cbpswb", [128, 16])
        kb.step(sync=kb.dmas([
            (sb3(wif[:, :], 16), w["L1a_wifT"].rearrange("(c p) e -> p c e", p=128)),
            (bif[:, :], w["L1a_bif"]),
        ]))
        def mmsw(t):
            for c in range(DCH):
                last = t.matmul(ps_sw[:, :], ones_c[:, :], wif[:, c * 16:(c + 1) * 16],
                                start=(c == 0), stop=(c == DCH - 1))
            return last
        kb.step(tensor=kb.comp(mmsw))
        kb.step(vector=kb.comp(lambda v: v.tensor_copy(swrow[:, :], ps_sw[:, :])))
        kb.step(tensor=kb.comp(lambda t: t.matmul(ps_swb[:, :], ones_r[:, :], swrow[:, :],
                                                  start=True, stop=True)))
        kb.step(vector=kb.comp(lambda v: v.tensor_copy(swbc[:, :], ps_swb[:, :])))
        for i in range(NCH):
            kb.step(sync=kb.dma("sync", sb3(x_sb[:, :], 128), fm(x2T, i * 128, 128)))
            kb.step(scalar=kb.comp(lambda s: s.activation(sq_sb[:, :], x_sb[:, :], AF.Square)))

            def mm(t):
                for c in range(DCH):
                    t.matmul(ps_m[:, :], x_sb[:, c * 128:(c + 1) * 128], ones_c[:, :],
                             start=(c == 0), stop=(c == DCH - 1))
                for c in range(DCH):
                    t.matmul(ps_q[:, :], sq_sb[:, c * 128:(c + 1) * 128], ones_c[:, :],
                             start=(c == 0), stop=(c == DCH - 1))
                for c in range(DCH):
                    last = t.matmul(ps_r[:, :], x_sb[:, c * 128:(c + 1) * 128],
                                    wif[:, c * 16:(c + 1) * 16],
                                    start=(c == 0), stop=(c == DCH - 1))
                return last
            kb.step(tensor=kb.comp(mm))

            def collect(v, _i=i):
                v.tensor_copy(msum[:, _i:_i + 1], ps_m[:, :])
                v.tensor_copy(qsum[:, _i:_i + 1], ps_q[:, :])
                return v.tensor_copy(raw[:, _i * 16:(_i + 1) * 16], ps_r[:, :])
            kb.step(vector=kb.comp(collect))

        def vstep(fn):
            kb.step(vector=kb.comp(fn))
        vstep(lambda v: (v.tensor_scalar(m_all[:, :], msum[:, :], 1.0 / D, None, op0=ALU.mult),
                         v.tensor_scalar(qsum[:, :], qsum[:, :], 1.0 / D, None, op0=ALU.mult))[-1])
        vstep(lambda v: v.tensor_tensor(nvar[:, :], m_all[:, :], m_all[:, :], op=ALU.mult))
        vstep(lambda v: v.tensor_tensor(nvar[:, :], nvar[:, :], qsum[:, :], op=ALU.subtract))
        kb.step(scalar=kb.comp(lambda s: s.activation(a_all[:, :], nvar[:, :], AF.Ln,
                                                      scale=-1.0, bias=kb.c(1e-5))))
        kb.step(scalar=kb.comp(lambda s: s.activation(a_all[:, :], a_all[:, :], AF.Exp,
                                                      scale=-0.5)))

        def ptile(fn):
            def run(v):
                for i in range(NCH):
                    last = fn(v, i)
                return last
            vstep(run)
        ptile(lambda v, i: v.tensor_scalar(tmp[:, i * 16:(i + 1) * 16], swbc[:, :],
                                           m_all[:, i:i + 1], None, op0=ALU.mult))
        vstep(lambda v: v.tensor_tensor(tmp[:, :], raw[:, :], tmp[:, :], op=ALU.subtract))
        ptile(lambda v, i: v.tensor_scalar(tmp[:, i * 16:(i + 1) * 16], tmp[:, i * 16:(i + 1) * 16],
                                           a_all[:, i:i + 1], None, op0=ALU.mult))
        vstep(lambda v: v.tensor_tensor(tmp[:, :], tmp[:, :], bif[:, :], op=ALU.add))
        kb.step(scalar=kb.comp(lambda s: s.activation(tmp[:, :], tmp[:, :], AF.Tanh,
                                                      scale=1.0 / 15.0)))
        vstep(lambda v: v.tensor_scalar(tmp[:, :], tmp[:, :], 15.0, None, op0=ALU.mult))
        kb.step(sync=kb.dma("sync",
                            if_h.rearrange("(i p) f -> p i f", p=128),
                            tmp[:, :].rearrange("p (i f) -> p i f", f=16)))

    with kb.scope():
        _c1b()

    def _c2():
        qk_db = [kb.sbuf("c2qk%d" % k, [64, 16 * CL], BF16) for k in range(2)]
        ktk_db = [kb.sbuf("c2kt%d" % k, [128, 512], BF16) for k in range(2)]
        vtk_db = [kb.sbuf("c2vt%d" % k, [128, D], BF16) for k in range(2)]
        ift_db = [kb.sbuf("c2if%d" % k, [128, 16], F32) for k in range(2)]
        floc = kb.sbuf("c2fl", [128, 8], F32)
        g_tok = kb.sbuf("c2g", [128, 8], F32)
        gT = kb.sbuf("c2gT", [8, 128], F32)
        flT = kb.sbuf("c2flT", [8, 128], F32)
        rmA = kb.sbuf("c2rmA", [8, 128], F32)
        rmB = kb.sbuf("c2rmB", [8, 128], F32)
        dT = kb.sbuf("c2dT", [8, 128], F32)
        d_tok = kb.sbuf("c2dt", [128, 8], F32)
        m_tok = kb.sbuf("c2mt", [128, 8], F32)
        em_tok = kb.sbuf("c2em", [128, 8], F32)
        ea_tok = kb.sbuf("c2ea", [128, 8], F32)
        e_tok = kb.sbuf("c2et", [128, 8], F32)
        acol = kb.sbuf("c2ac", [8, 1], F32)
        arow = kb.sbuf("c2ar", [1, 8], F32)
        dlrow = kb.sbuf("c2dlr", [1, 8], F32)
        eacol = kb.sbuf("c2eac", [8, 1], F32)
        earow = kb.sbuf("c2ear", [1, 8], F32)
        ea64 = kb.sbuf("c2ea64", [64, 8], F32)
        t8 = kb.sbuf("c2t8", [8, 1], F32)
        utri = kb.sbuf("c2ut", [128, 128], F32)
        eh8 = kb.sbuf("c2eh", [8, 8 * 128], F32)
        masku = kb.sbuf("c2mk", [128, 8 * CL], F32)
        w_sb = kb.sbuf("c2w", [128, 8 * CL], F32)
        t1_sb = kb.sbuf("c2t1", [128, 8 * CL], F32)
        sw_sb = kb.sbuf("c2sw", [128, 8 * CL], BF16)
        o_sb = kb.sbuf("c2o", [128, 8 * CL], F32)
        num_sb = kb.sbuf("c2num", [128, 8 * CL], F32)
        h_sb = kb.sbuf("c2h", [128, 8 * CL], BF16)
        ke_sb = kb.sbuf("c2ke", [128, 512], BF16)
        qn1 = kb.sbuf("c2qn1", [128, 8], F32)
        qn2 = kb.sbuf("c2qn2", [128, 8], F32)
        dn = kb.sbuf("c2dn", [128, 8], F32)
        rcp = kb.sbuf("c2rcp", [128, 8], F32)
        cn_sb = kb.sbuf("c2cn", [64, 8 * DV1], F32)
        nn_sb = kb.sbuf("c2nn", [64, 8], F32)
        cn_bf = kb.sbuf("c2cnb", [64, 8 * DV1], BF16)
        nn_bf = kb.sbuf("c2nnb", [64, 8], BF16)
        ps_sm = kb.psum("c2psm", [128, 512])
        ps_t8 = ps_sm[0:8, 0:256]
        ps_tr = ps_sm[:, 256:264]
        ps_f = ps_sm[:, 264:272]
        ps_qn = ps_sm[:, 272:288]
        ps_b8 = ps_sm[:, 288:296]
        ps_nn = ps_sm[0:64, 296:304]
        ps_b64 = ps_sm[0:64, 304:312]
        ps_s = kb.psum("c2ps", [128, 8 * CL])
        ps_w = kb.psum("c2pw", [128, 8 * CL])
        ps_o = kb.psum("c2po", [128, 8 * CL])
        ps_cn = ps_o[0:64, :]
        kb.step(sync=kb.dmas([
            (utri[:, :], w["utri128"]),
            (masku[:, :], w["maskU8"]),
            (eh8[:, :], w["eh8"]),
        ]), vector=kb.comp(lambda v: (v.memset(cn_sb[:, :], 0.0), v.memset(nn_sb[:, :], 0.0),
                                      v.memset(cn_bf[:, :], 0.0), v.memset(nn_bf[:, :], 0.0),
                                      v.memset(acol[:, :], 0.0), v.memset(arow[:, :], 0.0))[-1]))

        def vs(fn):
            kb.step(vector=kb.comp(fn))

        def chunk_load(ch):
            t0 = ch * CL
            k = ch % 2
            return kb.dmas([
                (qk_db[k][:, :].rearrange("p (c t) -> p c t", t=CL),
                 qkT_h[:, t0:t0 + CL].rearrange("(c p) t -> p c t", p=64)),
                (ktk_db[k][:, :], ktok_h[t0:t0 + CL, :]),
                (vtk_db[k][:, :], vtok_h[t0:t0 + CL, :]),
                (ift_db[k][:, :], if_h[t0:t0 + CL, :]),
            ])
        kb.step(sync=chunk_load(0))
        for ch in range(NCH):
            t0 = ch * CL
            qk_ch = qk_db[ch % 2]
            ktk = ktk_db[ch % 2]
            vtk = vtk_db[ch % 2]
            if_t = ift_db[ch % 2]
            parts = {"tensor": kb.comp(lambda t, _if=if_t: t.matmul(
                ps_f[:, :], utri[:, :], _if[:, 8:16], start=True, stop=True))}
            if ch + 1 < NCH:
                parts["sync"] = chunk_load(ch + 1)
            kb.step(**parts)
            vs(lambda v: v.tensor_copy(floc[:, :], ps_f[:, :]))
            vs(lambda v, _if=if_t: v.tensor_tensor(g_tok[:, :], _if[:, 0:8], floc[:, :], op=ALU.subtract))
            def tr1(t):
                t.transpose(ps_t8[0:8, 0:128], g_tok[:, :], id128f[:, :])
                return t.transpose(ps_t8[0:8, 128:256], floc[:, :], id128f[:, :])
            kb.step(tensor=kb.comp(tr1))
            vs(lambda v: (v.tensor_copy(gT[:, :], ps_t8[0:8, 0:128]),
                          v.tensor_copy(flT[:, :], ps_t8[0:8, 128:256]))[-1])
            vs(lambda v: v.tensor_copy(rmA[:, :], gT[:, :]))
            cur, nxt = rmA, rmB
            s = 1
            while s < CL:
                def rmstep(v, _c=cur, _n=nxt, _s=s):
                    v.tensor_copy(_n[:, 0:_s], _c[:, 0:_s])
                    return v.tensor_tensor(_n[:, _s:], _c[:, _s:], _c[:, 0:CL - _s], op=ALU.max)
                vs(rmstep)
                cur, nxt = nxt, cur
                s *= 2
            vs(lambda v, _c=cur: v.tensor_scalar(dT[:, :], _c[:, :], acol[:, 0:1], None, op0=ALU.max))
            kb.step(tensor=kb.comp(lambda t: t.transpose(ps_tr[:, :], dT[:, :], id128f[0:8, 0:8])))
            vs(lambda v: v.tensor_copy(d_tok[:, :], ps_tr[:, :]))
            vs(lambda v: v.tensor_tensor(m_tok[:, :], floc[:, :], d_tok[:, :], op=ALU.add))
            kb.step(scalar=kb.comp(lambda s: s.activation(em_tok[:, :], m_tok[:, :], AF.Exp,
                                                          scale=-1.0)))
            kb.step(tensor=kb.comp(lambda t: t.matmul(ps_b8[:, :], ones_r[:, :], arow[:, :],
                                                      start=True, stop=True)))
            vs(lambda v: v.tensor_tensor(ea_tok[:, :], ps_b8[:, :], d_tok[:, :], op=ALU.subtract))
            kb.step(scalar=kb.comp(lambda s: s.activation(ea_tok[:, :], ea_tok[:, :], AF.Exp)))
            def bcD(t):
                for h in range(NH1):
                    last = t.matmul(ps_w[:, h * CL:(h + 1) * CL], eh8[:, h * 128:(h + 1) * 128],
                                    dT[:, :], start=True, stop=True)
                return last
            kb.step(tensor=kb.comp(bcD))

            def wsub(v):
                for h in range(NH1):
                    last = v.tensor_scalar(t1_sb[:, h * CL:(h + 1) * CL],
                                           ps_w[:, h * CL:(h + 1) * CL],
                                           g_tok[:, h:h + 1], None, op0=ALU.subtract)
                return last
            vs(wsub)
            vs(lambda v: v.tensor_tensor(t1_sb[:, :], t1_sb[:, :], masku[:, :], op=ALU.subtract))
            kb.step(scalar=kb.comp(lambda s: s.activation(w_sb[:, :], t1_sb[:, :], AF.Exp,
                                                          scale=-1.0)))
            def mmS(t, _qk=qk_ch):
                for h in range(NH1):
                    last = t.matmul(ps_s[:, h * CL:(h + 1) * CL],
                                    _qk[:, (8 + h) * CL:(9 + h) * CL],
                                    _qk[:, h * CL:(h + 1) * CL],
                                    start=True, stop=True)
                return last
            kb.step(tensor=kb.comp(mmS))
            vs(lambda v: v.tensor_tensor(sw_sb[:, :], ps_s[:, :], w_sb[:, :], op=ALU.mult))

            def mmO(t, _vt=vtk):
                for h in range(NH1):
                    t.matmul(ps_o[:, h * CL:(h + 1) * CL], sw_sb[:, h * CL:(h + 1) * CL],
                             _vt[:, h * DV1:(h + 1) * DV1], start=True, stop=True)
                for h in range(NH1):
                    last = t.matmul(ps_qn[:, h:h + 1], sw_sb[:, h * CL:(h + 1) * CL],
                                    ones_cb[:, :], start=True, stop=True)
                return last
            kb.step(tensor=kb.comp(mmO))

            def mmQC(t, _qk=qk_ch):
                for h in range(NH1):
                    t.matmul(ps_w[:, h * CL:(h + 1) * CL],
                             _qk[:, h * CL:(h + 1) * CL],
                             cn_bf[:, h * DV1:(h + 1) * DV1], start=True, stop=True)
                for h in range(NH1):
                    last = t.matmul(ps_qn[:, 8 + h:9 + h],
                                    _qk[:, h * CL:(h + 1) * CL],
                                    nn_bf[:, h:h + 1], start=True, stop=True)
                return last
            kb.step(tensor=kb.comp(mmQC))
            vs(lambda v: v.tensor_copy(o_sb[:, :], ps_o[:, :]))

            def addinter(v):
                for h in range(NH1):
                    last = v.scalar_tensor_tensor(num_sb[:, h * CL:(h + 1) * CL],
                                                  ps_w[:, h * CL:(h + 1) * CL],
                                                  ea_tok[:, h:h + 1],
                                                  o_sb[:, h * CL:(h + 1) * CL],
                                                  op0=ALU.mult, op1=ALU.add)
                return last
            vs(addinter)
            vs(lambda v: v.tensor_copy(qn1[:, :], ps_qn[:, 0:8]))
            vs(lambda v: v.tensor_tensor(qn2[:, :], ea_tok[:, :], ps_qn[:, 8:16], op=ALU.mult))
            vs(lambda v: v.tensor_tensor(qn1[:, :], qn1[:, :], qn2[:, :], op=ALU.add))
            kb.step(scalar=kb.comp(lambda s: s.activation(qn1[:, :], qn1[:, :], AF.Abs)))
            vs(lambda v: v.tensor_tensor(dn[:, :], qn1[:, :], em_tok[:, :], op=ALU.max))
            vs(lambda v: v.tensor_scalar(dn[:, :], dn[:, :], 1e-6, None, op0=ALU.add))
            vs(lambda v: v.reciprocal(rcp[:, :], dn[:, :]))

            def hdiv(v):
                for h in range(NH1):
                    last = v.tensor_scalar(h_sb[:, h * CL:(h + 1) * CL],
                                           num_sb[:, h * CL:(h + 1) * CL],
                                           rcp[:, h:h + 1], None, op0=ALU.mult)
                return last
            vs(hdiv)
            kb.step(sync=kb.dma("sync", htok_h[t0:t0 + CL, :], h_sb[:, :]))
            kb.step(tensor=kb.comp(lambda t: t.transpose(ps_t8[0:1, 0:8], dT[:, 127:128],
                                                         id128f[0:8, 0:8])))
            vs(lambda v: v.tensor_copy(dlrow[:, :], ps_t8[0:1, 0:8]))
            kb.step(tensor=kb.comp(lambda t: t.matmul(ps_b8[:, :], ones_r[:, :], dlrow[:, :],
                                                      start=True, stop=True)))
            vs(lambda v: v.tensor_tensor(e_tok[:, :], g_tok[:, :], ps_b8[:, :], op=ALU.subtract))
            kb.step(scalar=kb.comp(lambda s: s.activation(e_tok[:, :], e_tok[:, :], AF.Exp)))

            def kemul(v, _kt=ktk):
                for h in range(NH1):
                    last = v.tensor_scalar(ke_sb[:, h * DQK:(h + 1) * DQK],
                                           _kt[:, h * DQK:(h + 1) * DQK],
                                           e_tok[:, h:h + 1], None, op0=ALU.mult)
                return last
            vs(kemul)

            def mmCN(t, _vt=vtk):
                for h in range(NH1):
                    t.matmul(ps_cn[:, h * DV1:(h + 1) * DV1], ke_sb[:, h * DQK:(h + 1) * DQK],
                             _vt[:, h * DV1:(h + 1) * DV1], start=True, stop=True)
                for h in range(NH1):
                    last = t.matmul(ps_nn[:, h:h + 1], ke_sb[:, h * DQK:(h + 1) * DQK],
                                    ones_cb[:, :], start=True, stop=True)
                return last
            kb.step(tensor=kb.comp(mmCN))
            vs(lambda v: v.tensor_tensor(t8[:, :], acol[:, :], dT[:, 127:128], op=ALU.subtract))
            kb.step(scalar=kb.comp(lambda s: s.activation(t8[:, :], t8[:, :], AF.Exp)))
            vs(lambda v: v.tensor_copy(eacol[:, :], t8[:, :]))
            kb.step(tensor=kb.comp(lambda t: t.transpose(ps_t8[0:1, 0:8], eacol[:, :],
                                                         id128f[0:8, 0:8])))
            vs(lambda v: v.tensor_copy(earow[:, :], ps_t8[0:1, 0:8]))
            kb.step(tensor=kb.comp(lambda t: t.matmul(ps_b64[:, :], ones_r[0:1, 0:64],
                                                      earow[:, :], start=True, stop=True)))
            vs(lambda v: v.tensor_copy(ea64[:, :], ps_b64[:, :]))

            def cnup(v):
                for h in range(NH1):
                    last = v.scalar_tensor_tensor(cn_sb[:, h * DV1:(h + 1) * DV1],
                                                  cn_sb[:, h * DV1:(h + 1) * DV1],
                                                  ea64[:, h:h + 1],
                                                  ps_cn[:, h * DV1:(h + 1) * DV1],
                                                  op0=ALU.mult, op1=ALU.add)
                return last
            vs(cnup)

            def nnup(v):
                for h in range(NH1):
                    last = v.scalar_tensor_tensor(nn_sb[:, h:h + 1], nn_sb[:, h:h + 1],
                                                  ea64[:, h:h + 1], ps_nn[:, h:h + 1],
                                                  op0=ALU.mult, op1=ALU.add)
                return last
            vs(nnup)
            vs(lambda v: (v.tensor_copy(cn_bf[:, :], cn_sb[:, :]),
                          v.tensor_copy(nn_bf[:, :], nn_sb[:, :]))[-1])
            vs(lambda v: v.tensor_tensor(acol[:, :], dT[:, 127:128], flT[:, 127:128], op=ALU.add))
            kb.step(tensor=kb.comp(lambda t: t.transpose(ps_t8[0:1, 0:8], acol[:, :],
                                                         id128f[0:8, 0:8])))
            vs(lambda v: v.tensor_copy(arow[:, :], ps_t8[0:1, 0:8]))

    with kb.scope():
        _c2()

    def _c3():
        TQ = 512
        wout = kb.sbuf("c3wo", [128, DCH * D], BF16)
        lnw = kb.sbuf("c3lw", [128, DCH], F32)
        lnb = kb.sbuf("c3lb", [128, DCH], F32)
        h_tok = kb.sbuf("c3ht", [128, D], BF16)
        hT = kb.sbuf("c3hT", [128, DCH * TQ], BF16)
        sq_sb = kb.sbuf("c3sq", [128, DCH * TQ], F32)
        sog = kb.sbuf("c3sog", [128, DCH * TQ], BF16)
        x2_t = kb.sbuf("c3x2", [128, DCH * TQ], F32)
        hout = kb.sbuf("c3ho", [128, DCH * TQ], BF16)
        rows = kb.sbuf("c3rw", [1, 2 * TQ], F32)
        mbc = kb.sbuf("c3mb", [128, TQ], F32)
        abc = kb.sbuf("c3ab", [128, TQ], F32)
        x3_t = kb.sbuf("c3x3", [128, DCH * TQ], F32)
        ps_tx = kb.psum("c3ptx", [128, 512], BF16)
        ps_s = kb.psum("c3psu", [1, TQ])
        ps_q = kb.psum("c3psq", [1, TQ])
        ps_m = kb.psum("c3pm", [128, TQ])
        ps_a = kb.psum("c3pa", [128, TQ])
        ps_yq = [kb.psum("c3py%d" % k, [128, TQ]) for k in range(2)]
        kb.step(gpsimd=kb.dma("gpsimd", sb3(wout[:, :], D),
                              w["L1a_woutT"].rearrange("(c p) e -> p c e", p=128)),
                sync=kb.dmas([(lnw[:, :], w["L1a_lnwc"]), (lnb[:, :], w["L1a_lnbc"])]))
        m_row = rows[0:1, 0:TQ]
        a_row = rows[0:1, TQ:2 * TQ]
        for i in range(T // TQ):
            t0 = i * TQ
            for sub in range(4):
                kb.step(sync=kb.dma("sync", h_tok[:, :], htok_h[t0 + sub * 128:t0 + (sub + 1) * 128, :]))

                def trh1(t, _sub=sub):
                    for c in range(4):
                        last = t.transpose(ps_tx[:, c * 128:(c + 1) * 128],
                                           h_tok[:, c * 128:(c + 1) * 128], id128b[:, :])
                    return last
                def cph1(v, _sub=sub):
                    for c in range(4):
                        last = v.tensor_copy(hT[:, c * TQ + _sub * 128: c * TQ + (_sub + 1) * 128],
                                             ps_tx[:, c * 128:(c + 1) * 128])
                    return last
                def trh2(t, _sub=sub):
                    for c in range(4, DCH):
                        last = t.transpose(ps_tx[:, (c - 4) * 128:(c - 3) * 128],
                                           h_tok[:, c * 128:(c + 1) * 128], id128b[:, :])
                    return last
                def cph2(v, _sub=sub):
                    for c in range(4, DCH):
                        last = v.tensor_copy(hT[:, c * TQ + _sub * 128: c * TQ + (_sub + 1) * 128],
                                             ps_tx[:, (c - 4) * 128:(c - 3) * 128])
                    return last
                kb.step(tensor=kb.comp(trh1))
                kb.step(vector=kb.comp(cph1))
                kb.step(tensor=kb.comp(trh2))
                kb.step(vector=kb.comp(cph2))
            kb.step(sync=kb.dmas([
                (sb3(sog[:, :], TQ), fm(sogT_h, t0, TQ)),
                (sb3(x2_t[:, :], TQ), fm(x2T, t0, TQ)),
            ]), scalar=kb.comp(lambda s: s.activation(sq_sb[:, :], hT[:, :], AF.Square)))

            def mmstat(t):
                for c in range(DCH):
                    t.matmul(ps_s[:, :], ones_cb[:, :], hT[:, c * TQ:(c + 1) * TQ],
                             start=(c == 0), stop=(c == DCH - 1))
                for c in range(DCH):
                    last = t.matmul(ps_q[:, :], ones_c[:, :], sq_sb[:, c * TQ:(c + 1) * TQ],
                                    start=(c == 0), stop=(c == DCH - 1))
                return last
            kb.step(tensor=kb.comp(mmstat))

            kb.step(vector=kb.comp(lambda v: (
                v.tensor_scalar(m_row, ps_s[:, :], 1.0 / D, None, op0=ALU.mult),
                v.tensor_scalar(a_row, ps_q[:, :], 1.0 / D, None, op0=ALU.mult))[-1]))
            kb.step(vector=kb.comp(lambda v: v.tensor_tensor(ps_s[0:1, :], m_row, m_row,
                                                             op=ALU.mult)))
            kb.step(vector=kb.comp(lambda v: v.tensor_tensor(a_row, a_row, ps_s[0:1, :],
                                                             op=ALU.subtract)))
            kb.step(scalar=kb.comp(lambda s: s.activation(a_row, a_row, AF.Ln,
                                                          bias=kb.c(1e-6, p=1))))
            kb.step(scalar=kb.comp(lambda s: s.activation(a_row, a_row, AF.Exp, scale=-0.5)))

            def bc(t):
                t.matmul(ps_m[:, :], ones_r[:, :], m_row, start=True, stop=True)
                return t.matmul(ps_a[:, :], ones_r[:, :], a_row, start=True, stop=True)
            kb.step(tensor=kb.comp(bc))
            kb.step(vector=kb.comp(lambda v: (v.tensor_copy(mbc[:, :], ps_m[:, :]),
                                              v.tensor_copy(abc[:, :], ps_a[:, :]))[-1]))

            def ap1(v):
                for c in range(DCH):
                    sl = slice(c * TQ, (c + 1) * TQ)
                    last = v.tensor_tensor(sq_sb[:, sl], hT[:, sl], mbc[:, :], op=ALU.subtract)
                return last
            def ap2(v):
                for c in range(DCH):
                    sl = slice(c * TQ, (c + 1) * TQ)
                    last = v.tensor_tensor(sq_sb[:, sl], sq_sb[:, sl], abc[:, :], op=ALU.mult)
                return last
            def ap3(v):
                for c in range(DCH):
                    sl = slice(c * TQ, (c + 1) * TQ)
                    last = v.tensor_scalar(sq_sb[:, sl], sq_sb[:, sl], lnw[:, c:c + 1],
                                           None, op0=ALU.mult)
                return last
            def ap4(v):
                for c in range(DCH):
                    sl = slice(c * TQ, (c + 1) * TQ)
                    last = v.scalar_tensor_tensor(hout[:, sl], sq_sb[:, sl], lnb[:, c:c + 1],
                                                  sog[:, sl], op0=ALU.add, op1=ALU.mult)
                return last
            kb.step(vector=kb.comp(ap1))
            kb.step(vector=kb.comp(ap2))
            kb.step(vector=kb.comp(ap3))
            kb.step(vector=kb.comp(ap4))

            def mmy3(t, _fc):
                for vc in range(DCH):
                    last = t.matmul(ps_yq[_fc % 2][:, :],
                                    wout[:, vc * D + _fc * 128: vc * D + _fc * 128 + 128],
                                    hout[:, vc * TQ:(vc + 1) * TQ],
                                    start=(vc == 0), stop=(vc == DCH - 1))
                return last

            def evy3(v, _fc):
                return v.tensor_tensor(x3_t[:, _fc * TQ:(_fc + 1) * TQ], ps_yq[_fc % 2][:, :],
                                       x2_t[:, _fc * TQ:(_fc + 1) * TQ], op=ALU.add)
            for fc in range(DCH + 1):
                parts = {}
                if fc < DCH:
                    parts["tensor"] = kb.comp(lambda t, _x=fc, _f=mmy3: _f(t, _x))
                if fc > 0:
                    parts["vector"] = kb.comp(lambda v, _x=fc - 1, _f=evy3: _f(v, _x))
                kb.step(**parts)
            kb.step(sync=kb.dma("sync", fm(x3T, t0, TQ), sb3(x3_t[:, :], TQ)))

    with kb.scope():
        _c3()



TOWN = 4096
CAP_E = 640
BIGI = 1024.0


def fm_dyn(hbm_ap, idx_sv, tt):
    return hbm_ap[:, bass.ts(idx_sv, tt)].rearrange("(c p) t -> p c t", p=128)


def emit_l1_ffn(kb, x3T, out_tok, w, scr):
    nc = kb.nc
    lnT = scr["d_ln"]
    acc_h = scr["d_acc"]
    fe_h = scr["d_fe"]
    pt_h = scr["d_pt"]
    hf_sb = kb.sbuf("d_hf", [1, 1], I32)
    slot = kb.sbuf("d_slot", [128, TOWN // 128], F32)
    sloti = kb.sbuf("d_sloti", [128, TOWN // 128], I32)
    regs = {}
    kb.step(sync=kb.dma("sync", hf_sb[:, :], w["own_hf"]))

    def loadreg(eng_name):
        def fn(e):
            regs[eng_name] = e.value_load(hf_sb[0:1, 0:1])
            return e.nop()
        return kb.comp(fn)
    kb.step(sync=loadreg("sync"))

    x3own = scr["d_x3own"]
    def cpy(e, semh):
        for q in range(4):
            e.dma_start(out=x3own[:, q * 1024:(q + 1) * 1024],
                        in_=x3T[:, bass.ts(regs["sync"] * 4 + q, 1024)]).then_inc(semh, 16)
    kb.step(sync=("dma", 64, cpy))

    def xsrc(t0, tt):
        return fm(x3own, t0, tt)

    with kb.scope():
        emit_layernorm(kb, xsrc, lnT, "ln3", ntok=TOWN)

    def _d2():
        NT = TOWN // 128
        x_sb = kb.sbuf("d2x", [128, DCH * 128], F32)
        sq_sb = kb.sbuf("d2q", [128, DCH * 128], F32)
        rwT = kb.sbuf("d2rw", [128, DCH * 8], F32)
        ones_c = kb.sbuf("d2o", [128, 1], F32)
        ones_r = kb.sbuf("d2or", [1, 128], F32)
        iota_big = kb.sbuf("d2ib", [128, 8], F32)
        srbc = kb.sbuf("d2sr", [128, 8], F32)
        msum = kb.sbuf("d2ms", [128, NT], F32)
        qsum = kb.sbuf("d2qs", [128, NT], F32)
        raw = kb.sbuf("d2raw", [128, NT * 8], F32)
        tmp = kb.sbuf("d2tmp", [128, NT * 8], F32)
        lall = kb.sbuf("d2l", [128, NT * 8], F32)
        m_all = kb.sbuf("d2m", [128, NT], F32)
        q_all = kb.sbuf("d2qq", [128, NT], F32)
        mm2 = kb.sbuf("d2m2", [128, NT], F32)
        nvar = kb.sbuf("d2nv", [128, NT], F32)
        a_all = kb.sbuf("d2a", [128, NT], F32)
        mx_all = kb.sbuf("d2mx", [128, NT], F32)
        sm_all = kb.sbuf("d2sm", [128, NT], F32)
        rc_all = kb.sbuf("d2rc", [128, NT], F32)
        pm_all = kb.sbuf("d2pm_", [128, NT], F32)
        ptil = kb.sbuf("d2pt", [128, NT], F32)
        band = kb.sbuf("d2bd", [128, NT], F32)
        ecol = kb.sbuf("d2ec", [128, NT], F32)
        fe_sb = kb.sbuf("d2fe", [128, NT], F32)
        pbc = kb.sbuf("d2pb", [128, TOWN], F32)
        prow = kb.sbuf("d2pr", [1, TOWN], F32)
        junk = kb.sbuf("d2jk", [128, TOWN], BF16)
        r1 = kb.sbuf("d2r1", [128, NT], F32)
        r2 = kb.sbuf("d2r2", [128, NT], F32)
        sr_col = kb.sbuf("d2src", [8, 1], F32)
        sr_row = kb.sbuf("d2srw", [1, 8], F32)
        id128 = kb.sbuf("d2id", [128, 128], F32)
        fe_t = kb.sbuf("d2fet", [NT, 128], F32)
        ps_m = kb.psum("d2pm", [128, 1])
        ps_q = kb.psum("d2pq", [128, 1])
        ps_r = kb.psum("d2praw", [128, 8])
        ps_sr = kb.psum("d2psr", [8, 1])
        ps_srr = kb.psum("d2psrr", [1, 8])
        ps_bc = kb.psum("d2pbc", [128, 512])
        ps_fe = kb.psum("d2pfe", [NT, 128])

        kb.step(sync=kb.dmas([
            (sb3(rwT[:, :], 8), w["L1f_rwT"].rearrange("(c p) e -> p c e", p=128)),
            (id128[:, :], w["ident_f32"]),
            (iota_big[:, :], w["iota8_big"]),
        ]), vector=kb.comp(lambda v: (v.memset(ones_c[:], 1.0),
                                      v.memset(ones_r[:], 1.0))[-1]))

        def srmm(t):
            for c in range(DCH):
                last = t.matmul(ps_sr[:, :], rwT[:, c * 8:(c + 1) * 8], ones_c[:, :],
                                start=(c == 0), stop=(c == DCH - 1))
            return last
        kb.step(tensor=kb.comp(srmm))
        kb.step(vector=kb.comp(lambda v: v.tensor_copy(sr_col[:, :], ps_sr[:, :])))
        kb.step(tensor=kb.comp(lambda t: t.transpose(ps_srr[:, :], sr_col[:, :], id128[0:8, 0:8])))
        kb.step(vector=kb.comp(lambda v: v.tensor_copy(sr_row[:, :], ps_srr[:, :])))
        kb.step(tensor=kb.comp(lambda t: t.matmul(ps_r[:, :], ones_r[:, :], sr_row[:, :],
                                                  start=True, stop=True)))
        kb.step(vector=kb.comp(lambda v: v.tensor_copy(srbc[:, :], ps_r[:, :])))

        for i in range(NT):
            kb.step(sync=kb.dma("sync", sb3(x_sb[:, :], 128), xsrc(i * 128, 128)))
            kb.step(scalar=kb.comp(lambda s: s.activation(sq_sb[:, :], x_sb[:, :], AF.Square)))

            def mm(t):
                for c in range(DCH):
                    t.matmul(ps_m[:, :], x_sb[:, c * 128:(c + 1) * 128], ones_c[:, :],
                             start=(c == 0), stop=(c == DCH - 1))
                for c in range(DCH):
                    t.matmul(ps_q[:, :], sq_sb[:, c * 128:(c + 1) * 128], ones_c[:, :],
                             start=(c == 0), stop=(c == DCH - 1))
                for c in range(DCH):
                    last = t.matmul(ps_r[:, :], x_sb[:, c * 128:(c + 1) * 128],
                                    rwT[:, c * 8:(c + 1) * 8],
                                    start=(c == 0), stop=(c == DCH - 1))
                return last
            kb.step(tensor=kb.comp(mm))

            def collect(v, _i=i):
                v.tensor_copy(msum[:, _i:_i + 1], ps_m[:, :])
                v.tensor_copy(qsum[:, _i:_i + 1], ps_q[:, :])
                return v.tensor_copy(raw[:, _i * 8:(_i + 1) * 8], ps_r[:, :])
            kb.step(vector=kb.comp(collect))

        def vstep(fn):
            kb.step(vector=kb.comp(fn))

        def per_tile(fn):
            def run(v):
                for i in range(NT):
                    last = fn(v, i)
                return last
            vstep(run)

        vstep(lambda v: (v.tensor_scalar(m_all[:, :], msum[:, :], 1.0 / D, None, op0=ALU.mult),
                         v.tensor_scalar(q_all[:, :], qsum[:, :], 1.0 / D, None, op0=ALU.mult))[-1])
        vstep(lambda v: v.tensor_tensor(mm2[:, :], m_all[:, :], m_all[:, :], op=ALU.mult))
        vstep(lambda v: v.tensor_tensor(nvar[:, :], mm2[:, :], q_all[:, :], op=ALU.subtract))
        kb.step(scalar=kb.comp(lambda s: s.activation(a_all[:, :], nvar[:, :], AF.Ln,
                                                      scale=-1.0, bias=kb.c(1e-5))))
        kb.step(scalar=kb.comp(lambda s: s.activation(a_all[:, :], a_all[:, :], AF.Exp, scale=-0.5)))
        per_tile(lambda v, i: v.tensor_scalar(tmp[:, i * 8:(i + 1) * 8], srbc[:, :],
                                              m_all[:, i:i + 1], None, op0=ALU.mult))
        vstep(lambda v: v.tensor_tensor(lall[:, :], raw[:, :], tmp[:, :], op=ALU.subtract))
        per_tile(lambda v, i: v.tensor_scalar(lall[:, i * 8:(i + 1) * 8], lall[:, i * 8:(i + 1) * 8],
                                              a_all[:, i:i + 1], None, op0=ALU.mult))
        per_tile(lambda v, i: v.tensor_reduce(mx_all[:, i:i + 1], lall[:, i * 8:(i + 1) * 8],
                                              axis=AXX, op=ALU.max))
        per_tile(lambda v, i: v.tensor_scalar(lall[:, i * 8:(i + 1) * 8], lall[:, i * 8:(i + 1) * 8],
                                              mx_all[:, i:i + 1], None, op0=ALU.subtract))
        kb.step(scalar=kb.comp(lambda s: s.activation(lall[:, :], lall[:, :], AF.Exp)))
        per_tile(lambda v, i: v.tensor_reduce(sm_all[:, i:i + 1], lall[:, i * 8:(i + 1) * 8],
                                              axis=AXX, op=ALU.add))
        vstep(lambda v: v.reciprocal(rc_all[:, :], sm_all[:, :]))
        per_tile(lambda v, i: v.tensor_scalar(lall[:, i * 8:(i + 1) * 8], lall[:, i * 8:(i + 1) * 8],
                                              rc_all[:, i:i + 1], None, op0=ALU.mult))
        per_tile(lambda v, i: v.tensor_reduce(pm_all[:, i:i + 1], lall[:, i * 8:(i + 1) * 8],
                                              axis=AXX, op=ALU.max))
        per_tile(lambda v, i: v.tensor_scalar(tmp[:, i * 8:(i + 1) * 8], lall[:, i * 8:(i + 1) * 8],
                                              pm_all[:, i:i + 1], None, op0=ALU.is_equal))
        vstep(lambda v: v.tensor_scalar(tmp[:, :], tmp[:, :], BIGI, None, op0=ALU.mult))
        per_tile(lambda v, i: v.tensor_tensor(tmp[:, i * 8:(i + 1) * 8], iota_big[:, :],
                                              tmp[:, i * 8:(i + 1) * 8], op=ALU.subtract))
        per_tile(lambda v, i: v.tensor_reduce(ecol[:, i:i + 1], tmp[:, i * 8:(i + 1) * 8],
                                              axis=AXX, op=ALU.min))
        vstep(lambda v: v.tensor_scalar(ptil[:, :], ecol[:, :], 2.0, None, op0=ALU.mult))
        vstep(lambda v: v.tensor_scalar(band[:, :], ptil[:, :], 2.0, None, op0=ALU.add))
        vstep(lambda v: v.tensor_tensor(ptil[:, :], ptil[:, :], pm_all[:, :], op=ALU.add))

        kb.step(sync=kb.dma("sync", pt_h.ap(), ptil[:, :]))
        kb.step(sync=kb.dma("sync", prow[:, :],
                            pt_h.ap().rearrange("p i -> (p i)").rearrange("(a b) -> a b", a=1)))
        for j in range(8):
            kb.step(tensor=kb.comp(lambda t, _j=j: t.matmul(
                ps_bc[:, :], ones_r[:, :], prow[:, _j * 512:(_j + 1) * 512],
                start=True, stop=True)))
            kb.step(vector=kb.comp(lambda v, _j=j: v.tensor_copy(
                pbc[:, _j * 512:(_j + 1) * 512], ps_bc[:, :])))

        for i in range(NT):
            kb.step(vector=kb.comp(lambda v, _i=i: v.tensor_scalar(
                junk[:, :], pbc[:, :], ptil[:, _i:_i + 1], 0.0,
                op0=ALU.is_gt, op1=ALU.add, accum_out=r1[:, _i:_i + 1])))
            kb.step(vector=kb.comp(lambda v, _i=i: v.tensor_scalar(
                junk[:, :], pbc[:, :], band[:, _i:_i + 1], 0.0,
                op0=ALU.is_ge, op1=ALU.add, accum_out=r2[:, _i:_i + 1])))

        vstep(lambda v: v.tensor_tensor(r1[:, :], r1[:, :], r2[:, :], op=ALU.subtract))
        vstep(lambda v: v.tensor_scalar(r1[:, :], r1[:, :], float(CAP_E), None, op0=ALU.is_lt))
        vstep(lambda v: v.tensor_tensor(fe_sb[:, :], ecol[:, :], r1[:, :], op=ALU.mult))
        kb.step(tensor=kb.comp(lambda t: t.transpose(ps_fe[:, :], fe_sb[:, :], id128[:, :])))
        kb.step(vector=kb.comp(lambda v: v.tensor_copy(fe_t[:, :], ps_fe[:, :])))
        kb.step(sync=kb.dma("sync", fe_h.ap().rearrange("(i p) -> i p", p=128), fe_t[:, :]))

    with kb.scope():
        _d2()

    S0C = 1536
    SEG = [0, S0C] + [S0C + CAP_E * e for e in range(1, 8)]
    NSLOT = SEG[8]

    def _d3():
        NT = TOWN // 128
        ln_sort = scr["d_lnsort"]
        idb = kb.sbuf("d3idb", [128, 128], BF16)
        idf = kb.sbuf("d3idf", [128, 128], F32)
        iota8 = kb.sbuf("d3io8", [128, 8], F32)
        utris = kb.sbuf("d3uts", [128, 128], F32)
        eh32 = kb.sbuf("d3eh", [32, 32 * 128], F32)
        seg8 = kb.sbuf("d3sg", [128, 8], F32)
        cap8 = kb.sbuf("d3cp", [128, 8], F32)
        tokid = kb.sbuf("d3tk", [128, NT], I32)
        ones_c = kb.sbuf("d3oc", [128, 1], F32)
        fe2 = kb.sbuf("d3fe", [128, NT], F32)
        fe_rows = kb.sbuf("d3fer", [32, 128], F32)
        lnst = kb.sbuf("d3lns", [128, DCH * 128], BF16)
        lt2 = kb.sbuf("d3lt2", [128, D], BF16)
        oh_sb = kb.sbuf("d3oh", [128, 8], F32)
        t8 = kb.sbuf("d3t8", [128, 8], F32)
        rw = kb.sbuf("d3rw", [128, NT], F32)
        capr = kb.sbuf("d3cr", [128, 1], F32)
        cntT = kb.sbuf("d3cT", [8, 32], F32)
        cnt2 = kb.sbuf("d3c2", [32, 8], F32)
        adjT = kb.sbuf("d3aj", [32, 8], F32)
        zln = kb.sbuf("d3z", [128, D], BF16)
        bigi = kb.sbuf("d3bg", [128, 64], I32)
        ps_tx = kb.psum("d3ptx", [128, 512], BF16)
        ps_sm = kb.psum("d3psm", [128, 128])
        ps_c = ps_sm[0:8, 0:32]
        ps_p = ps_sm[:, 32:40]
        ps_c2 = ps_sm[0:32, 40:48]
        ps_bc = ps_sm[:, 48:56]
        ps_fe2 = ps_sm[:, 56:88]
        kb.step(sync=kb.dmas([
            (idf[:, :], w["ident_f32"]),
            (utris[:, :], w["utris128"]),
            (eh32[:, :], w["eh32"]),
            (seg8[:, :], w["seg8c"]),
            (cap8[:, :], w["cap8c"]),
            (tokid[:, :], w["tokid32i"]),
            (iota8[:, :], w["iota8c"]),
        ]), gpsimd=kb.dma("gpsimd", fe_rows[:, :],
                          fe_h.ap().rearrange("(i p) -> i p", p=128)),
           vector=kb.comp(lambda v: (v.memset(ones_c[:], 1.0), v.memset(zln[:, :], 0.0),
                                     v.memset(bigi[:, :], float(1 << 29)))[-1]))
        kb.step(tensor=kb.comp(lambda t: t.transpose(ps_fe2[:, :], fe_rows[:, :],
                                                     idf[0:32, 0:32])))
        kb.step(vector=kb.comp(lambda v: v.tensor_copy(fe2[:, :], ps_fe2[:, :])))
        kb.step(vector=kb.comp(lambda v: v.tensor_copy(idb[:, :], idf[:, :])))
        for i in range(NSLOT // 128):
            kb.step(gpsimd=kb.dma("gpsimd", ln_sort[i * 128:(i + 1) * 128, :], zln[:, :]))
        for i in range(NT):
            kb.step(vector=kb.comp(lambda v, _i=i: v.tensor_scalar(
                oh_sb[:, :], iota8[:, :], fe2[:, _i:_i + 1], None, op0=ALU.is_equal)))

            def mmc(t, _i=i):
                t.matmul(ps_c[:, _i:_i + 1], oh_sb[:, :], ones_c[:, :], start=True, stop=True)
                return t.matmul(ps_p[:, :], utris[:, :], oh_sb[:, :], start=True, stop=True)
            kb.step(tensor=kb.comp(mmc))
            kb.step(vector=kb.comp(lambda v: v.tensor_tensor(t8[:, :], oh_sb[:, :],
                                                             ps_p[:, :], op=ALU.mult)))
            kb.step(vector=kb.comp(lambda v, _i=i: v.tensor_reduce(
                rw[:, _i:_i + 1], t8[:, :], axis=AXX, op=ALU.add)))
        kb.step(vector=kb.comp(lambda v: v.tensor_copy(cntT[:, :], ps_c[:, :])))
        kb.step(tensor=kb.comp(lambda t: t.transpose(ps_c2[:, :], cntT[:, :], idf[0:8, 0:8])))
        kb.step(vector=kb.comp(lambda v: v.tensor_copy(cnt2[:, :], ps_c2[:, :])))
        kb.step(tensor=kb.comp(lambda t: t.matmul(ps_c2[:, :], utris[0:32, 0:32], cnt2[:, :],
                                                  start=True, stop=True)))
        kb.step(vector=kb.comp(lambda v: v.tensor_tensor(adjT[:, :], ps_c2[:, :],
                                                         seg8[0:32, :], op=ALU.add)))
        for i in range(NT):
            kb.step(tensor=kb.comp(lambda t, _i=i: t.matmul(
                ps_bc[:, :], eh32[:, _i * 128:(_i + 1) * 128], adjT[:, :],
                start=True, stop=True)))
            kb.step(vector=kb.comp(lambda v, _i=i: v.tensor_scalar(
                oh_sb[:, :], iota8[:, :], fe2[:, _i:_i + 1], None, op0=ALU.is_equal)))
            kb.step(vector=kb.comp(lambda v: v.tensor_tensor(t8[:, :], oh_sb[:, :],
                                                             ps_bc[:, :], op=ALU.mult)))
            kb.step(vector=kb.comp(lambda v, _i=i: v.tensor_reduce(
                slot[:, _i:_i + 1], t8[:, :], axis=AXX, op=ALU.add)))
            kb.step(vector=kb.comp(lambda v: v.tensor_tensor(t8[:, :], oh_sb[:, :],
                                                             cap8[:, :], op=ALU.mult)))
            kb.step(vector=kb.comp(lambda v: v.tensor_reduce(capr[:, :], t8[:, :],
                                                             axis=AXX, op=ALU.add)))
            kb.step(vector=kb.comp(lambda v, _i=i: v.tensor_tensor(
                slot[:, _i:_i + 1], slot[:, _i:_i + 1], rw[:, _i:_i + 1], op=ALU.add)))
            kb.step(vector=kb.comp(lambda v, _i=i: v.tensor_scalar(
                capr[:, :], rw[:, _i:_i + 1], capr[:, 0:1], None, op0=ALU.is_ge)))
            kb.step(vector=kb.comp(lambda v: v.tensor_scalar(
                capr[:, :], capr[:, :], float(1 << 29), None, op0=ALU.mult)))
            kb.step(vector=kb.comp(lambda v, _i=i: v.tensor_tensor(
                slot[:, _i:_i + 1], slot[:, _i:_i + 1], capr[:, :], op=ALU.add)))
        kb.step(vector=kb.comp(lambda v: v.tensor_copy(sloti[:, :], slot[:, :])))
        for i in range(NT):
            kb.step(sync=kb.dma("sync", sb3(lnst[:, :], 128), fm(lnT, i * 128, 128)))
            for g in range(2):
                def trg(t, _g=g):
                    for c in range(4):
                        last = t.transpose(ps_tx[:, c * 128:(c + 1) * 128],
                                           lnst[:, (_g * 4 + c) * 128:(_g * 4 + c + 1) * 128],
                                           idb[:, :])
                    return last

                def cpg(v, _g=g):
                    for c in range(4):
                        last = v.tensor_copy(lt2[:, (_g * 4 + c) * 128:(_g * 4 + c + 1) * 128],
                                             ps_tx[:, c * 128:(c + 1) * 128])
                    return last
                kb.step(tensor=kb.comp(trg))
                kb.step(vector=kb.comp(cpg))

            def sc1(e, semh, _i=i):
                e.indirect_dma_start(
                    out=ln_sort, out_offset=IOffs(ap=sloti[:, _i:_i + 1], axis=0),
                    in_=lt2[:, :], in_offset=None,
                    bounds_check=NSLOT - 1, oob_is_err=False).then_inc(semh, 16)
            kb.step(gpsimd=("dma", 16, sc1))

        w1 = kb.sbuf("d3w1", [128, DCH * 4096], BF16)
        w2 = kb.sbuf("d3w2", [128, 32 * D], BF16)
        b1 = kb.sbuf("d3b1", [128, 32], F32)
        b2r = kb.sbuf("d3b2r", [1, D], F32)
        b2bc = kb.sbuf("d3b2b", [128, D], F32)
        lnsp = kb.sbuf("d3lsp", [128, DCH * 512], BF16)
        h_sb = kb.sbuf("d3h", [128, 32 * 512], BF16)
        ytk2 = [kb.sbuf("d3y%d" % k, [128, D], BF16) for k in range(2)]
        ones_r = kb.sbuf("d3or", [1, 128], F32)
        ps_hp = [kb.psum("d3ph%d" % k, [128, 512]) for k in range(2)]
        ps_yp = [kb.psum("d3py%d" % k, [128, 512]) for k in range(2)]
        ps_b2 = kb.psum("d3pb2", [128, 512])
        y_sort = scr["d_ysort"]
        kb.step(vector=kb.comp(lambda v: v.memset(ones_r[:], 1.0)))
        for e in range(8):
            kb.step(gpsimd=kb.dmas([
                (sb3(w1[:, :], 4096), w["L1f_W1"][e].rearrange("(c p) j -> p c j", p=128)),
                (w2[:, :].rearrange("p (c f) -> p c f", f=D),
                 w["L1f_W2"][e].rearrange("(c p) f -> p c f", p=128)),
            ]), sync=kb.dmas([
                (b1[:, :], w["L1f_b1"][e]),
                (b2r[:, :], w["L1f_b2r"][e].rearrange("(a b) -> a b", a=1)),
            ]))
            for half in range(2):
                kb.step(tensor=kb.comp(lambda t, _h=half: t.matmul(
                    ps_b2[:, :], ones_r[:, :], b2r[:, _h * 512:(_h + 1) * 512],
                    start=True, stop=True)))
                kb.step(vector=kb.comp(lambda v, _h=half: v.tensor_copy(
                    b2bc[:, _h * 512:(_h + 1) * 512], ps_b2[:, :])))
            segl = SEG[e + 1] - SEG[e]
            spans = [(SEG[e] + o, min(512, segl - o)) for o in range(0, segl, 512)]
            for (s0, sl) in spans:
                ntile = sl // 128
                for j in range(ntile):
                    kb.step(sync=kb.dma("sync", lt2[:, :],
                                        ln_sort[s0 + j * 128:s0 + (j + 1) * 128, :]))
                    for g in range(2):
                        def trg2(t, _g=g):
                            for c in range(4):
                                last = t.transpose(ps_tx[:, c * 128:(c + 1) * 128],
                                                   lt2[:, (_g * 4 + c) * 128:(_g * 4 + c + 1) * 128],
                                                   idb[:, :])
                            return last

                        def cpg2(v, _g=g, _j=j, _sl=sl):
                            for c in range(4):
                                cc = _g * 4 + c
                                last = v.tensor_copy(
                                    lnsp[:, cc * _sl + _j * 128: cc * _sl + (_j + 1) * 128],
                                    ps_tx[:, c * 128:(c + 1) * 128])
                            return last
                        kb.step(tensor=kb.comp(trg2))
                        kb.step(vector=kb.comp(cpg2))
                def mmh(t, _j, _sl=sl):
                    for c in range(DCH):
                        last = t.matmul(ps_hp[_j % 2][:, 0:_sl],
                                        w1[:, c * 4096 + _j * 128: c * 4096 + _j * 128 + 128],
                                        lnsp[:, c * _sl:(c + 1) * _sl],
                                        start=(c == 0), stop=(c == DCH - 1))
                    return last

                def evh(s, _j, _sl=sl):
                    return s.activation(h_sb[:, _j * _sl:(_j + 1) * _sl], ps_hp[_j % 2][:, 0:_sl],
                                        AF.Relu, bias=b1[:, _j:_j + 1])
                for j in range(33):
                    parts = {}
                    if j < 32:
                        parts["tensor"] = kb.comp(lambda t, _j=j, _f=mmh: _f(t, _j))
                    if j > 0:
                        parts["scalar"] = kb.comp(lambda s, _j=j - 1, _f=evh: _f(s, _j))
                    kb.step(**parts)
                def mmy(t, _q, _sl=sl):
                    _j, _h = _q // 2, _q % 2
                    for jc in range(32):
                        last = t.matmul(ps_yp[_q % 2][:, :],
                                        h_sb[:, jc * _sl + _j * 128: jc * _sl + (_j + 1) * 128],
                                        w2[:, jc * D + _h * 512: jc * D + (_h + 1) * 512],
                                        start=(jc == 0), stop=(jc == 31))
                    return last

                def evy(v, _q):
                    _j, _h = _q // 2, _q % 2
                    return v.tensor_tensor(ytk2[_j % 2][:, _h * 512:(_h + 1) * 512],
                                           ps_yp[_q % 2][:, :],
                                           b2bc[:, _h * 512:(_h + 1) * 512], op=ALU.add)
                nq = 2 * ntile
                for q in range(nq + 2):
                    parts = {}
                    if q < nq:
                        parts["tensor"] = kb.comp(lambda t, _q=q, _f=mmy: _f(t, _q))
                    if 0 < q <= nq:
                        parts["vector"] = kb.comp(lambda v, _q=q - 1, _f=evy: _f(v, _q))
                    if q > 1 and (q - 2) % 2 == 1:
                        _j = (q - 2) // 2
                        parts["gpsimd"] = kb.dma("gpsimd",
                                                 y_sort[s0 + _j * 128:s0 + (_j + 1) * 128, :],
                                                 ytk2[_j % 2][:, :])
                    kb.step(**parts)

    srow_h = scr["d_srow"]

    def _d3u():
        y_sort = scr["d_ysort"]
        idf2 = kb.sbuf("duid", [128, 128], F32)
        slot_row = kb.sbuf("duslr", [1, TOWN], F32)
        slot_bc = kb.sbuf("duslb", [128, TOWN], F32)
        slotc47 = kb.sbuf("dus47", [128, 47], F32)
        bnc = kb.sbuf("dubnc", [32, 128], F32)
        oh_all = kb.sbuf("duoha", [128, 47 * 512], BF16)
        ysb = kb.sbuf("duys", [128, 24 * D], BF16)
        x3t2 = kb.sbuf("dux3", [128, DCH * 512], F32)
        o2 = kb.sbuf("duo2", [128, DCH * 512], F32)
        ones_r = kb.sbuf("duor", [1, 128], F32)
        kb.scopes.append(ExitStack())
        ps_bn2 = kb.psum("dupbn", [32, 128])
        kb.step(sync=kb.dmas([(slotc47[:, :], w["slotcol47"]),
                              (idf2[:, :], w["ident_f32"])]),
                vector=kb.comp(lambda v: v.memset(ones_r[:], 1.0)))
        kb.step(tensor=kb.comp(lambda t: t.transpose(ps_bn2[:, :], slot[:, :], idf2[:, :])))
        kb.step(vector=kb.comp(lambda v: v.tensor_copy(bnc[:, :], ps_bn2[:, :])))
        kb.scopes.pop().close()
        ps_u = [kb.psum("dupu%d" % fc, [128, 512]) for fc in range(8)]
        kb.step(sync=kb.dma("sync", srow_h.ap(), bnc[:, :]))
        kb.step(sync=kb.dma("sync", slot_row[:, :],
                            srow_h.ap().rearrange("i p -> (i p)").rearrange("(a b) -> a b", a=1)))
        for jj in range(TOWN // 512):
            kb.step(tensor=kb.comp(lambda t, _j=jj: t.matmul(
                ps_u[0][:, :], ones_r[:, :], slot_row[:, _j * 512:(_j + 1) * 512],
                start=True, stop=True)))
            kb.step(vector=kb.comp(lambda v, _j=jj: v.tensor_copy(
                slot_bc[:, _j * 512:(_j + 1) * 512], ps_u[0][:, :])))
        SBLK = [(0, 24), (24, 23)]
        for sp in range(TOWN // 512):
            t0 = sp * 512

            def bld(v, _sp=sp):
                for sc in range(47):
                    last = v.tensor_scalar(oh_all[:, sc * 512:(sc + 1) * 512],
                                           slot_bc[:, _sp * 512:(_sp + 1) * 512],
                                           slotc47[:, sc:sc + 1], None, op0=ALU.is_equal)
                return last
            kb.step(vector=kb.comp(bld),
                    gpsimd=kb.dma("gpsimd", sb3(x3t2[:, :], 512), fm(x3own, t0, 512)))
            for (c0, nch) in SBLK:
                kb.step(sync=kb.dma("sync",
                                    ysb[:, :].rearrange("p (c f) -> p c f", f=D)[:, 0:nch, :],
                                    y_sort[c0 * 128:(c0 + nch) * 128, :]
                                    .rearrange("(c p) f -> p c f", p=128)))
                for fc in range(8):
                    def mmu2(t, _fc=fc, _c0=c0, _n=nch):
                        for sc in range(_n):
                            last = t.matmul(ps_u[_fc][:, :],
                                            ysb[:, sc * D + _fc * 128:sc * D + (_fc + 1) * 128],
                                            oh_all[:, (_c0 + sc) * 512:(_c0 + sc + 1) * 512],
                                            start=(_c0 == 0 and sc == 0),
                                            stop=(_c0 == 24 and sc == _n - 1))
                        return last
                    kb.step(tensor=kb.comp(mmu2))

            def ev8(v):
                for fc in range(8):
                    last = v.tensor_tensor(o2[:, fc * 512:(fc + 1) * 512], ps_u[fc][:, :],
                                           x3t2[:, fc * 512:(fc + 1) * 512], op=ALU.add)
                return last
            kb.step(vector=kb.comp(ev8))
            kb.step(sync=kb.dma("sync", fm(out_tok, t0, 512), sb3(o2[:, :], 512)))
    with kb.scope():
        _d3()
    with kb.scope():
        _d3u()



def build_program(stages=("A", "D"), x3_is_input=False, debug=False):
    nc = bass.Bass("TRN2", target_bir_lowering=False, debug=False)
    w = {
        "L0a_wqkvT": nc.dram_tensor("L0a_wqkvT", [D, 3072], F32, kind="ExternalInput").ap(),
        "L0a_woutT": nc.dram_tensor("L0a_woutT", [D, D], F32, kind="ExternalInput").ap(),
        "L0a_bout": nc.dram_tensor("L0a_bout", [128, DCH], F32, kind="ExternalInput").ap(),
        "L1f_rwT": nc.dram_tensor("L1f_rwT", [D, 8], F32, kind="ExternalInput").ap(),
        "L1f_W1": nc.dram_tensor("L1f_W1", [8, D, 4096], F32, kind="ExternalInput").ap(),
        "L1f_W2": nc.dram_tensor("L1f_W2", [8, 4096, D], F32, kind="ExternalInput").ap(),
        "L1f_b1": nc.dram_tensor("L1f_b1", [8, 128, 32], F32, kind="ExternalInput").ap(),
        "L1f_b2": nc.dram_tensor("L1f_b2", [8, 128, 8], F32, kind="ExternalInput").ap(),
        "ident_f32": nc.dram_tensor("ident_f32", [128, 128], F32, kind="ExternalInput").ap(),
        "iota8_big": nc.dram_tensor("iota8_big", [128, 8], F32, kind="ExternalInput").ap(),
        "own_hf": nc.dram_tensor("own_hf", [1, 1], I32, kind="ExternalInput").ap(),
    }
    w.update({
        "L0f_wrc": nc.dram_tensor("L0f_wrc", [128, DCH], F32, kind="ExternalInput").ap(),
        "L0f_wqkvT": nc.dram_tensor("L0f_wqkvT", [D, 3072], F32, kind="ExternalInput").ap(),
        "L0f_woutT": nc.dram_tensor("L0f_woutT", [D, D], F32, kind="ExternalInput").ap(),
        "L0f_g1c": nc.dram_tensor("L0f_g1c", [128, DCH], F32, kind="ExternalInput").ap(),
        "L0f_g2c": nc.dram_tensor("L0f_g2c", [128, DCH], F32, kind="ExternalInput").ap(),
        "L0f_w1T": nc.dram_tensor("L0f_w1T", [D, DFF0], F32, kind="ExternalInput").ap(),
        "L0f_w2T": nc.dram_tensor("L0f_w2T", [D, DFF0], F32, kind="ExternalInput").ap(),
        "L0f_w3T": nc.dram_tensor("L0f_w3T", [DFF0, D], F32, kind="ExternalInput").ap(),
        "iota512": nc.dram_tensor("iota512", [128, NSEL], F32, kind="ExternalInput").ap(),
        "mask512": nc.dram_tensor("mask512", [NSEL, NSEL], F32, kind="ExternalInput").ap(),
        "slotcol4": nc.dram_tensor("slotcol4", [128, 4], F32, kind="ExternalInput").ap(),
        "L1a_wqkT": nc.dram_tensor("L1a_wqkT", [D, D], F32, kind="ExternalInput").ap(),
        "L1a_wvT": nc.dram_tensor("L1a_wvT", [D, D], F32, kind="ExternalInput").ap(),
        "L1a_wogT": nc.dram_tensor("L1a_wogT", [D, D], F32, kind="ExternalInput").ap(),
        "L1a_wifT": nc.dram_tensor("L1a_wifT", [D, 16], F32, kind="ExternalInput").ap(),
        "L1a_bif": nc.dram_tensor("L1a_bif", [128, T // 128 * 16], F32, kind="ExternalInput").ap(),
        "L1a_lnwc": nc.dram_tensor("L1a_lnwc", [128, DCH], F32, kind="ExternalInput").ap(),
        "L1a_lnbc": nc.dram_tensor("L1a_lnbc", [128, DCH], F32, kind="ExternalInput").ap(),
        "L1a_woutT": nc.dram_tensor("L1a_woutT", [D, D], F32, kind="ExternalInput").ap(),
        "utri128": nc.dram_tensor("utri128", [128, 128], F32, kind="ExternalInput").ap(),
        "eh8": nc.dram_tensor("eh8", [8, 8 * 128], F32, kind="ExternalInput").ap(),
        "eh32": nc.dram_tensor("eh32", [32, 32 * 128], F32, kind="ExternalInput").ap(),
        "utris128": nc.dram_tensor("utris128", [128, 128], F32, kind="ExternalInput").ap(),
        "seg8c": nc.dram_tensor("seg8c", [128, 8], F32, kind="ExternalInput").ap(),
        "cap8c": nc.dram_tensor("cap8c", [128, 8], F32, kind="ExternalInput").ap(),
        "tokid32i": nc.dram_tensor("tokid32i", [128, 32], I32, kind="ExternalInput").ap(),
        "slotcol47": nc.dram_tensor("slotcol47", [128, 47], F32, kind="ExternalInput").ap(),
        "iota8c": nc.dram_tensor("iota8c", [128, 8], F32, kind="ExternalInput").ap(),
        "L1f_b2r": nc.dram_tensor("L1f_b2r", [8, D], F32, kind="ExternalInput").ap(),
        "maskU8": nc.dram_tensor("maskU8", [128, 8 * 128], F32, kind="ExternalInput").ap(),
    })
    xT = nc.dram_tensor("xT", [D, T], F32, kind="ExternalInput").ap()
    out_tok = nc.dram_tensor("outT", [D, TOWN], F32, kind="ExternalOutput").ap()
    x1T = nc.dram_tensor("x1T", [D, T], F32,
                         kind="ExternalOutput" if "A" in stages else "Internal").ap()
    x2T = nc.dram_tensor("x2T", [D, T], F32,
                         kind="ExternalOutput" if "B" in stages else "Internal").ap()
    x3T = nc.dram_tensor("x3T", [D, T], F32,
                         kind="ExternalOutput" if "C" in stages else "Internal").ap()
    lnT = nc.dram_tensor("lnT", [D, T], BF16).ap()
    scratch = {
        "k_tok": nc.dram_tensor("k_tok", [T, D], BF16).ap(),
        "v_tok": nc.dram_tensor("v_tok", [T, D], BF16).ap(),
        "ksum": nc.dram_tensor("ksum", [D], F32),
        "d_ln": nc.dram_tensor("d_ln", [D, TOWN], BF16).ap(),
        "d_acc": nc.dram_tensor("d_acc", [D, TOWN], F32).ap(),
        "d_fe": nc.dram_tensor("d_fe", [TOWN], F32,
                               kind="ExternalOutput" if debug else "Internal"),
        "d_pt": nc.dram_tensor("d_pt", [128, TOWN // 128], F32),
        "d_x3own": nc.dram_tensor("d_x3own", [D, TOWN], F32).ap(),
        "d_lnsort": nc.dram_tensor("d_lnsort", [6016, D], BF16).ap(),
        "d_ysort": nc.dram_tensor("d_ysort", [6016, D], BF16).ap(),
        "d_srow": nc.dram_tensor("d_srow", [32, 128], F32),
        "b_lntok": nc.dram_tensor("b_lntok", [T, D], BF16).ap(),
        "b_row": nc.dram_tensor("b_row", [T // 128, 128], F32),
        "c_qkT": nc.dram_tensor("c_qkT", [D, T], BF16).ap(),
        "c_sogT": nc.dram_tensor("c_sogT", [D, T], BF16).ap(),
        "c_ktok": nc.dram_tensor("c_ktok", [T, 512], BF16).ap(),
        "c_vtok": nc.dram_tensor("c_vtok", [T, D], BF16).ap(),
        "c_if": nc.dram_tensor("c_if", [T, 16], F32).ap(),
        "c_htok": nc.dram_tensor("c_htok", [T, D], BF16).ap(),
    }
    if debug:
        scratch["dbg"] = nc.dram_tensor("dbg", [128, 256], F32, kind="ExternalOutput").ap()
    kb = KB(nc)
    for v in (1e-5, 1e-6):
        kb.c(v)
    kb.init_consts_step()
    if "A" in stages:
        with kb.scope():
            emit_layernorm(kb, lambda t0, tt: fm(xT, t0, tt), lnT, "ln0")
        with kb.scope():
            emit_l0_attn(kb, xT, lnT, x1T, w, scratch)
    if "B" in stages:
        xb = xT if "A" not in stages else x1T
        with kb.scope():
            emit_layernorm(kb, lambda t0, tt: fm(xb, t0, tt), lnT, "ln1")
        with kb.scope():
            emit_l0_ffn(kb, xb, lnT, x2T, w, scratch)
    if "C" in stages:
        if "B" in stages:
            xc = x2T
        elif "A" in stages:
            xc = x1T
        else:
            xc = xT
        with kb.scope():
            emit_layernorm(kb, lambda t0, tt: fm(xc, t0, tt), lnT, "ln2")
        with kb.scope():
            emit_l1_attn(kb, xc, lnT, x3T, w, scratch)
    if "D" in stages:
        if x3_is_input:
            x3 = xT
        elif "C" in stages:
            x3 = x3T
        elif "B" in stages:
            x3 = x2T
        else:
            x3 = x1T
        with kb.scope():
            emit_l1_ffn(kb, x3, out_tok, w, scratch)
    kb.final_wait()
    kb.build()
    return nc


def prep_weights(inputs):
    iota8_big = (np.arange(8, dtype=np.float32) + BIGI)[None, :].repeat(128, 0)
    return {
        "L0a_wqkvT": np.ascontiguousarray(inputs["L0_attn_W_qkv"].T),
        "L0a_woutT": np.ascontiguousarray(inputs["L0_attn_W_out"].T),
        "L0a_bout": np.ascontiguousarray(inputs["L0_attn_b_out"].reshape(DCH, 128).T),
        "L1f_rwT": np.ascontiguousarray(inputs["L1_ffn_router_w"].T),
        "L1f_W1": np.ascontiguousarray(inputs["L1_ffn_W1"]),
        "L1f_W2": np.ascontiguousarray(inputs["L1_ffn_W2"]),
        "L1f_b1": np.ascontiguousarray(inputs["L1_ffn_b1"].reshape(8, 32, 128).transpose(0, 2, 1)),
        "L1f_b2": np.ascontiguousarray(inputs["L1_ffn_b2"].reshape(8, 8, 128).transpose(0, 2, 1)),
        "ident_f32": np.eye(128, dtype=np.float32),
        "iota8_big": np.ascontiguousarray(iota8_big),
        "L0f_wrc": np.ascontiguousarray(inputs["L0_ffn_w_router"].reshape(DCH, 128).T),
        "L0f_wqkvT": np.ascontiguousarray(inputs["L0_ffn_W_qkv"].T),
        "L0f_woutT": np.ascontiguousarray(inputs["L0_ffn_W_out"].T),
        "L0f_g1c": np.ascontiguousarray(inputs["L0_ffn_g1"].reshape(DCH, 128).T),
        "L0f_g2c": np.ascontiguousarray(inputs["L0_ffn_g2"].reshape(DCH, 128).T),
        "L0f_w1T": _padT(inputs["L0_ffn_W1"]),
        "L0f_w2T": _padT(inputs["L0_ffn_W2"]),
        "L0f_w3T": _padC(inputs["L0_ffn_W3"]),
        "iota512": np.tile(np.arange(NSEL, dtype=np.float32)[None, :], (128, 1)),
        "mask512": np.where(np.triu(np.ones((NSEL, NSEL), bool), 1), -1e9, 0.0).astype(np.float32),
        "slotcol4": (np.arange(4, dtype=np.float32)[None, :] * 128
                     + np.arange(128, dtype=np.float32)[:, None]),
        "L1a_wqkT": np.ascontiguousarray(np.concatenate(
            [inputs["L1_attn_Wq"].T, inputs["L1_attn_Wk"].T], axis=1)),
        "L1a_wvT": np.ascontiguousarray(inputs["L1_attn_Wv"].T),
        "L1a_wogT": np.ascontiguousarray(inputs["L1_attn_Wog"].T),
        "L1a_wifT": np.ascontiguousarray(np.concatenate(
            [inputs["L1_attn_Wi"].T, inputs["L1_attn_Wf"].T], axis=1)),
        "L1a_bif": np.ascontiguousarray(np.tile(np.concatenate(
            [inputs["L1_attn_bi"], inputs["L1_attn_bf"]]).astype(np.float32),
            (128, T // 128))),
        "L1a_lnwc": np.ascontiguousarray(inputs["L1_attn_ln_w"].reshape(DCH, 128).T),
        "L1a_lnbc": np.ascontiguousarray(inputs["L1_attn_ln_b"].reshape(DCH, 128).T),
        "L1a_woutT": np.ascontiguousarray(inputs["L1_attn_Wout"].T),
        "utri128": np.triu(np.ones((128, 128), np.float32)),
        "eh8": np.ascontiguousarray(np.concatenate(
            [np.where(np.arange(8)[:, None] == h, 1.0, 0.0).astype(np.float32)
             .repeat(128, axis=1).reshape(8, 128) for h in range(8)], axis=1)),
        "eh32": np.ascontiguousarray(np.concatenate(
            [np.where(np.arange(32)[:, None] == i, 1.0, 0.0).astype(np.float32)
             .repeat(128, axis=1).reshape(32, 128) for i in range(32)], axis=1)),
        "utris128": np.triu(np.ones((128, 128), np.float32), 1),
        "seg8c": np.tile(np.array([0, 1536] + [1536 + 640 * e for e in range(1, 7)],
                                  np.float32)[None, :], (128, 1)),
        "cap8c": np.tile(np.array([1536] + [640] * 7, np.float32)[None, :], (128, 1)),
        "tokid32i": (np.arange(128, dtype=np.int32)[:, None]
                     + 128 * np.arange(32, dtype=np.int32)[None, :]),
        "slotcol47": (np.arange(128, dtype=np.float32)[:, None]
                      + 128 * np.arange(47, dtype=np.float32)[None, :]),
        "iota8c": np.tile(np.arange(8, dtype=np.float32)[None, :], (128, 1)),
        "L1f_b2r": np.ascontiguousarray(inputs["L1_ffn_b2"]),
        "maskU8": np.ascontiguousarray(np.tile(
            np.where(np.arange(128)[:, None] <= np.arange(128)[None, :], 0.0, -1e9)
            .astype(np.float32), (1, 8))),
    }


def _padT(wm):
    out = np.zeros((D, DFF0), np.float32)
    out[:, :wm.shape[0]] = wm.T
    return out


def _padC(wm):
    out = np.zeros((DFF0, D), np.float32)
    out[:wm.shape[1], :] = wm.T
    return out


def kernel(**inputs):
    x = inputs["L0_attn_x"]
    nc = build_program(stages=("A", "B", "C", "D"))
    wmaps = prep_weights(inputs)
    in_maps = []
    for c in range(8):
        b, hf = c // 2, c % 2
        m = dict(wmaps)
        m["xT"] = np.ascontiguousarray(x[b].T)
        m["own_hf"] = np.array([[hf]], np.int32)
        in_maps.append(m)
    res = run_bass_kernel_spmd(nc, in_maps, core_ids=list(range(8)))
    out = np.empty((4, T, D), np.float32)
    for c in range(8):
        b, hf = c // 2, c % 2
        sl = slice(hf * TOWN, (hf + 1) * TOWN)
        out[b, sl, :] = res.results[c]["outT"].T
    return out
